# version 13
# baseline (speedup 1.0000x reference)
import math
from contextlib import ExitStack

import numpy as np

import concourse.bass as bass
import concourse.mybir as mybir
from concourse.bass_utils import run_bass_kernel_spmd
from concourse.bass_core_rust import IndirectOffsetOnAxis

F32 = mybir.dt.float32
BF16 = mybir.dt.bfloat16
I32 = mybir.dt.int32
AF = mybir.ActivationFunctionType
ALU = mybir.AluOpType
AX = mybir.AxisListType

NEG = -1.0e9


def make_cfg(d=2048, t=2048, b=4, nh=16, hid=5461):
    dh = 128
    assert d == nh * dh and t == nh * 128
    C = dict(d=d, t=t, b=b, nh=nh, dh=dh, hid=hid)
    C["n_cores"] = 2 * b
    C["T"] = t // 2
    C["n_dt"] = d // 128
    C["n_tt"] = C["T"] // 128
    C["nh_c"] = nh // 2
    C["n_hc"] = (hid + 127) // 128
    C["hidp"] = C["n_hc"] * 128
    C["rpb"] = 128 // nh
    C["n_qt"] = t // 128
    C["TG"] = min(512, C["T"])
    C["n_tg"] = C["T"] // C["TG"]
    C["QD"] = min(512, d)
    C["n_qd"] = d // C["QD"]
    C["KQ"] = min(256, C["T"])
    C["n_kq"] = C["T"] // C["KQ"]
    C["WCH"] = min(512, C["n_dt"] * 128)
    C["SW"] = min(t, 2048)
    C["scale"] = 1.0 / math.sqrt(dh)
    C["Nw"] = hid * d
    assert (C["Nw"] // 8) % 128 == 0
    C["Ns"] = C["Nw"] // 8 // 128
    C["SCH"] = min(C["WCH"], C["Ns"])
    assert C["QD"] <= C["WCH"]
    C["hw"] = C["nh_c"] * dh
    nq = C["n_qt"]
    if nq >= 16:
        C["CH"] = [(0, 8), (8, 12), (12, 15), (15, 16)]
    elif nq >= 2:
        C["CH"] = [(0, nq // 2), (nq // 2, nq)]
    else:
        C["CH"] = [(0, nq)]
    return C


CFG = make_cfg()


class PB:

    ENGS = ("te", "act", "dve", "gp", "sy")

    def __init__(self):
        self.q = {e: [] for e in self.ENGS}
        self.cnt = {}
        for s in ("te", "act", "dve", "gpc", "cc"):
            self.cnt[s] = 0
        self.hw = {}
        self.pending = {e: None for e in self.ENGS}

    def sem(self, name):
        if name not in self.cnt:
            self.cnt[name] = 0
        return name

    def wait(self, eng, sem, val):
        if val is None or val <= 0:
            return
        if self.hw.get((eng, sem), 0) >= val:
            return
        self.hw[(eng, sem)] = val
        self.q[eng].append(("w", sem, val))

    def fence(self):
        snap = dict(self.cnt)
        for e in self.ENGS:
            self.pending[e] = snap

    def on(self, eng, fn, inc=None, by=1, waits=()):
        if self.pending[eng] is not None:
            snap, self.pending[eng] = self.pending[eng], None
            for s, v in snap.items():
                self.wait(eng, s, v)
        for s, v in waits:
            self.wait(eng, s, v)
        post = None
        if inc is not None:
            self.sem(inc)
            self.cnt[inc] += by
            post = self.cnt[inc]
        self.q[eng].append(("i", fn, inc, by))
        return post

    def replay(self, eng_name, e, sems):
        for item in self.q[eng_name]:
            if item[0] == "w":
                e.wait_ge(sems[item[1]], item[2])
            else:
                _, fn, inc, by = item
                ins = fn(e)
                if inc is not None:
                    ins.then_inc(sems[inc], by)


class Arena:

    def __init__(self, tensor, cols):
        self.t = tensor
        self.cols = cols
        self.used = []

    def region(self, name, off, n):
        assert off + n <= self.cols, (name, off, n, self.cols)
        self.used.append((name, off, n))
        return self.t[:, off:off + n]


def build_nc(C):
    nc = bass.Bass("TRN2", target_bir_lowering=False)
    d, t, T = C["d"], C["t"], C["T"]
    nh, dh, nh_c, hw = C["nh"], C["dh"], C["nh_c"], C["hw"]
    n_dt, n_tt, n_hc = C["n_dt"], C["n_tt"], C["n_hc"]
    TG, n_tg, QD, n_qd = C["TG"], C["n_tg"], C["QD"], C["n_qd"]
    KQ, n_kq = C["KQ"], C["n_kq"]
    n_qt, rpb, SW = C["n_qt"], C["rpb"], C["SW"]
    WCH, SCH = C["WCH"], C["SCH"]
    n_wch = (n_dt * 128) // WCH
    dpc = WCH // 128

    ap = {}

    def din(name, shape, dt=F32):
        ap[name] = nc.dram_tensor(name, list(shape), dt, kind="ExternalInput").ap()

    din("xT", (128, n_dt, T))
    din("w1t", (128, n_hc, n_dt * 128))
    din("w3t", (128, n_hc, n_dt * 128))
    din("w2q", (128, n_qd, n_hc, QD))
    din("wqt", (128, n_dt, n_dt * 128))
    din("wkt", (128, nh_c, n_dt * 128))
    din("wvt", (128, n_dt, hw))
    din("wot", (128, n_dt, n_dt * 128))
    din("bqt", (128, n_dt))
    din("bkt", (128, nh_c))
    din("bvr", (1, hw))
    din("bot", (128, n_dt))
    din("w1s", (128, C["Ns"]))
    din("w3s", (128, C["Ns"]))
    din("w2s", (128, C["Ns"]))
    din("trib", (128, 128), BF16)
    din("iden", (128, 512), BF16)
    din("ones_d", (128, 1), BF16)
    din("ones_f", (128, 1))
    din("ones_r", (1, 128))
    din("ones_rb", (1, 128), BF16)
    din("gidx", (128, 2 * n_tt), I32)
    din("halfc", (128, 2))
    yT = nc.dram_tensor("yT", [128, n_dt, T], F32, kind="ExternalOutput").ap()

    agx_in = nc.dram_tensor("agx_in", [128, n_dt * T], BF16)
    agx_out = nc.dram_tensor("agx_out", [2 * 128, n_dt * T], BF16)
    aga_in = nc.dram_tensor("aga_in", [t, hw], BF16)
    aga_out = nc.dram_tensor("aga_out", [2 * t, hw], BF16)
    ar_in = nc.dram_tensor("ar_in", [3, 1], F32)
    ar_out = nc.dram_tensor("ar_out", [3, 1], F32)
    t1c = nc.dram_tensor("t1c", [128, n_hc, n_dt * 128], BF16)
    t3c = nc.dram_tensor("t3c", [128, n_hc, n_dt * 128], BF16)
    t2c = nc.dram_tensor("t2c", [128, n_qd, n_hc, QD], BF16)
    wqc = nc.dram_tensor("wqc", [128, n_dt, n_dt * 128], BF16)
    wkc = nc.dram_tensor("wkc", [128, nh_c, n_dt * 128], BF16)
    wvc = nc.dram_tensor("wvc", [128, n_dt, hw], BF16)
    woc = nc.dram_tensor("woc", [128, n_dt, n_dt * 128], BF16)

    pb = PB()
    st = ExitStack()

    def sb(name, shape, dt=F32):
        return st.enter_context(nc.sbuf_tensor(name, list(shape), dt))

    strm = sb("strm", (128, n_dt * T))
    AR_COLS = 65536 if d >= 2048 else 32768
    arena_t = sb("arena", (128, AR_COLS), BF16)
    A = Arena(arena_t, AR_COLS)
    SCR = max(WCH, TG)
    wf32 = sb("wf32", (128, 4 * WCH + 3 * SCR))
    trib = sb("trib_s", (128, 128), BF16)
    onesd = sb("onesd", (128, 1), BF16)
    onesf = sb("onesf", (128, 1))
    onesr = sb("onesr", (1, 128))
    onesrb = sb("onesrb", (1, 128), BF16)
    scal = sb("scal", (128, 8))
    bqt = sb("bqt_s", (128, n_dt))
    bkt = sb("bkt_s", (128, nh_c))
    bot = sb("bot_s", (128, n_dt))
    gidx = sb("gidx_s", (128, 2 * n_tt), I32)
    part = sb("part", (128, 8))
    srow = sb("srow", (1, 16))
    mx = sb("mx", (128, 12))
    halfc = sb("halfc_s", (128, 2))

    CON = 0
    iden4 = A.region("iden4", CON, 512)
    idenT = iden4[:, 0:128]
    bvr = A.region("bvr", CON + 512, hw)[0:1, :]
    outsb_base = CON + 512 + hw
    outsb = A.region("outsb", outsb_base, 512)
    arowb = arena_t[0:1, outsb_base + 128:outsb_base + 256]
    mcolb = arena_t[:, outsb_base + 384:outsb_base + 386]
    BASE = outsb_base + 512

    G_OFF = BASE
    G_COLS = n_hc * T
    TB_OFF = G_OFF + G_COLS
    X2_OFF = TB_OFF + 5 * WCH + TG
    X2_COLS = n_dt * T
    g_r = A.region("g", G_OFF, G_COLS)
    tring = [A.region("tr%d" % i, TB_OFF + i * WCH, WCH) for i in range(4)]
    abf_r = A.region("abf", TB_OFF + 4 * WCH, WCH)
    sA_r = A.region("sA", TB_OFF + 5 * WCH, TG)
    x2T = A.region("x2T", X2_OFF, X2_COLS)
    assert X2_OFF + X2_COLS <= AR_COLS, (X2_OFF, X2_COLS, AR_COLS)
    xbf = arena_t[:, G_OFF:G_OFF + n_dt * T]
    xsq = arena_t[:, G_OFF + n_dt * T:G_OFF + 2 * n_dt * T]
    assert 2 * n_dt * T <= G_COLS + 4 * WCH + TG

    QM_OFF = G_OFF
    K_OFF = QM_OFF + n_dt * T
    V_OFF = K_OFF + nh_c * t
    XQ_OFF = V_OFF + (t // 128) * hw
    P_OFF = XQ_OFF + 2 * n_dt * KQ
    PT_OFF = P_OFF + SW
    WB_OFF = PT_OFF + SW
    qmatT = arena_t[:, QM_OFF:QM_OFF + n_dt * T]
    K_r = arena_t[:, K_OFF:K_OFF + nh_c * t]
    V_r = arena_t[:, V_OFF:V_OFF + (t // 128) * hw]
    x2qb = [arena_t[:, XQ_OFF + i * n_dt * KQ:
                    XQ_OFF + (i + 1) * n_dt * KQ] for i in range(2)]
    P_r = arena_t[:, P_OFF:P_OFF + SW]
    Pt_r = arena_t[:, PT_OFF:PT_OFF + SW]
    wq_b = [arena_t[:, K_OFF + i * 2 * n_dt * 128:
                    K_OFF + (i + 1) * 2 * n_dt * 128] for i in range(2)]
    wk_b = [arena_t[:, P_OFF + i * (n_dt * 128):
                    P_OFF + (i + 1) * (n_dt * 128)] for i in range(2)]
    wv_b = [arena_t[:, WB_OFF + i * hw: WB_OFF + (i + 1) * hw] for i in range(2)]
    assert WB_OFF + 2 * hw <= AR_COLS, (WB_OFF + 2 * hw, AR_COLS)
    assert P_OFF + 2 * n_dt * 128 <= WB_OFF + 2 * hw
    o2tok = arena_t[:, K_OFF:K_OFF + n_tt * 2 * hw]
    o2T = arena_t[:, V_OFF:V_OFF + n_dt * T]
    assert V_OFF + n_dt * T <= AR_COLS
    wo_b = [arena_t[:, QM_OFF + i * 2 * n_dt * 128:
                    QM_OFF + (i + 1) * 2 * n_dt * 128] for i in range(2)]

    mb_r = wf32[:, 4 * WCH:4 * WCH + TG]
    rb_r = wf32[:, 4 * WCH + SCR:4 * WCH + SCR + TG]
    ln_tmp = wf32[:, 4 * WCH + 2 * SCR:4 * WCH + 2 * SCR + TG]
    ta_r = wf32[:, 4 * WCH:4 * WCH + WCH]
    tb_r = wf32[:, 4 * WCH + SCR:4 * WCH + SCR + WCH]
    m_row = rb_r[0:1, 0:TG]
    r_row = ln_tmp[0:1, 0:TG]

    def dma(eng, semname, out, in_, waits=()):
        pb.sem(semname)
        return pb.on(eng, lambda e, o=out, i=in_: e.dma_start(out=o, in_=i),
                     inc=semname, by=16, waits=waits)

    def t3(sb_ap, cdim):
        return sb_ap.rearrange("p (c f) -> p c f", c=cdim)

    c_x = dma("sy", "dx", t3(strm[:, :], n_dt), ap["xT"][:, :, :])
    dma("sy", "dcn", trib[:, :], ap["trib"][:, :])
    dma("sy", "dcn", iden4, ap["iden"][:, :])
    dma("sy", "dcn", onesd[:, :], ap["ones_d"][:, :])
    dma("sy", "dcn", onesf[:, :], ap["ones_f"][:, :])
    dma("sy", "dcn", onesr[:, :], ap["ones_r"][:, :])
    dma("sy", "dcn", onesrb[:, :], ap["ones_rb"][:, :])
    dma("sy", "dcn", bqt[:, :], ap["bqt"][:, :])
    dma("sy", "dcn", bkt[:, :], ap["bkt"][:, :])
    dma("sy", "dcn", bot[:, :], ap["bot"][:, :])
    dma("sy", "dcn", gidx[:, :], ap["gidx"][:, :])
    c_const = dma("sy", "dcn", halfc[:, :], ap["halfc"][:, :])
    c_bv = dma("gp", "dbv", bvr, ap["bvr"][:, :])

    n_sch = (C["Ns"] + SCH - 1) // SCH
    slot_busy = [0, 0]
    for j, nm in enumerate(("w1s", "w3s", "w2s")):
        for ch in range(n_sch):
            w = min(SCH, C["Ns"] - ch * SCH)
            sl = (j * n_sch + ch) % 2
            buf = wf32[:, sl * WCH: sl * WCH + w]
            c_l = dma("sy" if sl == 0 else "act", "dsl%d" % sl, buf,
                      ap[nm][:, ch * SCH: ch * SCH + w],
                      waits=[("dve", slot_busy[sl])])
            c_r = pb.on("dve", lambda e, buf=buf: e.reduce_sum(
                part[:, 6:7], buf, axis=AX.X, apply_absolute_value=True),
                inc="dve", waits=[("dsl%d" % sl, c_l)])
            slot_busy[sl] = c_r
            pb.wait("dve", "dve", c_r)
            if ch == 0:
                pb.on("dve", lambda e, j=j: e.tensor_copy(
                    part[:, j:j + 1], part[:, 6:7]), inc="dve")
            else:
                pb.on("dve", lambda e, j=j: e.tensor_tensor(
                    part[:, j:j + 1], part[:, j:j + 1], part[:, 6:7],
                    ALU.add), inc="dve")
            pb.wait("dve", "dve", pb.cnt["dve"])
    c_part = pb.cnt["dve"]

    with nc.psum_tensor([3, 512], F32) as ps_s3:
        c_mm3 = pb.on("te", lambda e: e.matmul(ps_s3[:, 0:1], part[:, 0:3],
                                               onesf[:, 0:1], start=True, stop=True),
                      inc="te", waits=[("dve", c_part), ("dcn", c_const)])
        pb.on("dve", lambda e: e.tensor_copy(part[0:3, 4:5], ps_s3[:, 0:1]),
              inc="dve", waits=[("te", c_mm3)])
    c_p3 = pb.cnt["dve"]
    c_ari = dma("sy", "dar", ar_in[:, :], part[0:3, 4:5],
                waits=[("dve", c_p3)])
    pb.on("gp", lambda e: e.collective_compute(
        "AllReduce", ALU.add, replica_groups=[list(range(C["n_cores"]))],
        ins=[ar_in.ap().opt()], outs=[ar_out.ap().opt()]), inc="cc",
        waits=[("dar", c_ari)])
    c_ar = pb.cnt["cc"]
    c_sr = dma("sy", "dsr", srow[0:1, 0:3],
               ar_out.ap().rearrange("a b -> b a"), waits=[("cc", c_ar)])

    dma("gp", "dcc", wqc.ap()[:, :, :], ap["wqt"][:, :, :],
        waits=[("dsl0", pb.cnt["dsl0"]), ("dsl1", pb.cnt["dsl1"])])
    dma("gp", "dcc", wkc.ap()[:, :, :], ap["wkt"][:, :, :])
    dma("gp", "dcc", wvc.ap()[:, :, :], ap["wvt"][:, :, :])
    c_cc = dma("gp", "dcc", woc.ap()[:, :, :], ap["wot"][:, :, :])

    def emit_scal_finalize():
        Nw = float(C["Nw"])
        pb.on("dve", lambda e: e.reciprocal(srow[0:1, 4:7], srow[0:1, 0:3]),
              inc="dve", waits=[("dsr", c_sr)])
        pb.wait("dve", "dve", pb.cnt["dve"])
        pb.on("dve", lambda e: e.tensor_scalar(srow[0:1, 8:11], srow[0:1, 4:7],
                                               Nw, None, ALU.mult), inc="dve")
        pb.on("dve", lambda e: e.tensor_scalar(srow[0:1, 11:12], srow[0:1, 0:1],
                                               0.5 / Nw, None, ALU.mult), inc="dve")
        pb.on("dve", lambda e: e.tensor_scalar(srow[0:1, 12:13], srow[0:1, 1:2],
                                               1.0 / Nw, None, ALU.mult), inc="dve")
        pb.on("dve", lambda e: e.tensor_scalar(srow[0:1, 13:14], srow[0:1, 2:3],
                                               0.25 / Nw, None, ALU.mult), inc="dve")
        c_fin = pb.cnt["dve"]
        with nc.psum_tensor([128, 512], F32) as ps_b6:
            c_b6 = pb.on("te", lambda e: e.matmul(
                ps_b6[:, 0:6], onesr[:, 0:128], srow[0:1, 8:14],
                start=True, stop=True), inc="te",
                waits=[("dve", c_fin), ("dcn", c_const)])
            pb.on("dve", lambda e: e.tensor_copy(scal[:, 0:6], ps_b6[:, 0:6]),
                  inc="dve", waits=[("te", c_b6)])
        return pb.cnt["dve"]

    INV1, INV3, INV2, HS1, HS3, FS2 = range(6)

    def emit_ln(extra_waits=(), stat_waits=()):
        c0 = pb.on("act", lambda e: e.activation(xbf[:, :], strm[:, :], AF.Copy),
                   inc="act", waits=list(extra_waits))
        c1 = pb.on("act", lambda e: e.activation(xsq[:, :], strm[:, :], AF.Square),
                   inc="act", waits=list(extra_waits))
        for tg_i in range(n_tg):
            t0 = tg_i * TG
            with nc.psum_tensor([1, max(TG, 512)], F32) as ps_m, \
                 nc.psum_tensor([1, max(TG, 512)], F32) as ps_q, \
                 nc.psum_tensor([128, max(TG, 512)], F32) as ps_mb, \
                 nc.psum_tensor([128, max(TG, 512)], F32) as ps_rb, \
                 nc.psum_tensor([128, max(TG, 512)], F32) as ps_tmp, \
                 nc.psum_tensor([1, max(TG, 512)], F32) as ln_s:
                for dt_i in range(n_dt):
                    pb.on("te", lambda e, i=dt_i, t0=t0: e.matmul(
                        ps_m[:, 0:TG], onesd[:, 0:1], xbf[:, i * T + t0:i * T + t0 + TG],
                        start=(i == 0), stop=(i == n_dt - 1)),
                        inc=("te" if dt_i == n_dt - 1 else None),
                        waits=(([("act", c0)] + list(stat_waits))
                               if dt_i == 0 else []))
                cm = pb.cnt["te"]
                for dt_i in range(n_dt):
                    pb.on("te", lambda e, i=dt_i, t0=t0: e.matmul(
                        ps_q[:, 0:TG], onesd[:, 0:1], xsq[:, i * T + t0:i * T + t0 + TG],
                        start=(i == 0), stop=(i == n_dt - 1)),
                        inc=("te" if dt_i == n_dt - 1 else None),
                        waits=([("act", c1)] if dt_i == 0 else []))
                cq = pb.cnt["te"]
                pb.on("dve", lambda e: e.tensor_scalar(m_row, ps_m[:, 0:TG], 1.0 / d,
                                                       None, ALU.mult), inc="dve",
                      waits=[("te", cm), ("gpc", pb.cnt["gpc"])])
                pb.wait("dve", "dve", pb.cnt["dve"])
                pb.on("dve", lambda e: e.tensor_tensor(r_row, m_row, m_row,
                                                       ALU.mult), inc="dve")
                pb.wait("dve", "dve", pb.cnt["dve"])
                pb.on("dve", lambda e: e.tensor_scalar(
                    r_row, r_row, -1.0, None, ALU.mult), inc="dve")
                pb.wait("dve", "dve", pb.cnt["dve"])
                pb.on("dve", lambda e: e.scalar_tensor_tensor(
                    r_row, ps_q[:, 0:TG], 1.0 / d, r_row, ALU.mult, ALU.add),
                    inc="dve", waits=[("te", cq)])
                pb.wait("dve", "dve", pb.cnt["dve"])
                cv = pb.on("dve", lambda e: e.tensor_scalar(r_row, r_row, 1e-5,
                                                            None, ALU.add),
                           inc="dve")
                cs = pb.on("act", lambda e: e.activation(
                    ln_s[0:1, 0:TG], r_row, AF.Sqrt), inc="act",
                    waits=[("dve", cv)])
                pb.on("dve", lambda e: e.reciprocal(mb_r[0:1, 0:TG],
                                                    ln_s[0:1, 0:TG]),
                      inc="dve", waits=[("act", cs), ("gpc", pb.cnt["gpc"])])
                pb.wait("dve", "dve", pb.cnt["dve"])
                pb.on("dve", lambda e: e.tensor_tensor(
                    ln_s[0:1, 0:TG], mb_r[0:1, 0:TG], mb_r[0:1, 0:TG], ALU.mult),
                    inc="dve")
                pb.wait("dve", "dve", pb.cnt["dve"])
                pb.on("dve", lambda e: e.tensor_tensor(
                    ln_s[0:1, 0:TG], ln_s[0:1, 0:TG], r_row, ALU.mult),
                    inc="dve")
                pb.wait("dve", "dve", pb.cnt["dve"])
                pb.on("dve", lambda e: e.tensor_scalar(
                    ln_s[0:1, 0:TG], ln_s[0:1, 0:TG], -0.5, 1.5, ALU.mult,
                    ALU.add), inc="dve")
                pb.wait("dve", "dve", pb.cnt["dve"])
                cr = pb.on("dve", lambda e: e.tensor_tensor(
                    r_row, mb_r[0:1, 0:TG], ln_s[0:1, 0:TG], ALU.mult),
                    inc="dve")
                c_rb1 = pb.on("te", lambda e: e.matmul(
                    ps_mb[:, 0:TG], onesr[:, 0:128], m_row, start=True, stop=True),
                    inc="te", waits=[("dve", cr)])
                c_rb2 = pb.on("te", lambda e: e.matmul(
                    ps_rb[:, 0:TG], onesr[:, 0:128], r_row, start=True, stop=True),
                    inc="te")
                pb.on("dve", lambda e: e.tensor_copy(mb_r, ps_mb[:, 0:TG]), inc="dve",
                      waits=[("te", c_rb1), ("gpc", pb.cnt["gpc"])])
                pb.on("dve", lambda e: e.tensor_copy(rb_r, ps_rb[:, 0:TG]), inc="dve",
                      waits=[("te", c_rb2)])
                c_bc = pb.cnt["dve"]
                c_bc_gp = c_bc
                for dt_i in range(n_dt):
                    if dt_i % 2 == 0:
                        c_t = pb.on("dve", lambda e, i=dt_i, t0=t0:
                                    e.tensor_tensor(
                                        ps_tmp[:, 0:TG],
                                        strm[:, i * T + t0:i * T + t0 + TG],
                                        mb_r, ALU.subtract), inc="dve",
                                    waits=[("dve", c_bc)])
                        pb.wait("dve", "dve", c_t)
                        pb.on("dve", lambda e, i=dt_i, t0=t0: e.tensor_tensor(
                            x2T[:, i * T + t0:i * T + t0 + TG], ps_tmp[:, 0:TG],
                            rb_r, ALU.mult), inc="dve")
                        pb.wait("dve", "dve", pb.cnt["dve"])
                    else:
                        c_t = pb.on("gp", lambda e, i=dt_i, t0=t0:
                                    e.tensor_tensor(
                                        ln_tmp[:, 0:TG],
                                        strm[:, i * T + t0:i * T + t0 + TG],
                                        mb_r, ALU.subtract), inc="gpc",
                                    waits=[("dve", c_bc)])
                        pb.wait("gp", "gpc", c_t)
                        pb.on("gp", lambda e, i=dt_i, t0=t0: e.tensor_tensor(
                            x2T[:, i * T + t0:i * T + t0 + TG], ln_tmp[:, 0:TG],
                            rb_r, ALU.mult), inc="gpc")
                        pb.wait("gp", "gpc", pb.cnt["gpc"])
        return pb.cnt["dve"]

    c_scal_done = [0]
    abf_busy = [0]
    tdve_busy = [0]

    def tern(src, dst, width, inv_col, waits):
        ab = abf_r[:, 0:width]
        ca = pb.on("act", lambda e, ab=ab: e.activation(
            ab, src, AF.Sign, bias=halfc[:, 0:1],
            scale=scal[:, inv_col:inv_col + 1]), inc="act",
            waits=list(waits) + [("dve", c_scal_done[0]),
                                 ("dve", abf_busy[0])])
        cb = pb.on("act", lambda e: e.activation(
            dst, src, AF.Sign, bias=halfc[:, 1:2],
            scale=scal[:, inv_col:inv_col + 1]), inc="act")
        cw = pb.on("dve", lambda e, ab=ab: e.tensor_tensor(dst, dst, ab,
                                                            ALU.add), inc="dve",
                   waits=[("act", cb)])
        abf_busy[0] = cw
        return ("dve", cw), ("act", cb)

    def tern_dve(src, dst, width, inv_col, waits):
        ta = ta_r[:, 0:width]
        tb = tb_r[:, 0:width]
        pb.on("dve", lambda e, ta=ta: e.tensor_scalar(
            ta, src, scal[:, inv_col:inv_col + 1], 0.5, ALU.mult, ALU.is_ge),
            inc="dve", waits=list(waits) + [("dve", c_scal_done[0]),
                                            ("dve", tdve_busy[0])])
        pb.on("dve", lambda e, tb=tb: e.tensor_scalar(
            tb, src, scal[:, inv_col:inv_col + 1], -0.5, ALU.mult, ALU.is_le),
            inc="dve")
        pb.wait("dve", "dve", pb.cnt["dve"])
        cw = pb.on("dve", lambda e, ta=ta, tb=tb: e.scalar_tensor_tensor(
            dst, ta, 1.0, tb, ALU.mult, ALU.subtract), inc="dve")
        tdve_busy[0] = cw
        return ("dve", cw), ("dve", cw)

    ffn_idx = [0]

    def emit_ffn(first, pre_waits=(), post_ln_cb=None, ln_stat_waits=()):
        c_ln = emit_ln(extra_waits=pre_waits, stat_waits=ln_stat_waits)
        if post_ln_cb is not None:
            post_ln_cb()
        pb.fence()
        with ExitStack() as hs:
            psb = [hs.enter_context(nc.psum_tensor("psb%d_%d" % (ffn_idx[0], i),
                                                    [128, TG], F32))
                   for i in range(8)]

            def hbank(mi, par, tg_i):
                return psb[par * 4 + mi * 2 + tg_i]

            w_busy = [None] * 4
            t_busy = [[], [], [], []]
            ev_busy = {}
            sA_busy = [0]
            ci = 0
            c_g = [0]

            def emit_evac(hc):
                par = hc % 2
                c_mm = ev_busy.get(("mm", hc), pb.cnt["te"])
                for tg_i in range(n_tg):
                    gdst = g_r[:, hc * T + tg_i * TG: hc * T + tg_i * TG + TG]
                    c_a = pb.on("act", lambda e, par=par, tg_i=tg_i: e.activation(
                        sA_r[:, 0:TG], hbank(0, par, tg_i)[:, :], AF.Silu,
                        scale=scal[:, HS1:HS1 + 1]), inc="act",
                        waits=[("te", c_mm), ("dve", sA_busy[0])])
                    c_b = pb.on("act", lambda e, par=par, tg_i=tg_i, gdst=gdst:
                                e.activation(gdst, hbank(1, par, tg_i)[:, :],
                                             AF.Identity,
                                             scale=scal[:, HS3:HS3 + 1]),
                                inc="act")
                    c_g[0] = pb.on("dve", lambda e, gdst=gdst: e.tensor_tensor(
                        gdst, gdst, sA_r[:, 0:TG], ALU.mult), inc="dve",
                        waits=[("act", c_b), ("act", c_a)])
                    sA_busy[0] = c_g[0]
                ev_busy[(par, "done")] = pb.cnt["act"]

            srcs = ((ap["w1t"], t1c, INV1, "act"), (ap["w3t"], t3c, INV3, "dve"))
            for hc in range(n_hc):
                par = hc % 2
                for mi, (wsrc, cache, inv, path) in enumerate(srcs):
                    for ch in range(n_wch):
                        ws = ci % 4
                        sl = ci % 4
                        ci += 1
                        if first:
                            wbuf = wf32[:, ws * WCH: ws * WCH + WCH]
                            ldq = "act" if path == "act" else "sy"
                            c_l = dma(ldq, "dwA%d" % ws, wbuf,
                                      wsrc[:, hc, ch * WCH:(ch + 1) * WCH],
                                      waits=([w_busy[ws]] if w_busy[ws] else []))
                            for s_, v_ in t_busy[sl]:
                                pb.wait("act" if path == "act" else "dve", s_, v_)
                            if path == "act":
                                c_t, frees = tern(wbuf, tring[sl], WCH, inv,
                                                  [("dwA%d" % ws, c_l)])
                            else:
                                c_t, frees = tern_dve(wbuf, tring[sl], WCH, inv,
                                                      [("dwA%d" % ws, c_l)])
                            w_busy[ws] = frees
                            c_st = dma("gp", "dtc%d" % sl,
                                       cache.ap()[:, hc, ch * WCH:(ch + 1) * WCH],
                                       tring[sl],
                                       waits=[c_t])
                            tb_new = [("dtc%d" % sl, c_st)]
                            mm_wait = c_t
                        else:
                            ldq = "sy" if sl % 2 == 0 else "act"
                            for s_, v_ in t_busy[sl]:
                                pb.wait(ldq, s_, v_)
                            c_l = dma(ldq, "dtA%d" % sl, tring[sl],
                                      cache.ap()[:, hc, ch * WCH:(ch + 1) * WCH])
                            tb_new = []
                            mm_wait = ("dtA%d" % sl, c_l)
                        for k in range(dpc):
                            dt_i = ch * dpc + k
                            for tg_i in range(n_tg):
                                wts = []
                                if k == 0 and tg_i == 0:
                                    wts = [mm_wait, ("dve", c_ln)]
                                    if ch == 0 and (par, "done") in ev_busy:
                                        wts.append(("act", ev_busy[(par, "done")]))
                                last = (k == dpc - 1 and tg_i == n_tg - 1)
                                pb.on("te", lambda e, k=k, tg_i=tg_i, mi=mi,
                                      par=par, sl=sl, dt_i=dt_i:
                                      e.matmul(hbank(mi, par, tg_i)[:, :],
                                               tring[sl][:, k * 128:(k + 1) * 128],
                                               x2T[:, dt_i * T + tg_i * TG:
                                                   dt_i * T + tg_i * TG + TG],
                                               start=(dt_i == 0),
                                               stop=(dt_i == n_dt - 1)),
                                      inc=("te" if last else None), waits=wts)
                        t_busy[sl] = tb_new + [("te", pb.cnt["te"])]
                ev_busy[("mm", hc)] = pb.cnt["te"]
                if hc > 0:
                    emit_evac(hc - 1)
            emit_evac(n_hc - 1)
        pb.fence()
        n_dq = QD // 128
        with ExitStack() as fs:
            psf = [[fs.enter_context(nc.psum_tensor(
                        "psf%d_%d_%d" % (ffn_idx[0], i, j), [128, TG], F32))
                    for j in range(n_tg)] for i in range(n_dq)]
            w_busy = [None] * 4
            t_busy = [[], [], [], []]
            prev_ev = 0
            ci2 = 0
            for qd_i in range(n_qd):
                for hc in range(n_hc):
                    ws = ci2 % 4
                    sl = ci2 % 4
                    ci2 += 1
                    if first:
                        wbuf = wf32[:, ws * WCH: ws * WCH + QD]
                        ldq = "sy" if ws % 2 == 0 else "act"
                        c_l = dma(ldq, "dwA%d" % ws, wbuf,
                                  ap["w2q"][:, qd_i, hc, :],
                                  waits=([w_busy[ws]] if w_busy[ws] else []))
                        for s_, v_ in t_busy[sl]:
                            pb.wait("act", s_, v_)
                        c_t, frees = tern(wbuf, tring[sl][:, 0:QD], QD, INV2,
                                          [("dwA%d" % ws, c_l)])
                        w_busy[ws] = frees
                        c_st = dma("gp", "dtc%d" % sl,
                                   t2c.ap()[:, qd_i, hc, :], tring[sl][:, 0:QD],
                                   waits=[c_t])
                        tb_new = [("dtc%d" % sl, c_st)]
                        mm_wait = c_t
                    else:
                        ldq = "sy" if sl % 2 == 0 else "act"
                        for s_, v_ in t_busy[sl]:
                            pb.wait(ldq, s_, v_)
                        c_l = dma(ldq, "dtA%d" % sl, tring[sl][:, 0:QD],
                                  t2c.ap()[:, qd_i, hc, :])
                        tb_new = []
                        mm_wait = ("dtA%d" % sl, c_l)
                    for dq in range(n_dq):
                        for tg_i in range(n_tg):
                            wts = []
                            if dq == 0 and tg_i == 0:
                                wts = [mm_wait, ("dve", c_g[0])]
                                if hc == 0 and qd_i > 0:
                                    wts.append(("dve", prev_ev))
                            last = (dq == n_dq - 1 and tg_i == n_tg - 1)
                            pb.on("te", lambda e, dq=dq, tg_i=tg_i, hc=hc, sl=sl:
                                  e.matmul(psf[dq][tg_i][:, :],
                                           tring[sl][:, dq * 128:(dq + 1) * 128],
                                           g_r[:, hc * T + tg_i * TG:
                                               hc * T + tg_i * TG + TG],
                                           start=(hc == 0), stop=(hc == n_hc - 1)),
                                  inc=("te" if last else None), waits=wts)
                    t_busy[sl] = tb_new + [("te", pb.cnt["te"])]
                c_fq = pb.cnt["te"]
                for dq in range(n_dq):
                    ot = qd_i * n_dq + dq
                    for tg_i in range(n_tg):
                        pb.on("dve", lambda e, dq=dq, tg_i=tg_i, ot=ot:
                              e.scalar_tensor_tensor(
                                  strm[:, ot * T + tg_i * TG:
                                       ot * T + tg_i * TG + TG],
                                  psf[dq][tg_i][:, :], scal[:, FS2:FS2 + 1],
                                  strm[:, ot * T + tg_i * TG:
                                       ot * T + tg_i * TG + TG],
                                  ALU.mult, ALU.add), inc="dve",
                              waits=[("te", c_fq)])
                prev_ev = pb.cnt["dve"]
        pb.fence()
        ffn_idx[0] += 1

    def emit_attn():
        c_ln2 = emit_ln()
        pb.fence()
        c_agw = dma("sy", "dagw", agx_in.ap()[:, :], x2T[:, :])
        pairs = [[2 * i, 2 * i + 1] for i in range(C["n_cores"] // 2)]
        pb.on("gp", lambda e: e.collective_compute(
            "AllGather", ALU.bypass, replica_groups=pairs,
            ins=[agx_in.ap().opt()], outs=[agx_out.ap().opt()]), inc="cc",
            waits=[("dagw", c_agw)])
        c_agx = pb.cnt["cc"]

        with ExitStack() as qs:
            qb = [qs.enter_context(nc.psum_tensor("qb%d" % i, [128, TG], F32))
                  for i in range(8)]
            wq_busy = [0, 0]
            bank_busy = {}
            for og in range(0, n_dt, 2):
                sl = (og // 2) % 2
                c_l = dma("sy" if sl == 0 else "act", "dwq%d" % sl, wq_b[sl],
                          wqc.ap()[:, og:og + 2, :]
                          .rearrange("p o f -> p (o f)"),
                          waits=[("te", wq_busy[sl]), ("dcc", c_cc)])
                for oi in range(2):
                    ot = og + oi
                    for tg_i in range(n_tg):
                        bank = qb[sl * 4 + oi * n_tg + tg_i]
                        bkey = sl * 4 + oi * n_tg + tg_i
                        for dc in range(n_dt):
                            wts = []
                            if dc == 0:
                                wts = [("dwq%d" % sl, c_l), ("dve", c_ln2)]
                                if bkey in bank_busy:
                                    wts.append(("act", bank_busy[bkey]))
                            pb.on("te", lambda e, oi=oi, tg_i=tg_i, dc=dc, sl=sl,
                                  bank=bank:
                                  e.matmul(bank[:, :],
                                           wq_b[sl][:, (oi * n_dt + dc) * 128:
                                                    (oi * n_dt + dc + 1) * 128],
                                           x2T[:, dc * T + tg_i * TG:
                                               dc * T + tg_i * TG + TG],
                                           start=(dc == 0), stop=(dc == n_dt - 1)),
                                  inc=("te" if dc == n_dt - 1 else None),
                                  waits=wts)
                        c_mm = pb.cnt["te"]
                        qsv = qmatT.rearrange("p (H G r c) -> p H G r c",
                                              G=n_qt, r=rpb, c=nh)
                        nhb = TG // 128
                        dst = qsv[:, tg_i * nhb:(tg_i + 1) * nhb, :, :, ot]
                        bank_busy[bkey] = pb.on(
                            "act", lambda e, dst=dst, ot=ot, bank=bank:
                            e.activation(dst, bank[:, :], AF.Identity,
                                         bias=bqt[:, ot:ot + 1]),
                            inc="act", waits=[("te", c_mm)])
                wq_busy[sl] = pb.cnt["te"]
        c_q = pb.cnt["act"]
        pb.fence()

        with ExitStack() as ks:
            kb = [ks.enter_context(nc.psum_tensor("kb%d" % i,
                                                  [128, max(KQ, 512)], F32))
                  for i in range(4)]
            vb = [ks.enter_context(nc.psum_tensor("vb%d" % i, [128, 512], F32))
                  for i in range(4)]
            GW = min(512, hw)
            n_gw = hw // GW
            kb_busy = {}
            vb_busy = {}
            wk_busy = [0, 0]
            wv_busy = [0, 0]
            x2q_busy = [0, 0]
            un = 0
            for half in range(2):
                for kq in range(n_kq):
                    x2q = x2qb[un % 2]
                    upar = un % 2
                    un += 1
                    c_xgl = []
                    for dcl in range(n_dt):
                        c_xgl.append(("dxq%d_%d" % (dcl, upar), dma(
                            "sy", "dxq%d_%d" % (dcl, upar),
                            x2q[:, dcl * KQ:(dcl + 1) * KQ],
                            agx_out.ap()[half * 128:(half + 1) * 128,
                                         dcl * T + kq * KQ:
                                         dcl * T + kq * KQ + KQ],
                            waits=([("cc", c_agx), ("te", x2q_busy[upar])]
                                   if dcl == 0 else ()))))
                    for hh in range(nh_c):
                        sl = hh % 2
                        c_wk = dma("sy", "dwk%d" % sl, wk_b[sl],
                                   wkc.ap()[:, hh, :],
                                   waits=[("te", wk_busy[sl]), ("dcc", c_cc)])
                        bank = kb[hh % 4]
                        for dc in range(n_dt):
                            wts = [c_xgl[dc]]
                            if dc == 0:
                                wts += [("dwk%d" % sl, c_wk)]
                                if (hh % 4) in kb_busy:
                                    wts.append(("act", kb_busy[hh % 4]))
                            pb.on("te", lambda e, hh=hh, dc=dc, bank=bank,
                                  sl=sl, x2q=x2q:
                                  e.matmul(bank[:, 0:KQ],
                                           wk_b[sl][:, dc * 128:(dc + 1) * 128],
                                           x2q[:, dc * KQ:(dc + 1) * KQ],
                                           start=(dc == 0), stop=(dc == n_dt - 1)),
                                  inc=("te" if dc == n_dt - 1 else None),
                                  waits=wts)
                        c_mm = pb.cnt["te"]
                        wk_busy[sl] = c_mm
                        kb_busy[hh % 4] = pb.on(
                            "act", lambda e, hh=hh, half=half, kq=kq, bank=bank:
                            e.activation(K_r[:, hh * t + half * T + kq * KQ:
                                             hh * t + half * T + kq * KQ + KQ],
                                         bank[:, 0:KQ], AF.Identity,
                                         bias=bkt[:, hh:hh + 1]),
                            inc="act", waits=[("te", c_mm)])
                    n_kt = KQ // 128
                    for gw in range(n_gw):
                        for dc in range(n_dt):
                            sl = dc % 2
                            c_wv = dma("act", "dwv%d" % sl, wv_b[sl][:, 0:GW],
                                       wvc.ap()[:, dc, gw * GW:(gw + 1) * GW],
                                       waits=[("te", wv_busy[sl]),
                                              ("dcc", c_cc)])
                            for kt in range(n_kt):
                                bank = vb[kt]
                                wts = [c_xgl[dc]]
                                if dc == 0:
                                    wts += [("dwv%d" % sl, c_wv),
                                           ("dbv", c_bv)]
                                    if kt in vb_busy:
                                        wts.append(("dve", vb_busy[kt]))
                                else:
                                    wts = [("dwv%d" % sl, c_wv)]
                                last = (kt == n_kt - 1)
                                pb.on("te", lambda e, dc=dc, kt=kt, gw=gw,
                                      bank=bank, sl=sl, x2q=x2q:
                                      e.matmul(bank[:, 0:GW],
                                               x2q[:, dc * KQ + kt * 128:
                                                   dc * KQ + (kt + 1) * 128],
                                               wv_b[sl][:, 0:GW],
                                               start=(dc == 0), stop=False),
                                      inc=("te" if last else None), waits=wts)
                            wv_busy[sl] = pb.cnt["te"]
                        for kt in range(n_kt):
                            ktile = half * n_tt + kq * n_kt + kt
                            bank = vb[kt]
                            pb.on("te", lambda e, gw=gw, bank=bank:
                                  e.matmul(bank[:, 0:GW], onesrb[:, 0:128],
                                           bvr[0:1, gw * GW:(gw + 1) * GW],
                                           start=False, stop=True),
                                  inc="te")
                            c_mm = pb.cnt["te"]
                            vb_busy[kt] = pb.on(
                                "dve", lambda e, ktile=ktile, gw=gw, bank=bank:
                                e.tensor_copy(V_r[:, ktile * hw + gw * GW:
                                                  ktile * hw + gw * GW + GW],
                                              bank[:, 0:GW]),
                                inc="dve", waits=[("te", c_mm)])
                    x2q_busy[upar] = pb.cnt["te"]
        c_kv_act = pb.cnt["act"]
        c_kv_dve = pb.cnt["dve"]
        pb.fence()

        CH = C["CH"]
        Pt2 = [P_r, Pt_r]
        arow4 = arena_t[0:1, WB_OFF:WB_OFF + 512]
        iters = [(gq, hh) for gq in range(n_qt) for hh in range(nh_c)]
        n_it = len(iters)
        with nc.psum_tensor([128, 512], F32) as ss0, \
             nc.psum_tensor([128, 512], F32) as ss1, \
             nc.psum_tensor([128, 512], F32) as st0, \
             nc.psum_tensor([128, 512], F32) as st1, \
             nc.psum_tensor([128, 512], F32) as po0, \
             nc.psum_tensor([128, 512], F32) as po1, \
             nc.psum_tensor([128, 512], F32) as den_c, \
             nc.psum_tensor([128, 512], F32) as misc:
            ssb = [ss0, ss1]
            stb = [st0, st1]
            pob = [po0, po1]
            S = {}
            ss_busy = [0, 0]
            st_busy = [0, 0]
            osb_busy = [0, 0]
            pob_busy = [0, 0]
            den_free = [0]
            misc_busy = [0]
            pt_free = [0, 0]
            arow_busy = [0]
            mcol_busy = [0, 0]
            mx_busy = [0, 0]
            c_aga = []

            def emitA(i):
                gq, hh = iters[i]
                n_k = (gq + 1) * 128
                nblk = (n_k + 511) // 512
                par = i % 2
                q_ap = qmatT[:, (hh * n_qt + gq) * 128:
                             (hh * n_qt + gq + 1) * 128]
                mb = 4 + 4 * par
                for kb in range(nblk):
                    kb0 = kb * 512
                    w_ = min(512, n_k - kb0)
                    diag = (gq // 4 == kb)
                    bi = kb % 2
                    wts = [("dve", ss_busy[bi]), ("dve", mx_busy[par])]
                    if i == 0 and kb == 0:
                        wts += [("act", c_q), ("act", c_kv_act)]
                    pb.on("te", lambda e, kb0=kb0, w_=w_, q_ap=q_ap, hh=hh,
                          diag=diag, bi=bi:
                          e.matmul(ssb[bi][:, 0:w_], q_ap,
                                   K_r[:, hh * t + kb0: hh * t + kb0 + w_],
                                   start=True, stop=not diag),
                          inc=(None if diag else "te"), waits=wts)
                    if diag:
                        pb.on("te", lambda e, gq=gq, kb0=kb0, bi=bi:
                              e.matmul(ssb[bi][:, gq * 128 - kb0:
                                               (gq + 1) * 128 - kb0],
                                       idenT, trib[:, :],
                                       start=False, stop=True), inc="te")
                    c_blk = pb.cnt["te"]
                    ss_busy[bi] = pb.on(
                        "dve", lambda e, w_=w_, kb=kb, bi=bi, mb=mb:
                        e.reduce_max(mx[:, mb + kb:mb + kb + 1],
                                     ssb[bi][:, 0:w_], axis=AX.X),
                        inc="dve", waits=[("te", c_blk)])
                S[i] = dict(q_ap=q_ap, n_k=n_k, nblk=nblk,
                            c_maxes=pb.cnt["dve"])

            def emitB1a(i):
                gq, hh = iters[i]
                st_ = S[i]
                n_k, nblk, q_ap = st_["n_k"], st_["nblk"], st_["q_ap"]
                par = i % 2
                mb = 4 + 4 * par
                pb.wait("dve", "dve", st_["c_maxes"])
                mcol = mb
                if nblk > 1:
                    pb.on("dve", lambda e, nblk=nblk, mb=mb, par=par:
                          e.reduce_max(mx[:, par:par + 1],
                                       mx[:, mb:mb + nblk], axis=AX.X),
                          inc="dve")
                    pb.wait("dve", "dve", pb.cnt["dve"])
                    mcol = par
                c_nm = pb.on("dve", lambda e, mcol=mcol, par=par:
                             e.tensor_scalar(mcolb[:, par:par + 1],
                                             mx[:, mcol:mcol + 1], -1.0, None,
                                             ALU.mult), inc="dve",
                             waits=[("te", mcol_busy[par])])
                mx_busy[par] = c_nm
                c_nmm = pb.on("te", lambda e, par=par: e.matmul(
                    misc[0:1, 0:512], mcolb[:, par:par + 1], iden4,
                    start=True, stop=True), inc="te",
                    waits=[("dve", c_nm), ("dve", misc_busy[0])])
                mcol_busy[par] = c_nmm
                c_nmev = pb.on("dve", lambda e: e.tensor_copy(
                    arow4[0:1, 0:512], misc[0:1, 0:512]), inc="dve",
                    waits=[("te", c_nmm), ("te", arow_busy[0])])
                misc_busy[0] = c_nmev
                st_["c_nmev"] = c_nmev

            def emitB1b(i):
                gq, hh = iters[i]
                st_ = S[i]
                n_k, q_ap = st_["n_k"], st_["q_ap"]
                n_kt = n_k // 128
                c_nmev = st_["c_nmev"]
                exp_of_grp = []
                ptbuf = Pt2[i % 2]
                st_["ptbuf"] = ptbuf
                for b0 in range(0, n_kt, 4):
                    nb = min(4, n_kt - b0)
                    bi = (b0 // 4) % 2
                    bank = stb[bi]
                    for j in range(nb):
                        kt = b0 + j
                        wts = []
                        if j == 0:
                            wts = [("act", st_busy[bi]), ("dve", c_nmev)]
                        pb.on("te", lambda e, j=j, kt=kt, hh=hh, q_ap=q_ap,
                              bank=bank:
                              e.matmul(bank[:, j * 128:(j + 1) * 128],
                                       K_r[:, hh * t + kt * 128:
                                           hh * t + (kt + 1) * 128],
                                       q_ap, start=(j == 0), stop=False),
                              inc=None, waits=wts)
                        if kt == gq:
                            pb.on("te", lambda e, j=j, bank=bank:
                                  e.matmul(bank[:, j * 128:(j + 1) * 128],
                                           trib[:, :], idenT,
                                           start=False, stop=False),
                                  inc=None)
                    pb.on("te", lambda e, nb=nb, bank=bank:
                          e.matmul(bank[:, 0:nb * 128],
                                   onesrb[0:1, 0:128], arow4[0:1, 0:nb * 128],
                                   start=False, stop=True), inc="te")
                    c_stp = pb.cnt["te"]
                    c_exp = pb.on("act", lambda e, b0=b0, nb=nb, bank=bank,
                                  ptbuf=ptbuf:
                                  e.activation(
                                      ptbuf[:, b0 * 128:(b0 + nb) * 128],
                                      bank[:, 0:nb * 128], AF.Exp,
                                      scale=C["scale"]), inc="act",
                                  waits=[("te", c_stp),
                                         ("te", pt_free[i % 2])])
                    st_busy[bi] = c_exp
                    exp_of_grp.append(c_exp)
                arow_busy[0] = pb.cnt["te"]
                st_["exps"] = exp_of_grp

            def emitB2(i):
                gq, hh = iters[i]
                st_ = S[i]
                n_k = st_["n_k"]
                n_kt = n_k // 128
                exp_of_grp = st_["exps"]
                ptbuf = st_["ptbuf"]
                oi = i % 2
                obank = pob[oi]
                for kt in range(n_kt):
                    wts = [("act", exp_of_grp[kt // 4])] if kt % 4 == 0 else []
                    if kt == 0:
                        wts += [("dve", c_kv_dve), ("dve", pob_busy[oi]),
                                ("dve", den_free[0])]
                    pb.on("te", lambda e, kt=kt, hh=hh, obank=obank,
                          n_kt=n_kt, ptbuf=ptbuf:
                          e.matmul(obank[:, 0:128],
                                   ptbuf[:, kt * 128:(kt + 1) * 128],
                                   V_r[:, kt * hw + hh * 128:
                                       kt * hw + (hh + 1) * 128],
                                   start=(kt == 0), stop=(kt == n_kt - 1)),
                          inc=None, waits=wts)
                    pb.on("te", lambda e, kt=kt, n_kt=n_kt, ptbuf=ptbuf:
                          e.matmul(den_c[:, 0:1],
                                   ptbuf[:, kt * 128:(kt + 1) * 128],
                                   onesd[:, 0:1],
                                   start=(kt == 0), stop=(kt == n_kt - 1)),
                          inc=("te" if kt == n_kt - 1 else None))
                c_pv = pb.cnt["te"]
                pt_free[i % 2] = c_pv
                pb.wait("dve", "dve", pb.cnt["dve"])
                c_rc = pb.on("dve", lambda e, oi=oi: e.reciprocal(
                    mx[:, 2 + oi:3 + oi], den_c[:, 0:1]), inc="dve",
                    waits=[("te", c_pv)])
                den_free[0] = c_rc
                osb = outsb[:, oi * 256: oi * 256 + 128]
                c_oe = pb.on("dve", lambda e, obank=obank, osb=osb, oi=oi:
                             e.tensor_scalar(osb, obank[:, 0:128],
                                             mx[:, 2 + oi:3 + oi], None,
                                             ALU.mult),
                             inc="dve",
                             waits=[("te", c_pv), ("dve", c_rc),
                                    ("dos%d" % oi, osb_busy[oi])])
                pob_busy[oi] = c_oe
                osb_busy[oi] = dma(
                    "sy", "dos%d" % oi,
                    aga_in.ap()[gq * 128:(gq + 1) * 128,
                                hh * 128:(hh + 1) * 128],
                    osb, waits=[("dve", c_oe)])
                del S[i]
                if hh == nh_c - 1:
                    for (q0, q1) in CH:
                        if gq + 1 == q1:
                            r0, r1 = q0 * 128, q1 * 128
                            pb.on("gp", lambda e, r0=r0, r1=r1:
                                  e.collective_compute(
                                      "AllGather", ALU.bypass,
                                      replica_groups=pairs,
                                      ins=[aga_in.ap()[r0:r1, :].opt()],
                                      outs=[aga_out.ap()[2 * r0:2 * r1, :]
                                            .opt()]), inc="cc",
                                  waits=[("dos0", osb_busy[0]),
                                         ("dos1", osb_busy[1])])
                            c_aga.append(pb.cnt["cc"])

            for s in range(n_it + 2):
                if 1 <= s and s - 1 < n_it:
                    emitB1a(s - 1)
                if s < n_it:
                    emitA(s)
                if 1 <= s and s - 1 < n_it:
                    emitB1b(s - 1)
                if s >= 2:
                    emitB2(s - 2)
        pb.fence()

        for j in range(2):
            for tt in range(n_tt):
                col = j * n_tt + tt
                pb.on("gp", lambda e, j=j, tt=tt, col=col: e.indirect_dma_start(
                    o2tok[:, (tt * 2 + j) * hw: (tt * 2 + j + 1) * hw], None,
                    aga_out.ap()[:, :],
                    IndirectOffsetOnAxis(gidx[:, col:col + 1], 0)),
                    inc="dga", by=16, waits=[("cc", c_aga[-1])])
        c_gath = pb.cnt["dga"]

        with nc.psum_tensor([128, 1024], BF16) as tq0, \
             nc.psum_tensor([128, 1024], BF16) as tq1:
            tqb = [tq0, tq1]
            busy = [0, 0]
            bi = 0
            n_dd = 2 * hw // 128
            for tt in range(n_tt):
                for dc0 in range(0, n_dd, 4):
                    nb = min(4, n_dd - dc0)
                    bank = tqb[bi % 2]
                    for j in range(nb):
                        dc = dc0 + j
                        wts = []
                        if j == 0:
                            wts = [("dga", c_gath), ("dve", busy[bi % 2])]
                        pb.on("te", lambda e, tt=tt, dc=dc, j=j, bank=bank:
                              e.transpose(bank[:, j * 128:(j + 1) * 128],
                                          o2tok[:, tt * 2 * hw + dc * 128:
                                                tt * 2 * hw + (dc + 1) * 128],
                                          idenT),
                              inc=("te" if j == nb - 1 else None), waits=wts)
                    c_tp = pb.cnt["te"]
                    o2v = o2T.rearrange("p (c f) -> p c f", c=n_dd)
                    busy[bi % 2] = pb.on(
                        "dve", lambda e, tt=tt, dc0=dc0, nb=nb, bank=bank, o2v=o2v:
                        e.tensor_copy(
                            o2v[:, dc0:dc0 + nb, tt * 128:(tt + 1) * 128],
                            bank[:, 0:nb * 128]
                            .rearrange("p (c f) -> p c f", c=nb)),
                        inc="dve", waits=[("te", c_tp)])
                    bi += 1
        c_o2t = pb.cnt["dve"]
        pb.fence()

        with ExitStack() as osx:
            ob = [osx.enter_context(nc.psum_tensor("ob%d" % i, [128, TG], F32))
                  for i in range(8)]
            wo_busy = [0, 0]
            bank_busy = {}
            for og in range(0, n_dt, 2):
                sl = (og // 2) % 2
                c_l = dma("sy", "dwo%d" % sl, wo_b[sl],
                          woc.ap()[:, og:og + 2, :]
                          .rearrange("p o f -> p (o f)"),
                          waits=[("te", wo_busy[sl]), ("dcc", c_cc)])
                for oi in range(2):
                    ot = og + oi
                    for tg_i in range(n_tg):
                        bank = ob[sl * 4 + oi * n_tg + tg_i]
                        bkey = sl * 4 + oi * n_tg + tg_i
                        for dc in range(n_dt):
                            wts = []
                            if dc == 0:
                                wts = [("dwo%d" % sl, c_l), ("dve", c_o2t)]
                                if bkey in bank_busy:
                                    wts.append(("dve", bank_busy[bkey]))
                            pb.on("te", lambda e, oi=oi, tg_i=tg_i, dc=dc,
                                  sl=sl, bank=bank:
                                  e.matmul(bank[:, :],
                                           wo_b[sl][:, (oi * n_dt + dc) * 128:
                                                    (oi * n_dt + dc + 1) * 128],
                                           o2T[:, dc * T + tg_i * TG:
                                               dc * T + tg_i * TG + TG],
                                           start=(dc == 0), stop=(dc == n_dt - 1)),
                                  inc=("te" if dc == n_dt - 1 else None),
                                  waits=wts)
                        c_mm = pb.cnt["te"]
                        bank_busy[bkey] = pb.on(
                            "dve", lambda e, ot=ot, tg_i=tg_i, bank=bank:
                            e.scalar_tensor_tensor(
                                strm[:, ot * T + tg_i * TG:
                                     ot * T + tg_i * TG + TG],
                                bank[:, :], bot[:, ot:ot + 1],
                                strm[:, ot * T + tg_i * TG:
                                     ot * T + tg_i * TG + TG],
                                ALU.add, ALU.add), inc="dve",
                            waits=[("te", c_mm)])
                wo_busy[sl] = pb.cnt["te"]
        pb.fence()

    def first_post_ln():
        c_scal_done[0] = emit_scal_finalize()

    emit_ffn(True, pre_waits=[("dx", c_x), ("dcn", c_const)],
             post_ln_cb=first_post_ln, ln_stat_waits=[("dve", c_p3)])
    emit_attn()
    emit_ffn(False)

    c_out = dma("sy", "dou", yT[:, :, :], t3(strm[:, :], n_dt),
                waits=[("dve", pb.cnt["dve"])])
    pb.wait("sy", "dou", c_out)

    sems = {s: st.enter_context(nc.semaphore("s_" + s)) for s in pb.cnt}

    with nc.Block() as block:
        @block.tensor
        def _(e):
            pb.replay("te", e, sems)

        @block.scalar
        def _(e):
            pb.replay("act", e, sems)

        @block.vector
        def _(e):
            pb.replay("dve", e, sems)

        @block.gpsimd
        def _(e):
            pb.replay("gp", e, sems)

        @block.sync
        def _(e):
            pb.replay("sy", e, sems)

    st.close()
    return nc



def stage_inputs(C, inputs):
    import ml_dtypes
    d, t, b, nh, dh, hid = C["d"], C["t"], C["b"], C["nh"], C["dh"], C["hid"]
    T, n_dt, n_tt, nh_c = C["T"], C["n_dt"], C["n_tt"], C["nh_c"]
    n_hc, hidp, QD, n_qd, hw = C["n_hc"], C["hidp"], C["QD"], C["n_qd"], C["hw"]

    x = np.ascontiguousarray(np.asarray(inputs["L0_attn_x"], dtype=np.float32))
    Wq = np.asarray(inputs["L0_attn_Wq"], np.float32)
    Wk = np.asarray(inputs["L0_attn_Wk"], np.float32)
    Wv = np.asarray(inputs["L0_attn_Wv"], np.float32)
    Wo = np.asarray(inputs["L0_attn_Wo"], np.float32)
    bq = np.asarray(inputs["L0_attn_bq"], np.float32)
    bk = np.asarray(inputs["L0_attn_bk"], np.float32)
    bv = np.asarray(inputs["L0_attn_bv"], np.float32)
    bo = np.asarray(inputs["L0_attn_bo"], np.float32)
    w1 = np.asarray(inputs["L0_ffn_w1"], np.float32)
    w3 = np.asarray(inputs["L0_ffn_w3"], np.float32)
    w2 = np.asarray(inputs["L0_ffn_w2"], np.float32)

    def tile_w13(w):
        wp = np.zeros((hidp, d), np.float32)
        wp[:hid] = w
        a = wp.reshape(n_hc, 128, n_dt, 128)
        return np.ascontiguousarray(
            a.transpose(3, 0, 2, 1).reshape(128, n_hc, n_dt * 128))

    w1t, w3t = tile_w13(w1), tile_w13(w3)
    w2p = np.zeros((d, hidp), np.float32)
    w2p[:, :hid] = w2
    w2q = np.ascontiguousarray(
        w2p.reshape(n_qd, QD, n_hc, 128).transpose(3, 0, 2, 1))
    wqt = np.ascontiguousarray(
        Wq.reshape(n_dt, 128, n_dt, 128).transpose(1, 2, 0, 3)
        .reshape(128, n_dt, n_dt * 128))
    wot = np.ascontiguousarray(
        Wo.reshape(n_dt, 128, n_dt, 128).transpose(1, 2, 0, 3)
        .reshape(128, n_dt, n_dt * 128))
    bqt = np.ascontiguousarray(bq.reshape(n_dt, 128).T)
    bot = np.ascontiguousarray(bo.reshape(n_dt, 128).T)

    Ns = C["Ns"]

    def slc(w, c):
        fl = w.reshape(-1)
        seg = fl.size // 8
        return np.ascontiguousarray(fl[c * seg:(c + 1) * seg].reshape(128, Ns))

    trib = np.triu(np.full((128, 128), NEG, np.float32), k=1).astype(
        ml_dtypes.bfloat16)
    iden = np.tile(np.eye(128, dtype=np.float32), (1, 4)).astype(
        ml_dtypes.bfloat16)
    ones_d = np.ones((128, 1), ml_dtypes.bfloat16)
    ones_f = np.ones((128, 1), np.float32)
    ones_r = np.ones((1, 128), np.float32)
    ones_rb = np.ones((1, 128), ml_dtypes.bfloat16)

    in_maps = []
    for c in range(C["n_cores"]):
        bb, h = c // 2, c % 2
        m = {}
        xt = x[bb, h * T:(h + 1) * T, :].T
        m["xT"] = np.ascontiguousarray(
            xt.reshape(n_dt, 128, T).transpose(1, 0, 2))
        m["w1t"], m["w3t"], m["w2q"] = w1t, w3t, w2q
        m["wqt"], m["wot"] = wqt, wot
        cols = slice(h * hw, (h + 1) * hw)
        m["wkt"] = np.ascontiguousarray(
            Wk[:, cols].reshape(n_dt, 128, nh_c, 128)
            .transpose(1, 2, 0, 3).reshape(128, nh_c, n_dt * 128))
        m["wvt"] = np.ascontiguousarray(
            Wv[:, cols].reshape(n_dt, 128, hw).transpose(1, 0, 2))
        m["bqt"], m["bot"] = bqt, bot
        m["bkt"] = np.ascontiguousarray(bk[cols].reshape(nh_c, 128).T)
        m["bvr"] = np.ascontiguousarray(bv[cols].reshape(1, hw))
        m["w1s"], m["w3s"], m["w2s"] = slc(w1, c), slc(w3, c), slc(w2, c)
        m["trib"], m["iden"] = trib, iden
        m["ones_d"], m["ones_f"] = ones_d, ones_f
        m["ones_r"], m["ones_rb"] = ones_r, ones_rb
        m["halfc"] = np.repeat(np.array([[-0.5, 0.5]], np.float32), 128, 0)
        CH = C["CH"]
        starts = {}
        off = 0
        for (q0, q1) in CH:
            starts[q0 * 128] = (off, (q1 - q0) * 128)
            off += 2 * (q1 - q0) * 128
        gi = np.zeros((128, 2 * n_tt), np.int32)
        for j in range(2):
            for tt in range(n_tt):
                r = h * T + tt * 128 + np.arange(128)
                base = np.zeros_like(r)
                loc = np.zeros_like(r)
                ln_ = np.zeros_like(r)
                for s0, (o_, l_) in starts.items():
                    mseg = (r >= s0) & (r < s0 + l_)
                    base[mseg] = o_
                    loc[mseg] = r[mseg] - s0
                    ln_[mseg] = l_
                gi[:, j * n_tt + tt] = base + j * ln_ + loc
        m["gidx"] = gi
        in_maps.append(m)
    return in_maps


def assemble_output(C, results):
    out = np.empty((C["b"], C["t"], C["d"]), np.float32)
    T, n_dt = C["T"], C["n_dt"]
    for c in range(C["n_cores"]):
        bb, h = c // 2, c % 2
        y = np.asarray(results[c]["yT"]).reshape(128, n_dt, T)
        out[bb, h * T:(h + 1) * T, :] = (
            y.transpose(1, 0, 2).reshape(C["d"], T).T)
    return out


def kernel(**inputs):
    C = CFG
    in_maps = stage_inputs(C, inputs)
    nc = build_nc(C)
    res = run_bass_kernel_spmd(nc, in_maps, core_ids=list(range(C["n_cores"])))
    return assemble_output(C, res.results)


# version 14
# speedup vs baseline: 1.0201x; 1.0201x over previous
import math
from contextlib import ExitStack

import numpy as np

import concourse.bass as bass
import concourse.mybir as mybir
from concourse.bass_utils import run_bass_kernel_spmd
from concourse.bass_core_rust import IndirectOffsetOnAxis

F32 = mybir.dt.float32
BF16 = mybir.dt.bfloat16
I32 = mybir.dt.int32
AF = mybir.ActivationFunctionType
ALU = mybir.AluOpType
AX = mybir.AxisListType

NEG = -1.0e9


def make_cfg(d=2048, t=2048, b=4, nh=16, hid=5461):
    dh = 128
    assert d == nh * dh and t == nh * 128
    C = dict(d=d, t=t, b=b, nh=nh, dh=dh, hid=hid)
    C["n_cores"] = 2 * b
    C["T"] = t // 2
    C["n_dt"] = d // 128
    C["n_tt"] = C["T"] // 128
    C["nh_c"] = nh // 2
    C["n_hc"] = (hid + 127) // 128
    C["hidp"] = C["n_hc"] * 128
    C["rpb"] = 128 // nh
    C["n_qt"] = t // 128
    C["TG"] = min(512, C["T"])
    C["n_tg"] = C["T"] // C["TG"]
    C["QD"] = min(512, d)
    C["n_qd"] = d // C["QD"]
    C["KQ"] = min(256, C["T"])
    C["n_kq"] = C["T"] // C["KQ"]
    C["WCH"] = min(512, C["n_dt"] * 128)
    C["SW"] = min(t, 2048)
    C["scale"] = 1.0 / math.sqrt(dh)
    C["Nw"] = hid * d
    assert (C["Nw"] // 8) % 128 == 0
    C["Ns"] = C["Nw"] // 8 // 128
    C["SCH"] = min(C["WCH"], C["Ns"])
    assert C["QD"] <= C["WCH"]
    C["hw"] = C["nh_c"] * dh
    nq = C["n_qt"]
    if nq >= 16:
        C["CH"] = [(0, 8), (8, 12), (12, 15), (15, 16)]
    elif nq >= 2:
        C["CH"] = [(0, nq // 2), (nq // 2, nq)]
    else:
        C["CH"] = [(0, nq)]
    return C


CFG = make_cfg()


class PB:

    ENGS = ("te", "act", "dve", "gp", "sy")

    def __init__(self):
        self.q = {e: [] for e in self.ENGS}
        self.cnt = {}
        for s in ("te", "act", "dve", "gpc", "cc"):
            self.cnt[s] = 0
        self.hw = {}
        self.pending = {e: None for e in self.ENGS}

    def sem(self, name):
        if name not in self.cnt:
            self.cnt[name] = 0
        return name

    def wait(self, eng, sem, val):
        if val is None or val <= 0:
            return
        if self.hw.get((eng, sem), 0) >= val:
            return
        self.hw[(eng, sem)] = val
        self.q[eng].append(("w", sem, val))

    def fence(self):
        snap = dict(self.cnt)
        for e in self.ENGS:
            self.pending[e] = snap

    def on(self, eng, fn, inc=None, by=1, waits=()):
        if self.pending[eng] is not None:
            snap, self.pending[eng] = self.pending[eng], None
            for s, v in snap.items():
                self.wait(eng, s, v)
        for s, v in waits:
            self.wait(eng, s, v)
        post = None
        if inc is not None:
            self.sem(inc)
            self.cnt[inc] += by
            post = self.cnt[inc]
        self.q[eng].append(("i", fn, inc, by))
        return post

    def replay(self, eng_name, e, sems):
        for item in self.q[eng_name]:
            if item[0] == "w":
                e.wait_ge(sems[item[1]], item[2])
            else:
                _, fn, inc, by = item
                ins = fn(e)
                if inc is not None:
                    ins.then_inc(sems[inc], by)


class Arena:

    def __init__(self, tensor, cols):
        self.t = tensor
        self.cols = cols
        self.used = []

    def region(self, name, off, n):
        assert off + n <= self.cols, (name, off, n, self.cols)
        self.used.append((name, off, n))
        return self.t[:, off:off + n]


def build_nc(C):
    nc = bass.Bass("TRN2", target_bir_lowering=False)
    d, t, T = C["d"], C["t"], C["T"]
    nh, dh, nh_c, hw = C["nh"], C["dh"], C["nh_c"], C["hw"]
    n_dt, n_tt, n_hc = C["n_dt"], C["n_tt"], C["n_hc"]
    TG, n_tg, QD, n_qd = C["TG"], C["n_tg"], C["QD"], C["n_qd"]
    KQ, n_kq = C["KQ"], C["n_kq"]
    n_qt, rpb, SW = C["n_qt"], C["rpb"], C["SW"]
    WCH, SCH = C["WCH"], C["SCH"]
    n_wch = (n_dt * 128) // WCH
    dpc = WCH // 128

    ap = {}

    def din(name, shape, dt=F32):
        ap[name] = nc.dram_tensor(name, list(shape), dt, kind="ExternalInput").ap()

    din("xT", (128, n_dt, T))
    din("w1t", (128, n_hc, n_dt * 128))
    din("w3t", (128, n_hc, n_dt * 128))
    din("w2q", (128, n_qd, n_hc, QD))
    din("wqt", (128, n_dt, n_dt * 128))
    din("wkt", (128, nh_c, n_dt * 128))
    din("wvt", (128, n_dt, hw))
    din("wot", (128, n_dt, n_dt * 128))
    din("bqt", (128, n_dt))
    din("bkt", (128, nh_c))
    din("bvr", (1, hw))
    din("bot", (128, n_dt))
    din("w1s", (128, C["Ns"]))
    din("w3s", (128, C["Ns"]))
    din("w2s", (128, C["Ns"]))
    din("trib", (128, 128), BF16)
    din("iden", (128, 512), BF16)
    din("ones_d", (128, 1), BF16)
    din("ones_f", (128, 1))
    din("ones_r", (1, 128))
    din("ones_rb", (1, 128), BF16)
    din("gidx", (128, 2 * n_tt), I32)
    din("halfc", (128, 2))
    yT = nc.dram_tensor("yT", [128, n_dt, T], F32, kind="ExternalOutput").ap()

    agx_in = nc.dram_tensor("agx_in", [128, n_dt * T], BF16)
    agx_out = nc.dram_tensor("agx_out", [2 * 128, n_dt * T], BF16)
    aga_in = nc.dram_tensor("aga_in", [t, hw], BF16)
    aga_out = nc.dram_tensor("aga_out", [2 * t, hw], BF16)
    ar_in = nc.dram_tensor("ar_in", [3, 1], F32)
    ar_out = nc.dram_tensor("ar_out", [3, 1], F32)
    t1c = nc.dram_tensor("t1c", [128, n_hc, n_dt * 128], BF16)
    t3c = nc.dram_tensor("t3c", [128, n_hc, n_dt * 128], BF16)
    t2c = nc.dram_tensor("t2c", [128, n_qd, n_hc, QD], BF16)
    wqc = nc.dram_tensor("wqc", [128, n_dt, n_dt * 128], BF16)
    wkc = nc.dram_tensor("wkc", [128, nh_c, n_dt * 128], BF16)
    wvc = nc.dram_tensor("wvc", [128, n_dt, hw], BF16)
    woc = nc.dram_tensor("woc", [128, n_dt, n_dt * 128], BF16)

    pb = PB()
    st = ExitStack()

    def sb(name, shape, dt=F32):
        return st.enter_context(nc.sbuf_tensor(name, list(shape), dt))

    strm = sb("strm", (128, n_dt * T))
    AR_COLS = 65536 if d >= 2048 else 32768
    arena_t = sb("arena", (128, AR_COLS), BF16)
    A = Arena(arena_t, AR_COLS)
    SCR = max(WCH, TG)
    wf32 = sb("wf32", (128, 4 * WCH + 3 * SCR))
    trib = sb("trib_s", (128, 128), BF16)
    onesd = sb("onesd", (128, 1), BF16)
    onesf = sb("onesf", (128, 1))
    onesr = sb("onesr", (1, 128))
    onesrb = sb("onesrb", (1, 128), BF16)
    scal = sb("scal", (128, 8))
    bqt = sb("bqt_s", (128, n_dt))
    bkt = sb("bkt_s", (128, nh_c))
    bot = sb("bot_s", (128, n_dt))
    gidx = sb("gidx_s", (128, 2 * n_tt), I32)
    part = sb("part", (128, 8))
    srow = sb("srow", (1, 16))
    mx = sb("mx", (128, 12))
    halfc = sb("halfc_s", (128, 2))

    CON = 0
    iden4 = A.region("iden4", CON, 512)
    idenT = iden4[:, 0:128]
    bvr = A.region("bvr", CON + 512, hw)[0:1, :]
    outsb_base = CON + 512 + hw
    outsb = A.region("outsb", outsb_base, 512)
    arowb = arena_t[0:1, outsb_base + 128:outsb_base + 256]
    mcolb = arena_t[:, outsb_base + 384:outsb_base + 386]
    BASE = outsb_base + 512

    G_OFF = BASE
    G_COLS = n_hc * T
    TB_OFF = G_OFF + G_COLS
    X2_OFF = TB_OFF + 5 * WCH + TG
    X2_COLS = n_dt * T
    g_r = A.region("g", G_OFF, G_COLS)
    tring = [A.region("tr%d" % i, TB_OFF + i * WCH, WCH) for i in range(4)]
    abf_r = A.region("abf", TB_OFF + 4 * WCH, WCH)
    sA_r = A.region("sA", TB_OFF + 5 * WCH, TG)
    x2T = A.region("x2T", X2_OFF, X2_COLS)
    assert X2_OFF + X2_COLS <= AR_COLS, (X2_OFF, X2_COLS, AR_COLS)
    xbf = arena_t[:, G_OFF:G_OFF + n_dt * T]
    xsq = arena_t[:, G_OFF + n_dt * T:G_OFF + 2 * n_dt * T]
    assert 2 * n_dt * T <= G_COLS + 4 * WCH + TG

    QM_OFF = G_OFF
    K_OFF = QM_OFF + n_dt * T
    V_OFF = K_OFF + nh_c * t
    XQ_OFF = V_OFF + (t // 128) * hw
    P_OFF = XQ_OFF + 2 * n_dt * KQ
    PT_OFF = P_OFF + SW
    WB_OFF = PT_OFF + SW
    qmatT = arena_t[:, QM_OFF:QM_OFF + n_dt * T]
    K_r = arena_t[:, K_OFF:K_OFF + nh_c * t]
    V_r = arena_t[:, V_OFF:V_OFF + (t // 128) * hw]
    x2qb = [arena_t[:, XQ_OFF + i * n_dt * KQ:
                    XQ_OFF + (i + 1) * n_dt * KQ] for i in range(2)]
    P_r = arena_t[:, P_OFF:P_OFF + SW]
    Pt_r = arena_t[:, PT_OFF:PT_OFF + SW]
    wq_b = [arena_t[:, K_OFF + i * 2 * n_dt * 128:
                    K_OFF + (i + 1) * 2 * n_dt * 128] for i in range(2)]
    wk_b = [arena_t[:, P_OFF + i * (n_dt * 128):
                    P_OFF + (i + 1) * (n_dt * 128)] for i in range(2)]
    wv_b = [arena_t[:, WB_OFF + i * hw: WB_OFF + (i + 1) * hw] for i in range(2)]
    assert WB_OFF + 2 * hw <= AR_COLS, (WB_OFF + 2 * hw, AR_COLS)
    assert P_OFF + 2 * n_dt * 128 <= WB_OFF + 2 * hw
    o2tok = arena_t[:, K_OFF:K_OFF + n_tt * 2 * hw]
    o2T = arena_t[:, V_OFF:V_OFF + n_dt * T]
    assert V_OFF + n_dt * T <= AR_COLS
    wo_b = [arena_t[:, QM_OFF + i * 2 * n_dt * 128:
                    QM_OFF + (i + 1) * 2 * n_dt * 128] for i in range(2)]

    mb_r = wf32[:, 4 * WCH:4 * WCH + TG]
    rb_r = wf32[:, 4 * WCH + SCR:4 * WCH + SCR + TG]
    ln_tmp = wf32[:, 4 * WCH + 2 * SCR:4 * WCH + 2 * SCR + TG]
    ta_r = wf32[:, 4 * WCH:4 * WCH + WCH]
    tb_r = wf32[:, 4 * WCH + SCR:4 * WCH + SCR + WCH]
    m_row = rb_r[0:1, 0:TG]
    r_row = ln_tmp[0:1, 0:TG]

    def dma(eng, semname, out, in_, waits=()):
        pb.sem(semname)
        return pb.on(eng, lambda e, o=out, i=in_: e.dma_start(out=o, in_=i),
                     inc=semname, by=16, waits=waits)

    def t3(sb_ap, cdim):
        return sb_ap.rearrange("p (c f) -> p c f", c=cdim)

    c_x = dma("sy", "dx", t3(strm[:, :], n_dt), ap["xT"][:, :, :])
    dma("sy", "dcn", trib[:, :], ap["trib"][:, :])
    dma("sy", "dcn", iden4, ap["iden"][:, :])
    dma("sy", "dcn", onesd[:, :], ap["ones_d"][:, :])
    dma("sy", "dcn", onesf[:, :], ap["ones_f"][:, :])
    dma("sy", "dcn", onesr[:, :], ap["ones_r"][:, :])
    dma("sy", "dcn", onesrb[:, :], ap["ones_rb"][:, :])
    dma("sy", "dcn", bqt[:, :], ap["bqt"][:, :])
    dma("sy", "dcn", bkt[:, :], ap["bkt"][:, :])
    dma("sy", "dcn", bot[:, :], ap["bot"][:, :])
    dma("sy", "dcn", gidx[:, :], ap["gidx"][:, :])
    c_const = dma("sy", "dcn", halfc[:, :], ap["halfc"][:, :])
    c_bv = dma("gp", "dbv", bvr, ap["bvr"][:, :])

    n_sch = (C["Ns"] + SCH - 1) // SCH
    slot_busy = [0, 0]
    for j, nm in enumerate(("w1s", "w3s", "w2s")):
        for ch in range(n_sch):
            w = min(SCH, C["Ns"] - ch * SCH)
            sl = (j * n_sch + ch) % 2
            buf = wf32[:, sl * WCH: sl * WCH + w]
            c_l = dma("sy" if sl == 0 else "act", "dsl%d" % sl, buf,
                      ap[nm][:, ch * SCH: ch * SCH + w],
                      waits=[("dve", slot_busy[sl])])
            c_r = pb.on("dve", lambda e, buf=buf: e.reduce_sum(
                part[:, 6:7], buf, axis=AX.X, apply_absolute_value=True),
                inc="dve", waits=[("dsl%d" % sl, c_l)])
            slot_busy[sl] = c_r
            pb.wait("dve", "dve", c_r)
            if ch == 0:
                pb.on("dve", lambda e, j=j: e.tensor_copy(
                    part[:, j:j + 1], part[:, 6:7]), inc="dve")
            else:
                pb.on("dve", lambda e, j=j: e.tensor_tensor(
                    part[:, j:j + 1], part[:, j:j + 1], part[:, 6:7],
                    ALU.add), inc="dve")
            pb.wait("dve", "dve", pb.cnt["dve"])
    c_part = pb.cnt["dve"]

    with nc.psum_tensor([3, 512], F32) as ps_s3:
        c_mm3 = pb.on("te", lambda e: e.matmul(ps_s3[:, 0:1], part[:, 0:3],
                                               onesf[:, 0:1], start=True, stop=True),
                      inc="te", waits=[("dve", c_part), ("dcn", c_const)])
        pb.on("dve", lambda e: e.tensor_copy(part[0:3, 4:5], ps_s3[:, 0:1]),
              inc="dve", waits=[("te", c_mm3)])
    c_p3 = pb.cnt["dve"]
    c_ari = dma("sy", "dar", ar_in[:, :], part[0:3, 4:5],
                waits=[("dve", c_p3)])
    pb.on("gp", lambda e: e.collective_compute(
        "AllReduce", ALU.add, replica_groups=[list(range(C["n_cores"]))],
        ins=[ar_in.ap().opt()], outs=[ar_out.ap().opt()]), inc="cc",
        waits=[("dar", c_ari)])
    c_ar = pb.cnt["cc"]
    c_sr = dma("sy", "dsr", srow[0:1, 0:3],
               ar_out.ap().rearrange("a b -> b a"), waits=[("cc", c_ar)])

    dma("gp", "dcc", wqc.ap()[:, :, :], ap["wqt"][:, :, :],
        waits=[("dsl0", pb.cnt["dsl0"]), ("dsl1", pb.cnt["dsl1"])])
    dma("gp", "dcc", wkc.ap()[:, :, :], ap["wkt"][:, :, :])
    dma("gp", "dcc", wvc.ap()[:, :, :], ap["wvt"][:, :, :])
    c_cc = dma("gp", "dcc", woc.ap()[:, :, :], ap["wot"][:, :, :])

    def emit_scal_finalize():
        Nw = float(C["Nw"])
        pb.on("dve", lambda e: e.reciprocal(srow[0:1, 4:7], srow[0:1, 0:3]),
              inc="dve", waits=[("dsr", c_sr)])
        pb.wait("dve", "dve", pb.cnt["dve"])
        pb.on("dve", lambda e: e.tensor_scalar(srow[0:1, 8:11], srow[0:1, 4:7],
                                               Nw, None, ALU.mult), inc="dve")
        pb.on("dve", lambda e: e.tensor_scalar(srow[0:1, 11:12], srow[0:1, 0:1],
                                               0.5 / Nw, None, ALU.mult), inc="dve")
        pb.on("dve", lambda e: e.tensor_scalar(srow[0:1, 12:13], srow[0:1, 1:2],
                                               1.0 / Nw, None, ALU.mult), inc="dve")
        pb.on("dve", lambda e: e.tensor_scalar(srow[0:1, 13:14], srow[0:1, 2:3],
                                               0.25 / Nw, None, ALU.mult), inc="dve")
        c_fin = pb.cnt["dve"]
        with nc.psum_tensor([128, 512], F32) as ps_b6:
            c_b6 = pb.on("te", lambda e: e.matmul(
                ps_b6[:, 0:6], onesr[:, 0:128], srow[0:1, 8:14],
                start=True, stop=True), inc="te",
                waits=[("dve", c_fin), ("dcn", c_const)])
            pb.on("dve", lambda e: e.tensor_copy(scal[:, 0:6], ps_b6[:, 0:6]),
                  inc="dve", waits=[("te", c_b6)])
        return pb.cnt["dve"]

    INV1, INV3, INV2, HS1, HS3, FS2 = range(6)

    def emit_ln(extra_waits=(), stat_waits=()):
        c0 = pb.on("act", lambda e: e.activation(xbf[:, :], strm[:, :], AF.Copy),
                   inc="act", waits=list(extra_waits))
        c1 = pb.on("act", lambda e: e.activation(xsq[:, :], strm[:, :], AF.Square),
                   inc="act", waits=list(extra_waits))
        for tg_i in range(n_tg):
            t0 = tg_i * TG
            with nc.psum_tensor([1, max(TG, 512)], F32) as ps_m, \
                 nc.psum_tensor([1, max(TG, 512)], F32) as ps_q, \
                 nc.psum_tensor([128, max(TG, 512)], F32) as ps_mb, \
                 nc.psum_tensor([128, max(TG, 512)], F32) as ps_rb, \
                 nc.psum_tensor([128, max(TG, 512)], F32) as ps_tmp, \
                 nc.psum_tensor([1, max(TG, 512)], F32) as ln_s:
                for dt_i in range(n_dt):
                    pb.on("te", lambda e, i=dt_i, t0=t0: e.matmul(
                        ps_m[:, 0:TG], onesd[:, 0:1], xbf[:, i * T + t0:i * T + t0 + TG],
                        start=(i == 0), stop=(i == n_dt - 1)),
                        inc=("te" if dt_i == n_dt - 1 else None),
                        waits=(([("act", c0)] + list(stat_waits))
                               if dt_i == 0 else []))
                cm = pb.cnt["te"]
                for dt_i in range(n_dt):
                    pb.on("te", lambda e, i=dt_i, t0=t0: e.matmul(
                        ps_q[:, 0:TG], onesd[:, 0:1], xsq[:, i * T + t0:i * T + t0 + TG],
                        start=(i == 0), stop=(i == n_dt - 1)),
                        inc=("te" if dt_i == n_dt - 1 else None),
                        waits=([("act", c1)] if dt_i == 0 else []))
                cq = pb.cnt["te"]
                pb.on("dve", lambda e: e.tensor_scalar(m_row, ps_m[:, 0:TG], 1.0 / d,
                                                       None, ALU.mult), inc="dve",
                      waits=[("te", cm), ("gpc", pb.cnt["gpc"])])
                pb.wait("dve", "dve", pb.cnt["dve"])
                pb.on("dve", lambda e: e.tensor_tensor(r_row, m_row, m_row,
                                                       ALU.mult), inc="dve")
                pb.wait("dve", "dve", pb.cnt["dve"])
                pb.on("dve", lambda e: e.tensor_scalar(
                    r_row, r_row, -1.0, None, ALU.mult), inc="dve")
                pb.wait("dve", "dve", pb.cnt["dve"])
                pb.on("dve", lambda e: e.scalar_tensor_tensor(
                    r_row, ps_q[:, 0:TG], 1.0 / d, r_row, ALU.mult, ALU.add),
                    inc="dve", waits=[("te", cq)])
                pb.wait("dve", "dve", pb.cnt["dve"])
                cv = pb.on("dve", lambda e: e.tensor_scalar(r_row, r_row, 1e-5,
                                                            None, ALU.add),
                           inc="dve")
                cs = pb.on("act", lambda e: e.activation(
                    ln_s[0:1, 0:TG], r_row, AF.Sqrt), inc="act",
                    waits=[("dve", cv)])
                pb.on("dve", lambda e: e.reciprocal(mb_r[0:1, 0:TG],
                                                    ln_s[0:1, 0:TG]),
                      inc="dve", waits=[("act", cs), ("gpc", pb.cnt["gpc"])])
                pb.wait("dve", "dve", pb.cnt["dve"])
                pb.on("dve", lambda e: e.tensor_tensor(
                    ln_s[0:1, 0:TG], mb_r[0:1, 0:TG], mb_r[0:1, 0:TG], ALU.mult),
                    inc="dve")
                pb.wait("dve", "dve", pb.cnt["dve"])
                pb.on("dve", lambda e: e.tensor_tensor(
                    ln_s[0:1, 0:TG], ln_s[0:1, 0:TG], r_row, ALU.mult),
                    inc="dve")
                pb.wait("dve", "dve", pb.cnt["dve"])
                pb.on("dve", lambda e: e.tensor_scalar(
                    ln_s[0:1, 0:TG], ln_s[0:1, 0:TG], -0.5, 1.5, ALU.mult,
                    ALU.add), inc="dve")
                pb.wait("dve", "dve", pb.cnt["dve"])
                cr = pb.on("dve", lambda e: e.tensor_tensor(
                    r_row, mb_r[0:1, 0:TG], ln_s[0:1, 0:TG], ALU.mult),
                    inc="dve")
                c_rb1 = pb.on("te", lambda e: e.matmul(
                    ps_mb[:, 0:TG], onesr[:, 0:128], m_row, start=True, stop=True),
                    inc="te", waits=[("dve", cr)])
                c_rb2 = pb.on("te", lambda e: e.matmul(
                    ps_rb[:, 0:TG], onesr[:, 0:128], r_row, start=True, stop=True),
                    inc="te")
                pb.on("dve", lambda e: e.tensor_copy(mb_r, ps_mb[:, 0:TG]), inc="dve",
                      waits=[("te", c_rb1), ("gpc", pb.cnt["gpc"])])
                pb.on("dve", lambda e: e.tensor_copy(rb_r, ps_rb[:, 0:TG]), inc="dve",
                      waits=[("te", c_rb2)])
                c_bc = pb.cnt["dve"]
                c_bc_gp = c_bc
                for dt_i in range(n_dt):
                    if dt_i % 2 == 0:
                        c_t = pb.on("dve", lambda e, i=dt_i, t0=t0:
                                    e.tensor_tensor(
                                        ps_tmp[:, 0:TG],
                                        strm[:, i * T + t0:i * T + t0 + TG],
                                        mb_r, ALU.subtract), inc="dve",
                                    waits=[("dve", c_bc)])
                        pb.wait("dve", "dve", c_t)
                        pb.on("dve", lambda e, i=dt_i, t0=t0: e.tensor_tensor(
                            x2T[:, i * T + t0:i * T + t0 + TG], ps_tmp[:, 0:TG],
                            rb_r, ALU.mult), inc="dve")
                        pb.wait("dve", "dve", pb.cnt["dve"])
                    else:
                        c_t = pb.on("gp", lambda e, i=dt_i, t0=t0:
                                    e.tensor_tensor(
                                        ln_tmp[:, 0:TG],
                                        strm[:, i * T + t0:i * T + t0 + TG],
                                        mb_r, ALU.subtract), inc="gpc",
                                    waits=[("dve", c_bc)])
                        pb.wait("gp", "gpc", c_t)
                        pb.on("gp", lambda e, i=dt_i, t0=t0: e.tensor_tensor(
                            x2T[:, i * T + t0:i * T + t0 + TG], ln_tmp[:, 0:TG],
                            rb_r, ALU.mult), inc="gpc")
                        pb.wait("gp", "gpc", pb.cnt["gpc"])
        return pb.cnt["dve"]

    c_scal_done = [0]
    abf_busy = [0]
    tdve_busy = [0]

    def tern(src, dst, width, inv_col, waits):
        ab = abf_r[:, 0:width]
        ca = pb.on("act", lambda e, ab=ab: e.activation(
            ab, src, AF.Sign, bias=halfc[:, 0:1],
            scale=scal[:, inv_col:inv_col + 1]), inc="act",
            waits=list(waits) + [("dve", c_scal_done[0]),
                                 ("dve", abf_busy[0])])
        cb = pb.on("act", lambda e: e.activation(
            dst, src, AF.Sign, bias=halfc[:, 1:2],
            scale=scal[:, inv_col:inv_col + 1]), inc="act")
        cw = pb.on("dve", lambda e, ab=ab: e.tensor_tensor(dst, dst, ab,
                                                            ALU.add), inc="dve",
                   waits=[("act", cb)])
        abf_busy[0] = cw
        return ("dve", cw), ("act", cb)

    def tern_dve(src, dst, width, inv_col, waits):
        ta = ta_r[:, 0:width]
        tb = tb_r[:, 0:width]
        pb.on("dve", lambda e, ta=ta: e.tensor_scalar(
            ta, src, scal[:, inv_col:inv_col + 1], 0.5, ALU.mult, ALU.is_ge),
            inc="dve", waits=list(waits) + [("dve", c_scal_done[0]),
                                            ("dve", tdve_busy[0])])
        pb.on("dve", lambda e, tb=tb: e.tensor_scalar(
            tb, src, scal[:, inv_col:inv_col + 1], -0.5, ALU.mult, ALU.is_le),
            inc="dve")
        pb.wait("dve", "dve", pb.cnt["dve"])
        cw = pb.on("dve", lambda e, ta=ta, tb=tb: e.scalar_tensor_tensor(
            dst, ta, 1.0, tb, ALU.mult, ALU.subtract), inc="dve")
        tdve_busy[0] = cw
        return ("dve", cw), ("dve", cw)

    ffn_idx = [0]

    def emit_ffn(first, pre_waits=(), post_ln_cb=None, ln_stat_waits=()):
        c_ln = emit_ln(extra_waits=pre_waits, stat_waits=ln_stat_waits)
        if post_ln_cb is not None:
            post_ln_cb()
        pb.fence()
        with ExitStack() as hs:
            psb = [hs.enter_context(nc.psum_tensor("psb%d_%d" % (ffn_idx[0], i),
                                                    [128, TG], F32))
                   for i in range(8)]

            def hbank(mi, par, tg_i):
                return psb[par * 4 + mi * 2 + tg_i]

            w_busy = [None] * 4
            t_busy = [[], [], [], []]
            ev_busy = {}
            sA_busy = [0]
            ci = 0
            c_g = [0]

            def emit_evac(hc):
                par = hc % 2
                c_mm = ev_busy.get(("mm", hc), pb.cnt["te"])
                for tg_i in range(n_tg):
                    gdst = g_r[:, hc * T + tg_i * TG: hc * T + tg_i * TG + TG]
                    c_a = pb.on("act", lambda e, par=par, tg_i=tg_i: e.activation(
                        sA_r[:, 0:TG], hbank(0, par, tg_i)[:, :], AF.Silu,
                        scale=scal[:, HS1:HS1 + 1]), inc="act",
                        waits=[("te", c_mm), ("dve", sA_busy[0])])
                    c_b = pb.on("act", lambda e, par=par, tg_i=tg_i, gdst=gdst:
                                e.activation(gdst, hbank(1, par, tg_i)[:, :],
                                             AF.Identity,
                                             scale=scal[:, HS3:HS3 + 1]),
                                inc="act")
                    c_g[0] = pb.on("dve", lambda e, gdst=gdst: e.tensor_tensor(
                        gdst, gdst, sA_r[:, 0:TG], ALU.mult), inc="dve",
                        waits=[("act", c_b), ("act", c_a)])
                    sA_busy[0] = c_g[0]
                ev_busy[(par, "done")] = pb.cnt["act"]

            srcs = ((ap["w1t"], t1c, INV1, "act"), (ap["w3t"], t3c, INV3, "dve"))
            for hc in range(n_hc):
                par = hc % 2
                for mi, (wsrc, cache, inv, path) in enumerate(srcs):
                    for ch in range(n_wch):
                        ws = ci % 4
                        sl = ci % 4
                        ci += 1
                        if first:
                            wbuf = wf32[:, ws * WCH: ws * WCH + WCH]
                            ldq = "act" if path == "act" else "sy"
                            c_l = dma(ldq, "dwA%d" % ws, wbuf,
                                      wsrc[:, hc, ch * WCH:(ch + 1) * WCH],
                                      waits=([w_busy[ws]] if w_busy[ws] else []))
                            for s_, v_ in t_busy[sl]:
                                pb.wait("act" if path == "act" else "dve", s_, v_)
                            if path == "act":
                                c_t, frees = tern(wbuf, tring[sl], WCH, inv,
                                                  [("dwA%d" % ws, c_l)])
                            else:
                                c_t, frees = tern_dve(wbuf, tring[sl], WCH, inv,
                                                      [("dwA%d" % ws, c_l)])
                            w_busy[ws] = frees
                            c_st = dma("gp", "dtc%d" % sl,
                                       cache.ap()[:, hc, ch * WCH:(ch + 1) * WCH],
                                       tring[sl],
                                       waits=[c_t])
                            tb_new = [("dtc%d" % sl, c_st)]
                            mm_wait = c_t
                        else:
                            ldq = "sy" if sl % 2 == 0 else "act"
                            for s_, v_ in t_busy[sl]:
                                pb.wait(ldq, s_, v_)
                            c_l = dma(ldq, "dtA%d" % sl, tring[sl],
                                      cache.ap()[:, hc, ch * WCH:(ch + 1) * WCH])
                            tb_new = []
                            mm_wait = ("dtA%d" % sl, c_l)
                        for k in range(dpc):
                            dt_i = ch * dpc + k
                            for tg_i in range(n_tg):
                                wts = []
                                if k == 0 and tg_i == 0:
                                    wts = [mm_wait, ("dve", c_ln)]
                                    if ch == 0 and (par, "done") in ev_busy:
                                        wts.append(("act", ev_busy[(par, "done")]))
                                last = (k == dpc - 1 and tg_i == n_tg - 1)
                                pb.on("te", lambda e, k=k, tg_i=tg_i, mi=mi,
                                      par=par, sl=sl, dt_i=dt_i:
                                      e.matmul(hbank(mi, par, tg_i)[:, :],
                                               tring[sl][:, k * 128:(k + 1) * 128],
                                               x2T[:, dt_i * T + tg_i * TG:
                                                   dt_i * T + tg_i * TG + TG],
                                               start=(dt_i == 0),
                                               stop=(dt_i == n_dt - 1)),
                                      inc=("te" if last else None), waits=wts)
                        t_busy[sl] = tb_new + [("te", pb.cnt["te"])]
                ev_busy[("mm", hc)] = pb.cnt["te"]
                if hc > 0:
                    emit_evac(hc - 1)
            emit_evac(n_hc - 1)
        pb.fence()
        n_dq = QD // 128
        with ExitStack() as fs:
            psf = [[fs.enter_context(nc.psum_tensor(
                        "psf%d_%d_%d" % (ffn_idx[0], i, j), [128, TG], F32))
                    for j in range(n_tg)] for i in range(n_dq)]
            w_busy = [None] * 4
            t_busy = [[], [], [], []]
            prev_ev = 0
            ci2 = 0
            for qd_i in range(n_qd):
                for hc in range(n_hc):
                    ws = ci2 % 4
                    sl = ci2 % 4
                    ci2 += 1
                    if first:
                        wbuf = wf32[:, ws * WCH: ws * WCH + QD]
                        ldq = "sy" if ws % 2 == 0 else "act"
                        c_l = dma(ldq, "dwA%d" % ws, wbuf,
                                  ap["w2q"][:, qd_i, hc, :],
                                  waits=([w_busy[ws]] if w_busy[ws] else []))
                        for s_, v_ in t_busy[sl]:
                            pb.wait("act", s_, v_)
                        c_t, frees = tern(wbuf, tring[sl][:, 0:QD], QD, INV2,
                                          [("dwA%d" % ws, c_l)])
                        w_busy[ws] = frees
                        c_st = dma("gp", "dtc%d" % sl,
                                   t2c.ap()[:, qd_i, hc, :], tring[sl][:, 0:QD],
                                   waits=[c_t])
                        tb_new = [("dtc%d" % sl, c_st)]
                        mm_wait = c_t
                    else:
                        ldq = "sy" if sl % 2 == 0 else "act"
                        for s_, v_ in t_busy[sl]:
                            pb.wait(ldq, s_, v_)
                        c_l = dma(ldq, "dtA%d" % sl, tring[sl][:, 0:QD],
                                  t2c.ap()[:, qd_i, hc, :])
                        tb_new = []
                        mm_wait = ("dtA%d" % sl, c_l)
                    for dq in range(n_dq):
                        for tg_i in range(n_tg):
                            wts = []
                            if dq == 0 and tg_i == 0:
                                wts = [mm_wait, ("dve", c_g[0])]
                                if hc == 0 and qd_i > 0:
                                    wts.append(("dve", prev_ev))
                            last = (dq == n_dq - 1 and tg_i == n_tg - 1)
                            pb.on("te", lambda e, dq=dq, tg_i=tg_i, hc=hc, sl=sl:
                                  e.matmul(psf[dq][tg_i][:, :],
                                           tring[sl][:, dq * 128:(dq + 1) * 128],
                                           g_r[:, hc * T + tg_i * TG:
                                               hc * T + tg_i * TG + TG],
                                           start=(hc == 0), stop=(hc == n_hc - 1)),
                                  inc=("te" if last else None), waits=wts)
                    t_busy[sl] = tb_new + [("te", pb.cnt["te"])]
                c_fq = pb.cnt["te"]
                for dq in range(n_dq):
                    ot = qd_i * n_dq + dq
                    for tg_i in range(n_tg):
                        pb.on("dve", lambda e, dq=dq, tg_i=tg_i, ot=ot:
                              e.scalar_tensor_tensor(
                                  strm[:, ot * T + tg_i * TG:
                                       ot * T + tg_i * TG + TG],
                                  psf[dq][tg_i][:, :], scal[:, FS2:FS2 + 1],
                                  strm[:, ot * T + tg_i * TG:
                                       ot * T + tg_i * TG + TG],
                                  ALU.mult, ALU.add), inc="dve",
                              waits=[("te", c_fq)])
                prev_ev = pb.cnt["dve"]
        pb.fence()
        ffn_idx[0] += 1

    def emit_attn():
        c_ln2 = emit_ln()
        pb.fence()
        c_agw = dma("sy", "dagw", agx_in.ap()[:, :], x2T[:, :])
        pairs = [[2 * i, 2 * i + 1] for i in range(C["n_cores"] // 2)]
        pb.on("gp", lambda e: e.collective_compute(
            "AllGather", ALU.bypass, replica_groups=pairs,
            ins=[agx_in.ap().opt()], outs=[agx_out.ap().opt()]), inc="cc",
            waits=[("dagw", c_agw)])
        c_agx = pb.cnt["cc"]

        with ExitStack() as qs:
            qb = [qs.enter_context(nc.psum_tensor("qb%d" % i, [128, TG], F32))
                  for i in range(8)]
            wq_busy = [0, 0]
            bank_busy = {}
            for og in range(0, n_dt, 2):
                sl = (og // 2) % 2
                c_l = dma("sy" if sl == 0 else "act", "dwq%d" % sl, wq_b[sl],
                          wqc.ap()[:, og:og + 2, :]
                          .rearrange("p o f -> p (o f)"),
                          waits=[("te", wq_busy[sl]), ("dcc", c_cc)])
                for oi in range(2):
                    ot = og + oi
                    for tg_i in range(n_tg):
                        bank = qb[sl * 4 + oi * n_tg + tg_i]
                        bkey = sl * 4 + oi * n_tg + tg_i
                        for dc in range(n_dt):
                            wts = []
                            if dc == 0:
                                wts = [("dwq%d" % sl, c_l), ("dve", c_ln2)]
                                if bkey in bank_busy:
                                    wts.append(("act", bank_busy[bkey]))
                            pb.on("te", lambda e, oi=oi, tg_i=tg_i, dc=dc, sl=sl,
                                  bank=bank:
                                  e.matmul(bank[:, :],
                                           wq_b[sl][:, (oi * n_dt + dc) * 128:
                                                    (oi * n_dt + dc + 1) * 128],
                                           x2T[:, dc * T + tg_i * TG:
                                               dc * T + tg_i * TG + TG],
                                           start=(dc == 0), stop=(dc == n_dt - 1)),
                                  inc=("te" if dc == n_dt - 1 else None),
                                  waits=wts)
                        c_mm = pb.cnt["te"]
                        qsv = qmatT.rearrange("p (H G r c) -> p H G r c",
                                              G=n_qt, r=rpb, c=nh)
                        nhb = TG // 128
                        dst = qsv[:, tg_i * nhb:(tg_i + 1) * nhb, :, :, ot]
                        bank_busy[bkey] = pb.on(
                            "act", lambda e, dst=dst, ot=ot, bank=bank:
                            e.activation(dst, bank[:, :], AF.Identity,
                                         bias=bqt[:, ot:ot + 1]),
                            inc="act", waits=[("te", c_mm)])
                wq_busy[sl] = pb.cnt["te"]
        c_q = pb.cnt["act"]
        pb.fence()

        with ExitStack() as ks:
            kb = [ks.enter_context(nc.psum_tensor("kb%d" % i,
                                                  [128, max(KQ, 512)], F32))
                  for i in range(4)]
            vb = [ks.enter_context(nc.psum_tensor("vb%d" % i, [128, 512], F32))
                  for i in range(4)]
            GW = min(512, hw)
            n_gw = hw // GW
            kb_busy = {}
            vb_busy = {}
            wk_busy = [0, 0]
            wv_busy = [0, 0]
            x2q_busy = [0, 0]
            un = 0
            for half in range(2):
                for kq in range(n_kq):
                    x2q = x2qb[un % 2]
                    upar = un % 2
                    un += 1
                    c_xgl = []
                    for dcl in range(n_dt):
                        c_xgl.append(("dxq%d_%d" % (dcl, upar), dma(
                            "sy", "dxq%d_%d" % (dcl, upar),
                            x2q[:, dcl * KQ:(dcl + 1) * KQ],
                            agx_out.ap()[half * 128:(half + 1) * 128,
                                         dcl * T + kq * KQ:
                                         dcl * T + kq * KQ + KQ],
                            waits=([("cc", c_agx), ("te", x2q_busy[upar])]
                                   if dcl == 0 else ()))))
                    for hh in range(nh_c):
                        sl = hh % 2
                        c_wk = dma("sy", "dwk%d" % sl, wk_b[sl],
                                   wkc.ap()[:, hh, :],
                                   waits=[("te", wk_busy[sl]), ("dcc", c_cc)])
                        bank = kb[hh % 4]
                        for dc in range(n_dt):
                            wts = [c_xgl[dc]]
                            if dc == 0:
                                wts += [("dwk%d" % sl, c_wk)]
                                if (hh % 4) in kb_busy:
                                    wts.append(("act", kb_busy[hh % 4]))
                            pb.on("te", lambda e, hh=hh, dc=dc, bank=bank,
                                  sl=sl, x2q=x2q:
                                  e.matmul(bank[:, 0:KQ],
                                           wk_b[sl][:, dc * 128:(dc + 1) * 128],
                                           x2q[:, dc * KQ:(dc + 1) * KQ],
                                           start=(dc == 0), stop=(dc == n_dt - 1)),
                                  inc=("te" if dc == n_dt - 1 else None),
                                  waits=wts)
                        c_mm = pb.cnt["te"]
                        wk_busy[sl] = c_mm
                        kb_busy[hh % 4] = pb.on(
                            "act", lambda e, hh=hh, half=half, kq=kq, bank=bank:
                            e.activation(K_r[:, hh * t + half * T + kq * KQ:
                                             hh * t + half * T + kq * KQ + KQ],
                                         bank[:, 0:KQ], AF.Identity,
                                         bias=bkt[:, hh:hh + 1]),
                            inc="act", waits=[("te", c_mm)])
                    n_kt = KQ // 128
                    for gw in range(n_gw):
                        for dc in range(n_dt):
                            sl = dc % 2
                            c_wv = dma("act", "dwv%d" % sl, wv_b[sl][:, 0:GW],
                                       wvc.ap()[:, dc, gw * GW:(gw + 1) * GW],
                                       waits=[("te", wv_busy[sl]),
                                              ("dcc", c_cc)])
                            for kt in range(n_kt):
                                bank = vb[kt]
                                wts = [c_xgl[dc]]
                                if dc == 0:
                                    wts += [("dwv%d" % sl, c_wv),
                                           ("dbv", c_bv)]
                                    if kt in vb_busy:
                                        wts.append(("dve", vb_busy[kt]))
                                else:
                                    wts = [("dwv%d" % sl, c_wv)]
                                last = (kt == n_kt - 1)
                                pb.on("te", lambda e, dc=dc, kt=kt, gw=gw,
                                      bank=bank, sl=sl, x2q=x2q:
                                      e.matmul(bank[:, 0:GW],
                                               x2q[:, dc * KQ + kt * 128:
                                                   dc * KQ + (kt + 1) * 128],
                                               wv_b[sl][:, 0:GW],
                                               start=(dc == 0), stop=False),
                                      inc=("te" if last else None), waits=wts)
                            wv_busy[sl] = pb.cnt["te"]
                        for kt in range(n_kt):
                            ktile = half * n_tt + kq * n_kt + kt
                            bank = vb[kt]
                            pb.on("te", lambda e, gw=gw, bank=bank:
                                  e.matmul(bank[:, 0:GW], onesrb[:, 0:128],
                                           bvr[0:1, gw * GW:(gw + 1) * GW],
                                           start=False, stop=True),
                                  inc="te")
                            c_mm = pb.cnt["te"]
                            vb_busy[kt] = pb.on(
                                "dve", lambda e, ktile=ktile, gw=gw, bank=bank:
                                e.tensor_copy(V_r[:, ktile * hw + gw * GW:
                                                  ktile * hw + gw * GW + GW],
                                              bank[:, 0:GW]),
                                inc="dve", waits=[("te", c_mm)])
                    x2q_busy[upar] = pb.cnt["te"]
        c_kv_act = pb.cnt["act"]
        c_kv_dve = pb.cnt["dve"]
        pb.fence()

        CH = C["CH"]
        Pt2 = [P_r, Pt_r]
        arow4 = arena_t[0:1, WB_OFF:WB_OFF + 512]
        iters = [(gq, hh) for gq in range(n_qt) for hh in range(nh_c)]
        n_it = len(iters)
        with nc.psum_tensor([128, 512], F32) as ss0, \
             nc.psum_tensor([128, 512], F32) as ss1, \
             nc.psum_tensor([128, 512], F32) as st0, \
             nc.psum_tensor([128, 512], F32) as st1, \
             nc.psum_tensor([128, 512], F32) as po0, \
             nc.psum_tensor([128, 512], F32) as po1, \
             nc.psum_tensor([128, 512], F32) as den_c, \
             nc.psum_tensor([128, 512], F32) as misc:
            ssb = [ss0, ss1]
            stb = [st0, st1]
            pob = [po0, po1]
            S = {}
            ss_busy = [0, 0]
            st_busy = [0, 0]
            osb_busy = [0, 0]
            pob_busy = [0, 0]
            den_free = [0]
            misc_busy = [0]
            pt_free = [0, 0]
            arow_busy = [0]
            mcol_busy = [0, 0]
            mx_busy = [0, 0]
            c_aga = []

            def emitA(i):
                gq, hh = iters[i]
                n_k = (gq + 1) * 128
                nblk = (n_k + 511) // 512
                par = i % 2
                q_ap = qmatT[:, (hh * n_qt + gq) * 128:
                             (hh * n_qt + gq + 1) * 128]
                mb = 4 + 4 * par
                for kb in range(nblk):
                    kb0 = kb * 512
                    w_ = min(512, n_k - kb0)
                    diag = (gq // 4 == kb)
                    bi = kb % 2
                    wts = [("dve", ss_busy[bi]), ("dve", mx_busy[par])]
                    if i == 0 and kb == 0:
                        wts += [("act", c_q), ("act", c_kv_act)]
                    pb.on("te", lambda e, kb0=kb0, w_=w_, q_ap=q_ap, hh=hh,
                          diag=diag, bi=bi:
                          e.matmul(ssb[bi][:, 0:w_], q_ap,
                                   K_r[:, hh * t + kb0: hh * t + kb0 + w_],
                                   start=True, stop=not diag),
                          inc=(None if diag else "te"), waits=wts)
                    if diag:
                        pb.on("te", lambda e, gq=gq, kb0=kb0, bi=bi:
                              e.matmul(ssb[bi][:, gq * 128 - kb0:
                                               (gq + 1) * 128 - kb0],
                                       idenT, trib[:, :],
                                       start=False, stop=True), inc="te")
                    c_blk = pb.cnt["te"]
                    ss_busy[bi] = pb.on(
                        "dve", lambda e, w_=w_, kb=kb, bi=bi, mb=mb:
                        e.reduce_max(mx[:, mb + kb:mb + kb + 1],
                                     ssb[bi][:, 0:w_], axis=AX.X),
                        inc="dve", waits=[("te", c_blk)])
                S[i] = dict(q_ap=q_ap, n_k=n_k, nblk=nblk,
                            c_maxes=pb.cnt["dve"])

            def emitB1a(i):
                gq, hh = iters[i]
                st_ = S[i]
                n_k, nblk, q_ap = st_["n_k"], st_["nblk"], st_["q_ap"]
                par = i % 2
                mb = 4 + 4 * par
                pb.wait("dve", "dve", st_["c_maxes"])
                mcol = mb
                if nblk > 1:
                    pb.on("dve", lambda e, nblk=nblk, mb=mb, par=par:
                          e.reduce_max(mx[:, par:par + 1],
                                       mx[:, mb:mb + nblk], axis=AX.X),
                          inc="dve")
                    pb.wait("dve", "dve", pb.cnt["dve"])
                    mcol = par
                c_nm = pb.on("dve", lambda e, mcol=mcol, par=par:
                             e.tensor_scalar(mcolb[:, par:par + 1],
                                             mx[:, mcol:mcol + 1], -1.0, None,
                                             ALU.mult), inc="dve",
                             waits=[("te", mcol_busy[par])])
                mx_busy[par] = c_nm
                c_nmm = pb.on("te", lambda e, par=par: e.matmul(
                    misc[0:1, 0:512], mcolb[:, par:par + 1], iden4,
                    start=True, stop=True), inc="te",
                    waits=[("dve", c_nm), ("act", misc_busy[0])])
                mcol_busy[par] = c_nmm
                c_nmev = pb.on("act", lambda e: e.activation(
                    arow4[0:1, 0:512], misc[0:1, 0:512], AF.Copy), inc="act",
                    waits=[("te", c_nmm), ("te", arow_busy[0])])
                misc_busy[0] = c_nmev
                st_["c_nmev"] = c_nmev

            def emitB1b(i):
                gq, hh = iters[i]
                st_ = S[i]
                n_k, q_ap = st_["n_k"], st_["q_ap"]
                n_kt = n_k // 128
                c_nmev = st_["c_nmev"]
                exp_of_grp = []
                ptbuf = Pt2[i % 2]
                st_["ptbuf"] = ptbuf
                for b0 in range(0, n_kt, 4):
                    nb = min(4, n_kt - b0)
                    bi = (b0 // 4) % 2
                    bank = stb[bi]
                    for j in range(nb):
                        kt = b0 + j
                        wts = []
                        if j == 0:
                            wts = [("act", st_busy[bi]), ("act", c_nmev)]
                        pb.on("te", lambda e, j=j, kt=kt, hh=hh, q_ap=q_ap,
                              bank=bank:
                              e.matmul(bank[:, j * 128:(j + 1) * 128],
                                       K_r[:, hh * t + kt * 128:
                                           hh * t + (kt + 1) * 128],
                                       q_ap, start=(j == 0), stop=False),
                              inc=None, waits=wts)
                        if kt == gq:
                            pb.on("te", lambda e, j=j, bank=bank:
                                  e.matmul(bank[:, j * 128:(j + 1) * 128],
                                           trib[:, :], idenT,
                                           start=False, stop=False),
                                  inc=None)
                    pb.on("te", lambda e, nb=nb, bank=bank:
                          e.matmul(bank[:, 0:nb * 128],
                                   onesrb[0:1, 0:128], arow4[0:1, 0:nb * 128],
                                   start=False, stop=True), inc="te")
                    c_stp = pb.cnt["te"]
                    c_exp = pb.on("act", lambda e, b0=b0, nb=nb, bank=bank,
                                  ptbuf=ptbuf:
                                  e.activation(
                                      ptbuf[:, b0 * 128:(b0 + nb) * 128],
                                      bank[:, 0:nb * 128], AF.Exp,
                                      scale=C["scale"]), inc="act",
                                  waits=[("te", c_stp),
                                         ("te", pt_free[i % 2])])
                    st_busy[bi] = c_exp
                    exp_of_grp.append(c_exp)
                arow_busy[0] = pb.cnt["te"]
                st_["exps"] = exp_of_grp

            def emitB2(i):
                gq, hh = iters[i]
                st_ = S[i]
                n_k = st_["n_k"]
                n_kt = n_k // 128
                exp_of_grp = st_["exps"]
                ptbuf = st_["ptbuf"]
                oi = i % 2
                obank = pob[oi]
                for kt in range(n_kt):
                    wts = [("act", exp_of_grp[kt // 4])] if kt % 4 == 0 else []
                    if kt == 0:
                        wts += [("dve", c_kv_dve), ("act", pob_busy[oi]),
                                ("dve", den_free[0])]
                    pb.on("te", lambda e, kt=kt, hh=hh, obank=obank,
                          n_kt=n_kt, ptbuf=ptbuf:
                          e.matmul(obank[:, 0:128],
                                   ptbuf[:, kt * 128:(kt + 1) * 128],
                                   V_r[:, kt * hw + hh * 128:
                                       kt * hw + (hh + 1) * 128],
                                   start=(kt == 0), stop=(kt == n_kt - 1)),
                          inc=None, waits=wts)
                    pb.on("te", lambda e, kt=kt, n_kt=n_kt, ptbuf=ptbuf:
                          e.matmul(den_c[:, 0:1],
                                   ptbuf[:, kt * 128:(kt + 1) * 128],
                                   onesd[:, 0:1],
                                   start=(kt == 0), stop=(kt == n_kt - 1)),
                          inc=("te" if kt == n_kt - 1 else None))
                c_pv = pb.cnt["te"]
                pt_free[i % 2] = c_pv
                pb.wait("dve", "dve", pb.cnt["dve"])
                c_rc = pb.on("dve", lambda e, oi=oi: e.reciprocal(
                    mx[:, 2 + oi:3 + oi], den_c[:, 0:1]), inc="dve",
                    waits=[("te", c_pv)])
                den_free[0] = c_rc
                osb = outsb[:, oi * 256: oi * 256 + 128]
                c_oe = pb.on("act", lambda e, obank=obank, osb=osb, oi=oi:
                             e.activation(osb, obank[:, 0:128], AF.Identity,
                                          scale=mx[:, 2 + oi:3 + oi]),
                             inc="act",
                             waits=[("te", c_pv), ("dve", c_rc),
                                    ("dos%d" % oi, osb_busy[oi])])
                pob_busy[oi] = c_oe
                osb_busy[oi] = dma(
                    "sy", "dos%d" % oi,
                    aga_in.ap()[gq * 128:(gq + 1) * 128,
                                hh * 128:(hh + 1) * 128],
                    osb, waits=[("act", c_oe)])
                del S[i]
                if hh == nh_c - 1:
                    for (q0, q1) in CH:
                        if gq + 1 == q1:
                            r0, r1 = q0 * 128, q1 * 128
                            pb.on("gp", lambda e, r0=r0, r1=r1:
                                  e.collective_compute(
                                      "AllGather", ALU.bypass,
                                      replica_groups=pairs,
                                      ins=[aga_in.ap()[r0:r1, :].opt()],
                                      outs=[aga_out.ap()[2 * r0:2 * r1, :]
                                            .opt()]), inc="cc",
                                  waits=[("dos0", osb_busy[0]),
                                         ("dos1", osb_busy[1])])
                            c_aga.append(pb.cnt["cc"])

            for s in range(n_it + 2):
                if 1 <= s and s - 1 < n_it:
                    emitB1a(s - 1)
                if s < n_it:
                    emitA(s)
                if 1 <= s and s - 1 < n_it:
                    emitB1b(s - 1)
                if s >= 2:
                    emitB2(s - 2)
        pb.fence()

        for j in range(2):
            for tt in range(n_tt):
                col = j * n_tt + tt
                pb.on("gp", lambda e, j=j, tt=tt, col=col: e.indirect_dma_start(
                    o2tok[:, (tt * 2 + j) * hw: (tt * 2 + j + 1) * hw], None,
                    aga_out.ap()[:, :],
                    IndirectOffsetOnAxis(gidx[:, col:col + 1], 0)),
                    inc="dga", by=16, waits=[("cc", c_aga[-1])])
        c_gath = pb.cnt["dga"]

        with nc.psum_tensor([128, 1024], BF16) as tq0, \
             nc.psum_tensor([128, 1024], BF16) as tq1:
            tqb = [tq0, tq1]
            busy = [0, 0]
            bi = 0
            n_dd = 2 * hw // 128
            for tt in range(n_tt):
                for dc0 in range(0, n_dd, 4):
                    nb = min(4, n_dd - dc0)
                    bank = tqb[bi % 2]
                    for j in range(nb):
                        dc = dc0 + j
                        wts = []
                        if j == 0:
                            wts = [("dga", c_gath), ("dve", busy[bi % 2])]
                        pb.on("te", lambda e, tt=tt, dc=dc, j=j, bank=bank:
                              e.transpose(bank[:, j * 128:(j + 1) * 128],
                                          o2tok[:, tt * 2 * hw + dc * 128:
                                                tt * 2 * hw + (dc + 1) * 128],
                                          idenT),
                              inc=("te" if j == nb - 1 else None), waits=wts)
                    c_tp = pb.cnt["te"]
                    o2v = o2T.rearrange("p (c f) -> p c f", c=n_dd)
                    busy[bi % 2] = pb.on(
                        "dve", lambda e, tt=tt, dc0=dc0, nb=nb, bank=bank, o2v=o2v:
                        e.tensor_copy(
                            o2v[:, dc0:dc0 + nb, tt * 128:(tt + 1) * 128],
                            bank[:, 0:nb * 128]
                            .rearrange("p (c f) -> p c f", c=nb)),
                        inc="dve", waits=[("te", c_tp)])
                    bi += 1
        c_o2t = pb.cnt["dve"]
        pb.fence()

        with ExitStack() as osx:
            ob = [osx.enter_context(nc.psum_tensor("ob%d" % i, [128, TG], F32))
                  for i in range(8)]
            wo_busy = [0, 0]
            bank_busy = {}
            for og in range(0, n_dt, 2):
                sl = (og // 2) % 2
                c_l = dma("sy", "dwo%d" % sl, wo_b[sl],
                          woc.ap()[:, og:og + 2, :]
                          .rearrange("p o f -> p (o f)"),
                          waits=[("te", wo_busy[sl]), ("dcc", c_cc)])
                for oi in range(2):
                    ot = og + oi
                    for tg_i in range(n_tg):
                        bank = ob[sl * 4 + oi * n_tg + tg_i]
                        bkey = sl * 4 + oi * n_tg + tg_i
                        for dc in range(n_dt):
                            wts = []
                            if dc == 0:
                                wts = [("dwo%d" % sl, c_l), ("dve", c_o2t)]
                                if bkey in bank_busy:
                                    wts.append(("dve", bank_busy[bkey]))
                            pb.on("te", lambda e, oi=oi, tg_i=tg_i, dc=dc,
                                  sl=sl, bank=bank:
                                  e.matmul(bank[:, :],
                                           wo_b[sl][:, (oi * n_dt + dc) * 128:
                                                    (oi * n_dt + dc + 1) * 128],
                                           o2T[:, dc * T + tg_i * TG:
                                               dc * T + tg_i * TG + TG],
                                           start=(dc == 0), stop=(dc == n_dt - 1)),
                                  inc=("te" if dc == n_dt - 1 else None),
                                  waits=wts)
                        c_mm = pb.cnt["te"]
                        bank_busy[bkey] = pb.on(
                            "dve", lambda e, ot=ot, tg_i=tg_i, bank=bank:
                            e.scalar_tensor_tensor(
                                strm[:, ot * T + tg_i * TG:
                                     ot * T + tg_i * TG + TG],
                                bank[:, :], bot[:, ot:ot + 1],
                                strm[:, ot * T + tg_i * TG:
                                     ot * T + tg_i * TG + TG],
                                ALU.add, ALU.add), inc="dve",
                            waits=[("te", c_mm)])
                wo_busy[sl] = pb.cnt["te"]
        pb.fence()

    def first_post_ln():
        c_scal_done[0] = emit_scal_finalize()

    emit_ffn(True, pre_waits=[("dx", c_x), ("dcn", c_const)],
             post_ln_cb=first_post_ln, ln_stat_waits=[("dve", c_p3)])
    emit_attn()
    emit_ffn(False)

    c_out = dma("sy", "dou", yT[:, :, :], t3(strm[:, :], n_dt),
                waits=[("dve", pb.cnt["dve"])])
    pb.wait("sy", "dou", c_out)

    sems = {s: st.enter_context(nc.semaphore("s_" + s)) for s in pb.cnt}

    with nc.Block() as block:
        @block.tensor
        def _(e):
            pb.replay("te", e, sems)

        @block.scalar
        def _(e):
            pb.replay("act", e, sems)

        @block.vector
        def _(e):
            pb.replay("dve", e, sems)

        @block.gpsimd
        def _(e):
            pb.replay("gp", e, sems)

        @block.sync
        def _(e):
            pb.replay("sy", e, sems)

    st.close()
    return nc



def stage_inputs(C, inputs):
    import ml_dtypes
    d, t, b, nh, dh, hid = C["d"], C["t"], C["b"], C["nh"], C["dh"], C["hid"]
    T, n_dt, n_tt, nh_c = C["T"], C["n_dt"], C["n_tt"], C["nh_c"]
    n_hc, hidp, QD, n_qd, hw = C["n_hc"], C["hidp"], C["QD"], C["n_qd"], C["hw"]

    x = np.ascontiguousarray(np.asarray(inputs["L0_attn_x"], dtype=np.float32))
    Wq = np.asarray(inputs["L0_attn_Wq"], np.float32)
    Wk = np.asarray(inputs["L0_attn_Wk"], np.float32)
    Wv = np.asarray(inputs["L0_attn_Wv"], np.float32)
    Wo = np.asarray(inputs["L0_attn_Wo"], np.float32)
    bq = np.asarray(inputs["L0_attn_bq"], np.float32)
    bk = np.asarray(inputs["L0_attn_bk"], np.float32)
    bv = np.asarray(inputs["L0_attn_bv"], np.float32)
    bo = np.asarray(inputs["L0_attn_bo"], np.float32)
    w1 = np.asarray(inputs["L0_ffn_w1"], np.float32)
    w3 = np.asarray(inputs["L0_ffn_w3"], np.float32)
    w2 = np.asarray(inputs["L0_ffn_w2"], np.float32)

    def tile_w13(w):
        wp = np.zeros((hidp, d), np.float32)
        wp[:hid] = w
        a = wp.reshape(n_hc, 128, n_dt, 128)
        return np.ascontiguousarray(
            a.transpose(3, 0, 2, 1).reshape(128, n_hc, n_dt * 128))

    w1t, w3t = tile_w13(w1), tile_w13(w3)
    w2p = np.zeros((d, hidp), np.float32)
    w2p[:, :hid] = w2
    w2q = np.ascontiguousarray(
        w2p.reshape(n_qd, QD, n_hc, 128).transpose(3, 0, 2, 1))
    wqt = np.ascontiguousarray(
        Wq.reshape(n_dt, 128, n_dt, 128).transpose(1, 2, 0, 3)
        .reshape(128, n_dt, n_dt * 128))
    wot = np.ascontiguousarray(
        Wo.reshape(n_dt, 128, n_dt, 128).transpose(1, 2, 0, 3)
        .reshape(128, n_dt, n_dt * 128))
    bqt = np.ascontiguousarray(bq.reshape(n_dt, 128).T)
    bot = np.ascontiguousarray(bo.reshape(n_dt, 128).T)

    Ns = C["Ns"]

    def slc(w, c):
        fl = w.reshape(-1)
        seg = fl.size // 8
        return np.ascontiguousarray(fl[c * seg:(c + 1) * seg].reshape(128, Ns))

    trib = np.triu(np.full((128, 128), NEG, np.float32), k=1).astype(
        ml_dtypes.bfloat16)
    iden = np.tile(np.eye(128, dtype=np.float32), (1, 4)).astype(
        ml_dtypes.bfloat16)
    ones_d = np.ones((128, 1), ml_dtypes.bfloat16)
    ones_f = np.ones((128, 1), np.float32)
    ones_r = np.ones((1, 128), np.float32)
    ones_rb = np.ones((1, 128), ml_dtypes.bfloat16)

    in_maps = []
    for c in range(C["n_cores"]):
        bb, h = c // 2, c % 2
        m = {}
        xt = x[bb, h * T:(h + 1) * T, :].T
        m["xT"] = np.ascontiguousarray(
            xt.reshape(n_dt, 128, T).transpose(1, 0, 2))
        m["w1t"], m["w3t"], m["w2q"] = w1t, w3t, w2q
        m["wqt"], m["wot"] = wqt, wot
        cols = slice(h * hw, (h + 1) * hw)
        m["wkt"] = np.ascontiguousarray(
            Wk[:, cols].reshape(n_dt, 128, nh_c, 128)
            .transpose(1, 2, 0, 3).reshape(128, nh_c, n_dt * 128))
        m["wvt"] = np.ascontiguousarray(
            Wv[:, cols].reshape(n_dt, 128, hw).transpose(1, 0, 2))
        m["bqt"], m["bot"] = bqt, bot
        m["bkt"] = np.ascontiguousarray(bk[cols].reshape(nh_c, 128).T)
        m["bvr"] = np.ascontiguousarray(bv[cols].reshape(1, hw))
        m["w1s"], m["w3s"], m["w2s"] = slc(w1, c), slc(w3, c), slc(w2, c)
        m["trib"], m["iden"] = trib, iden
        m["ones_d"], m["ones_f"] = ones_d, ones_f
        m["ones_r"], m["ones_rb"] = ones_r, ones_rb
        m["halfc"] = np.repeat(np.array([[-0.5, 0.5]], np.float32), 128, 0)
        CH = C["CH"]
        starts = {}
        off = 0
        for (q0, q1) in CH:
            starts[q0 * 128] = (off, (q1 - q0) * 128)
            off += 2 * (q1 - q0) * 128
        gi = np.zeros((128, 2 * n_tt), np.int32)
        for j in range(2):
            for tt in range(n_tt):
                r = h * T + tt * 128 + np.arange(128)
                base = np.zeros_like(r)
                loc = np.zeros_like(r)
                ln_ = np.zeros_like(r)
                for s0, (o_, l_) in starts.items():
                    mseg = (r >= s0) & (r < s0 + l_)
                    base[mseg] = o_
                    loc[mseg] = r[mseg] - s0
                    ln_[mseg] = l_
                gi[:, j * n_tt + tt] = base + j * ln_ + loc
        m["gidx"] = gi
        in_maps.append(m)
    return in_maps


def assemble_output(C, results):
    out = np.empty((C["b"], C["t"], C["d"]), np.float32)
    T, n_dt = C["T"], C["n_dt"]
    for c in range(C["n_cores"]):
        bb, h = c // 2, c % 2
        y = np.asarray(results[c]["yT"]).reshape(128, n_dt, T)
        out[bb, h * T:(h + 1) * T, :] = (
            y.transpose(1, 0, 2).reshape(C["d"], T).T)
    return out


def kernel(**inputs):
    C = CFG
    in_maps = stage_inputs(C, inputs)
    nc = build_nc(C)
    res = run_bass_kernel_spmd(nc, in_maps, core_ids=list(range(C["n_cores"])))
    return assemble_output(C, res.results)


# version 16
# speedup vs baseline: 1.0252x; 1.0049x over previous
import math
from contextlib import ExitStack

import numpy as np

import concourse.bass as bass
import concourse.mybir as mybir
from concourse.bass_utils import run_bass_kernel_spmd
from concourse.bass_core_rust import IndirectOffsetOnAxis

F32 = mybir.dt.float32
BF16 = mybir.dt.bfloat16
I32 = mybir.dt.int32
AF = mybir.ActivationFunctionType
ALU = mybir.AluOpType
AX = mybir.AxisListType

NEG = -1.0e9


def make_cfg(d=2048, t=2048, b=4, nh=16, hid=5461):
    dh = 128
    assert d == nh * dh and t == nh * 128
    C = dict(d=d, t=t, b=b, nh=nh, dh=dh, hid=hid)
    C["n_cores"] = 2 * b
    C["T"] = t // 2
    C["n_dt"] = d // 128
    C["n_tt"] = C["T"] // 128
    C["nh_c"] = nh // 2
    C["n_hc"] = (hid + 127) // 128
    C["hidp"] = C["n_hc"] * 128
    C["rpb"] = 128 // nh
    C["n_qt"] = t // 128
    C["TG"] = min(512, C["T"])
    C["n_tg"] = C["T"] // C["TG"]
    C["QD"] = min(512, d)
    C["n_qd"] = d // C["QD"]
    C["KQ"] = min(256, C["T"])
    C["n_kq"] = C["T"] // C["KQ"]
    C["WCH"] = min(512, C["n_dt"] * 128)
    C["SW"] = min(t, 2048)
    C["scale"] = 1.0 / math.sqrt(dh)
    C["Nw"] = hid * d
    assert (C["Nw"] // 8) % 128 == 0
    C["Ns"] = C["Nw"] // 8 // 128
    C["SCH"] = min(2 * C["WCH"], C["Ns"])
    assert C["QD"] <= C["WCH"]
    C["hw"] = C["nh_c"] * dh
    nq = C["n_qt"]
    if nq >= 16:
        C["CH"] = [(0, 8), (8, 12), (12, 15), (15, 16)]
    elif nq >= 2:
        C["CH"] = [(0, nq // 2), (nq // 2, nq)]
    else:
        C["CH"] = [(0, nq)]
    return C


CFG = make_cfg()


class PB:

    ENGS = ("te", "act", "dve", "gp", "sy")

    def __init__(self):
        self.q = {e: [] for e in self.ENGS}
        self.cnt = {}
        for s in ("te", "act", "dve", "gpc", "cc"):
            self.cnt[s] = 0
        self.hw = {}
        self.pending = {e: None for e in self.ENGS}

    def sem(self, name):
        if name not in self.cnt:
            self.cnt[name] = 0
        return name

    def wait(self, eng, sem, val):
        if val is None or val <= 0:
            return
        if self.hw.get((eng, sem), 0) >= val:
            return
        self.hw[(eng, sem)] = val
        self.q[eng].append(("w", sem, val))

    def fence(self):
        snap = dict(self.cnt)
        for e in self.ENGS:
            self.pending[e] = snap

    def on(self, eng, fn, inc=None, by=1, waits=()):
        if self.pending[eng] is not None:
            snap, self.pending[eng] = self.pending[eng], None
            for s, v in snap.items():
                self.wait(eng, s, v)
        for s, v in waits:
            self.wait(eng, s, v)
        post = None
        if inc is not None:
            self.sem(inc)
            self.cnt[inc] += by
            post = self.cnt[inc]
        self.q[eng].append(("i", fn, inc, by))
        return post

    def replay(self, eng_name, e, sems):
        for item in self.q[eng_name]:
            if item[0] == "w":
                e.wait_ge(sems[item[1]], item[2])
            else:
                _, fn, inc, by = item
                ins = fn(e)
                if inc is not None:
                    ins.then_inc(sems[inc], by)


class Arena:

    def __init__(self, tensor, cols):
        self.t = tensor
        self.cols = cols
        self.used = []

    def region(self, name, off, n):
        assert off + n <= self.cols, (name, off, n, self.cols)
        self.used.append((name, off, n))
        return self.t[:, off:off + n]


def build_nc(C):
    nc = bass.Bass("TRN2", target_bir_lowering=False)
    d, t, T = C["d"], C["t"], C["T"]
    nh, dh, nh_c, hw = C["nh"], C["dh"], C["nh_c"], C["hw"]
    n_dt, n_tt, n_hc = C["n_dt"], C["n_tt"], C["n_hc"]
    TG, n_tg, QD, n_qd = C["TG"], C["n_tg"], C["QD"], C["n_qd"]
    KQ, n_kq = C["KQ"], C["n_kq"]
    n_qt, rpb, SW = C["n_qt"], C["rpb"], C["SW"]
    WCH, SCH = C["WCH"], C["SCH"]
    n_wch = (n_dt * 128) // WCH
    dpc = WCH // 128

    ap = {}

    def din(name, shape, dt=F32):
        ap[name] = nc.dram_tensor(name, list(shape), dt, kind="ExternalInput").ap()

    din("xT", (128, n_dt, T))
    din("w1t", (128, n_hc, n_dt * 128))
    din("w3t", (128, n_hc, n_dt * 128))
    din("w2q", (128, n_qd, n_hc, QD))
    din("wqt", (128, n_dt, n_dt * 128))
    din("wkt", (128, nh_c, n_dt * 128))
    din("wvt", (128, n_dt, hw))
    din("wot", (128, n_dt, n_dt * 128))
    din("bqt", (128, n_dt))
    din("bkt", (128, nh_c))
    din("bvr", (1, hw))
    din("bot", (128, n_dt))
    din("w1s", (128, C["Ns"]))
    din("w3s", (128, C["Ns"]))
    din("w2s", (128, C["Ns"]))
    din("trib", (128, 128), BF16)
    din("iden", (128, 512), BF16)
    din("ones_d", (128, 1), BF16)
    din("ones_f", (128, 1))
    din("ones_r", (1, 128))
    din("ones_rb", (1, 128), BF16)
    din("gidx", (128, 2 * n_tt), I32)
    din("halfc", (128, 2))
    yT = nc.dram_tensor("yT", [128, n_dt, T], F32, kind="ExternalOutput").ap()

    agx_in = nc.dram_tensor("agx_in", [128, n_dt * T], BF16)
    agx_out = nc.dram_tensor("agx_out", [2 * 128, n_dt * T], BF16)
    aga_in = nc.dram_tensor("aga_in", [t, hw], BF16)
    aga_out = nc.dram_tensor("aga_out", [2 * t, hw], BF16)
    ar_in = nc.dram_tensor("ar_in", [3, 1], F32)
    ar_out = nc.dram_tensor("ar_out", [3, 1], F32)
    t1c = nc.dram_tensor("t1c", [128, n_hc, n_dt * 128], BF16)
    t3c = nc.dram_tensor("t3c", [128, n_hc, n_dt * 128], BF16)
    t2c = nc.dram_tensor("t2c", [128, n_qd, n_hc, QD], BF16)
    wqc = nc.dram_tensor("wqc", [128, n_dt, n_dt * 128], BF16)
    wkc = nc.dram_tensor("wkc", [128, nh_c, n_dt * 128], BF16)
    wvc = nc.dram_tensor("wvc", [128, n_dt, hw], BF16)
    woc = nc.dram_tensor("woc", [128, n_dt, n_dt * 128], BF16)

    pb = PB()
    st = ExitStack()

    def sb(name, shape, dt=F32):
        return st.enter_context(nc.sbuf_tensor(name, list(shape), dt))

    strm = sb("strm", (128, n_dt * T))
    AR_COLS = 65536 if d >= 2048 else 32768
    arena_t = sb("arena", (128, AR_COLS), BF16)
    A = Arena(arena_t, AR_COLS)
    SCR = max(WCH, TG)
    wf32 = sb("wf32", (128, 4 * WCH + 3 * SCR))
    trib = sb("trib_s", (128, 128), BF16)
    onesd = sb("onesd", (128, 1), BF16)
    onesf = sb("onesf", (128, 1))
    onesr = sb("onesr", (1, 128))
    onesrb = sb("onesrb", (1, 128), BF16)
    scal = sb("scal", (128, 8))
    bqt = sb("bqt_s", (128, n_dt))
    bkt = sb("bkt_s", (128, nh_c))
    bot = sb("bot_s", (128, n_dt))
    gidx = sb("gidx_s", (128, 2 * n_tt), I32)
    part = sb("part", (128, 8))
    srow = sb("srow", (1, 16))
    mx = sb("mx", (128, 12))
    halfc = sb("halfc_s", (128, 2))

    CON = 0
    iden4 = A.region("iden4", CON, 512)
    idenT = iden4[:, 0:128]
    bvr = A.region("bvr", CON + 512, hw)[0:1, :]
    outsb_base = CON + 512 + hw
    outsb = A.region("outsb", outsb_base, 512)
    arowb = arena_t[0:1, outsb_base + 128:outsb_base + 256]
    mcolb = arena_t[:, outsb_base + 384:outsb_base + 386]
    BASE = outsb_base + 512

    G_OFF = BASE
    G_COLS = n_hc * T
    TB_OFF = G_OFF + G_COLS
    X2_OFF = TB_OFF + 5 * WCH + TG
    X2_COLS = n_dt * T
    g_r = A.region("g", G_OFF, G_COLS)
    tring = [A.region("tr%d" % i, TB_OFF + i * WCH, WCH) for i in range(4)]
    abf_r = A.region("abf", TB_OFF + 4 * WCH, WCH)
    sA_r = A.region("sA", TB_OFF + 5 * WCH, TG)
    x2T = A.region("x2T", X2_OFF, X2_COLS)
    assert X2_OFF + X2_COLS <= AR_COLS, (X2_OFF, X2_COLS, AR_COLS)
    xbf = arena_t[:, G_OFF:G_OFF + n_dt * T]
    xsq = arena_t[:, G_OFF + n_dt * T:G_OFF + 2 * n_dt * T]
    assert 2 * n_dt * T <= G_COLS + 4 * WCH + TG

    QM_OFF = G_OFF
    K_OFF = QM_OFF + n_dt * T
    V_OFF = K_OFF + nh_c * t
    XQ_OFF = V_OFF + (t // 128) * hw
    P_OFF = XQ_OFF + 2 * n_dt * KQ
    PT_OFF = P_OFF + SW
    WB_OFF = PT_OFF + SW
    qmatT = arena_t[:, QM_OFF:QM_OFF + n_dt * T]
    K_r = arena_t[:, K_OFF:K_OFF + nh_c * t]
    V_r = arena_t[:, V_OFF:V_OFF + (t // 128) * hw]
    x2qb = [arena_t[:, XQ_OFF + i * n_dt * KQ:
                    XQ_OFF + (i + 1) * n_dt * KQ] for i in range(2)]
    P_r = arena_t[:, P_OFF:P_OFF + SW]
    Pt_r = arena_t[:, PT_OFF:PT_OFF + SW]
    wq_b = [arena_t[:, K_OFF + i * 2 * n_dt * 128:
                    K_OFF + (i + 1) * 2 * n_dt * 128] for i in range(2)]
    wk_b = [arena_t[:, P_OFF + i * (n_dt * 128):
                    P_OFF + (i + 1) * (n_dt * 128)] for i in range(2)]
    wv_b = [arena_t[:, WB_OFF + i * hw: WB_OFF + (i + 1) * hw] for i in range(2)]
    assert WB_OFF + 2 * hw <= AR_COLS, (WB_OFF + 2 * hw, AR_COLS)
    assert P_OFF + 2 * n_dt * 128 <= WB_OFF + 2 * hw
    o2tok = arena_t[:, K_OFF:K_OFF + n_tt * 2 * hw]
    o2T = arena_t[:, V_OFF:V_OFF + n_dt * T]
    assert V_OFF + n_dt * T <= AR_COLS
    wo_b = [arena_t[:, QM_OFF + i * 2 * n_dt * 128:
                    QM_OFF + (i + 1) * 2 * n_dt * 128] for i in range(2)]

    mb_r = wf32[:, 4 * WCH:4 * WCH + TG]
    rb_r = wf32[:, 4 * WCH + SCR:4 * WCH + SCR + TG]
    ln_tmp = wf32[:, 4 * WCH + 2 * SCR:4 * WCH + 2 * SCR + TG]
    ta_r = wf32[:, 4 * WCH:4 * WCH + WCH]
    tb_r = wf32[:, 4 * WCH + SCR:4 * WCH + SCR + WCH]
    m_row = rb_r[0:1, 0:TG]
    r_row = ln_tmp[0:1, 0:TG]

    def dma(eng, semname, out, in_, waits=()):
        pb.sem(semname)
        return pb.on(eng, lambda e, o=out, i=in_: e.dma_start(out=o, in_=i),
                     inc=semname, by=16, waits=waits)

    def t3(sb_ap, cdim):
        return sb_ap.rearrange("p (c f) -> p c f", c=cdim)

    c_x = dma("sy", "dx", t3(strm[:, :], n_dt), ap["xT"][:, :, :])
    dma("sy", "dcn", trib[:, :], ap["trib"][:, :])
    dma("sy", "dcn", iden4, ap["iden"][:, :])
    dma("sy", "dcn", onesd[:, :], ap["ones_d"][:, :])
    dma("sy", "dcn", onesf[:, :], ap["ones_f"][:, :])
    dma("sy", "dcn", onesr[:, :], ap["ones_r"][:, :])
    dma("sy", "dcn", onesrb[:, :], ap["ones_rb"][:, :])
    dma("sy", "dcn", bqt[:, :], ap["bqt"][:, :])
    dma("sy", "dcn", bkt[:, :], ap["bkt"][:, :])
    dma("sy", "dcn", bot[:, :], ap["bot"][:, :])
    dma("sy", "dcn", gidx[:, :], ap["gidx"][:, :])
    c_const = dma("sy", "dcn", halfc[:, :], ap["halfc"][:, :])
    c_bv = dma("gp", "dbv", bvr, ap["bvr"][:, :])

    n_sch = (C["Ns"] + SCH - 1) // SCH
    slot_busy = [0, 0]
    for j, nm in enumerate(("w1s", "w3s", "w2s")):
        for ch in range(n_sch):
            w = min(SCH, C["Ns"] - ch * SCH)
            sl = (j * n_sch + ch) % 2
            buf = wf32[:, sl * 2 * WCH: sl * 2 * WCH + w]
            c_l = dma("sy" if sl == 0 else "act", "dsl%d" % sl, buf,
                      ap[nm][:, ch * SCH: ch * SCH + w],
                      waits=[("dve", slot_busy[sl])])
            c_r = pb.on("dve", lambda e, buf=buf: e.reduce_sum(
                part[:, 6:7], buf, axis=AX.X, apply_absolute_value=True),
                inc="dve", waits=[("dsl%d" % sl, c_l)])
            slot_busy[sl] = c_r
            pb.wait("dve", "dve", c_r)
            if ch == 0:
                pb.on("dve", lambda e, j=j: e.tensor_copy(
                    part[:, j:j + 1], part[:, 6:7]), inc="dve")
            else:
                pb.on("dve", lambda e, j=j: e.tensor_tensor(
                    part[:, j:j + 1], part[:, j:j + 1], part[:, 6:7],
                    ALU.add), inc="dve")
            pb.wait("dve", "dve", pb.cnt["dve"])
    c_part = pb.cnt["dve"]

    with nc.psum_tensor([3, 512], F32) as ps_s3:
        c_mm3 = pb.on("te", lambda e: e.matmul(ps_s3[:, 0:1], part[:, 0:3],
                                               onesf[:, 0:1], start=True, stop=True),
                      inc="te", waits=[("dve", c_part), ("dcn", c_const)])
        pb.on("dve", lambda e: e.tensor_copy(part[0:3, 4:5], ps_s3[:, 0:1]),
              inc="dve", waits=[("te", c_mm3)])
    c_p3 = pb.cnt["dve"]
    c_ari = dma("sy", "dar", ar_in[:, :], part[0:3, 4:5],
                waits=[("dve", c_p3)])
    pb.on("gp", lambda e: e.collective_compute(
        "AllReduce", ALU.add, replica_groups=[list(range(C["n_cores"]))],
        ins=[ar_in.ap().opt()], outs=[ar_out.ap().opt()]), inc="cc",
        waits=[("dar", c_ari)])
    c_ar = pb.cnt["cc"]
    c_sr = dma("sy", "dsr", srow[0:1, 0:3],
               ar_out.ap().rearrange("a b -> b a"), waits=[("cc", c_ar)])

    dma("gp", "dcc", wqc.ap()[:, :, :], ap["wqt"][:, :, :],
        waits=[("dsl0", pb.cnt["dsl0"]), ("dsl1", pb.cnt["dsl1"])])
    dma("gp", "dcc", wkc.ap()[:, :, :], ap["wkt"][:, :, :])
    dma("gp", "dcc", wvc.ap()[:, :, :], ap["wvt"][:, :, :])
    c_cc = dma("gp", "dcc", woc.ap()[:, :, :], ap["wot"][:, :, :])

    def emit_scal_finalize():
        Nw = float(C["Nw"])
        pb.on("dve", lambda e: e.reciprocal(srow[0:1, 4:7], srow[0:1, 0:3]),
              inc="dve", waits=[("dsr", c_sr)])
        pb.wait("dve", "dve", pb.cnt["dve"])
        pb.on("dve", lambda e: e.tensor_scalar(srow[0:1, 8:11], srow[0:1, 4:7],
                                               Nw, None, ALU.mult), inc="dve")
        pb.on("dve", lambda e: e.tensor_scalar(srow[0:1, 11:12], srow[0:1, 0:1],
                                               0.5 / Nw, None, ALU.mult), inc="dve")
        pb.on("dve", lambda e: e.tensor_scalar(srow[0:1, 12:13], srow[0:1, 1:2],
                                               1.0 / Nw, None, ALU.mult), inc="dve")
        pb.on("dve", lambda e: e.tensor_scalar(srow[0:1, 13:14], srow[0:1, 2:3],
                                               0.25 / Nw, None, ALU.mult), inc="dve")
        c_fin = pb.cnt["dve"]
        with nc.psum_tensor([128, 512], F32) as ps_b6:
            c_b6 = pb.on("te", lambda e: e.matmul(
                ps_b6[:, 0:6], onesr[:, 0:128], srow[0:1, 8:14],
                start=True, stop=True), inc="te",
                waits=[("dve", c_fin), ("dcn", c_const)])
            pb.on("dve", lambda e: e.tensor_copy(scal[:, 0:6], ps_b6[:, 0:6]),
                  inc="dve", waits=[("te", c_b6)])
        return pb.cnt["dve"]

    INV1, INV3, INV2, HS1, HS3, FS2 = range(6)

    def emit_ln(extra_waits=(), stat_waits=()):
        c0 = pb.on("act", lambda e: e.activation(xbf[:, :], strm[:, :], AF.Copy),
                   inc="act", waits=list(extra_waits))
        c1 = pb.on("act", lambda e: e.activation(xsq[:, :], strm[:, :], AF.Square),
                   inc="act", waits=list(extra_waits))
        for tg_i in range(n_tg):
            t0 = tg_i * TG
            with nc.psum_tensor([1, max(TG, 512)], F32) as ps_m, \
                 nc.psum_tensor([1, max(TG, 512)], F32) as ps_q, \
                 nc.psum_tensor([128, max(TG, 512)], F32) as ps_mb, \
                 nc.psum_tensor([128, max(TG, 512)], F32) as ps_rb, \
                 nc.psum_tensor([128, max(TG, 512)], F32) as ps_tmp, \
                 nc.psum_tensor([1, max(TG, 512)], F32) as ln_s:
                for dt_i in range(n_dt):
                    pb.on("te", lambda e, i=dt_i, t0=t0: e.matmul(
                        ps_m[:, 0:TG], onesd[:, 0:1], xbf[:, i * T + t0:i * T + t0 + TG],
                        start=(i == 0), stop=(i == n_dt - 1)),
                        inc=("te" if dt_i == n_dt - 1 else None),
                        waits=(([("act", c0)] + list(stat_waits))
                               if dt_i == 0 else []))
                cm = pb.cnt["te"]
                for dt_i in range(n_dt):
                    pb.on("te", lambda e, i=dt_i, t0=t0: e.matmul(
                        ps_q[:, 0:TG], onesd[:, 0:1], xsq[:, i * T + t0:i * T + t0 + TG],
                        start=(i == 0), stop=(i == n_dt - 1)),
                        inc=("te" if dt_i == n_dt - 1 else None),
                        waits=([("act", c1)] if dt_i == 0 else []))
                cq = pb.cnt["te"]
                pb.on("dve", lambda e: e.tensor_scalar(m_row, ps_m[:, 0:TG], 1.0 / d,
                                                       None, ALU.mult), inc="dve",
                      waits=[("te", cm), ("gpc", pb.cnt["gpc"])])
                pb.wait("dve", "dve", pb.cnt["dve"])
                pb.on("dve", lambda e: e.tensor_tensor(r_row, m_row, m_row,
                                                       ALU.mult), inc="dve")
                pb.wait("dve", "dve", pb.cnt["dve"])
                pb.on("dve", lambda e: e.tensor_scalar(
                    r_row, r_row, -1.0, None, ALU.mult), inc="dve")
                pb.wait("dve", "dve", pb.cnt["dve"])
                pb.on("dve", lambda e: e.scalar_tensor_tensor(
                    r_row, ps_q[:, 0:TG], 1.0 / d, r_row, ALU.mult, ALU.add),
                    inc="dve", waits=[("te", cq)])
                pb.wait("dve", "dve", pb.cnt["dve"])
                cv = pb.on("dve", lambda e: e.tensor_scalar(r_row, r_row, 1e-5,
                                                            None, ALU.add),
                           inc="dve")
                cs = pb.on("act", lambda e: e.activation(
                    ln_s[0:1, 0:TG], r_row, AF.Sqrt), inc="act",
                    waits=[("dve", cv)])
                pb.on("dve", lambda e: e.reciprocal(mb_r[0:1, 0:TG],
                                                    ln_s[0:1, 0:TG]),
                      inc="dve", waits=[("act", cs), ("gpc", pb.cnt["gpc"])])
                pb.wait("dve", "dve", pb.cnt["dve"])
                pb.on("dve", lambda e: e.tensor_tensor(
                    ln_s[0:1, 0:TG], mb_r[0:1, 0:TG], mb_r[0:1, 0:TG], ALU.mult),
                    inc="dve")
                pb.wait("dve", "dve", pb.cnt["dve"])
                pb.on("dve", lambda e: e.tensor_tensor(
                    ln_s[0:1, 0:TG], ln_s[0:1, 0:TG], r_row, ALU.mult),
                    inc="dve")
                pb.wait("dve", "dve", pb.cnt["dve"])
                pb.on("dve", lambda e: e.tensor_scalar(
                    ln_s[0:1, 0:TG], ln_s[0:1, 0:TG], -0.5, 1.5, ALU.mult,
                    ALU.add), inc="dve")
                pb.wait("dve", "dve", pb.cnt["dve"])
                cr = pb.on("dve", lambda e: e.tensor_tensor(
                    r_row, mb_r[0:1, 0:TG], ln_s[0:1, 0:TG], ALU.mult),
                    inc="dve")
                c_rb1 = pb.on("te", lambda e: e.matmul(
                    ps_mb[:, 0:TG], onesr[:, 0:128], m_row, start=True, stop=True),
                    inc="te", waits=[("dve", cr)])
                c_rb2 = pb.on("te", lambda e: e.matmul(
                    ps_rb[:, 0:TG], onesr[:, 0:128], r_row, start=True, stop=True),
                    inc="te")
                pb.on("dve", lambda e: e.tensor_copy(mb_r, ps_mb[:, 0:TG]), inc="dve",
                      waits=[("te", c_rb1), ("gpc", pb.cnt["gpc"])])
                pb.on("dve", lambda e: e.tensor_copy(rb_r, ps_rb[:, 0:TG]), inc="dve",
                      waits=[("te", c_rb2)])
                c_bc = pb.cnt["dve"]
                c_bc_gp = c_bc
                for dt_i in range(n_dt):
                    if dt_i % 2 == 0:
                        c_t = pb.on("dve", lambda e, i=dt_i, t0=t0:
                                    e.tensor_tensor(
                                        ps_tmp[:, 0:TG],
                                        strm[:, i * T + t0:i * T + t0 + TG],
                                        mb_r, ALU.subtract), inc="dve",
                                    waits=[("dve", c_bc)])
                        pb.wait("dve", "dve", c_t)
                        pb.on("dve", lambda e, i=dt_i, t0=t0: e.tensor_tensor(
                            x2T[:, i * T + t0:i * T + t0 + TG], ps_tmp[:, 0:TG],
                            rb_r, ALU.mult), inc="dve")
                        pb.wait("dve", "dve", pb.cnt["dve"])
                    else:
                        c_t = pb.on("gp", lambda e, i=dt_i, t0=t0:
                                    e.tensor_tensor(
                                        ln_tmp[:, 0:TG],
                                        strm[:, i * T + t0:i * T + t0 + TG],
                                        mb_r, ALU.subtract), inc="gpc",
                                    waits=[("dve", c_bc)])
                        pb.wait("gp", "gpc", c_t)
                        pb.on("gp", lambda e, i=dt_i, t0=t0: e.tensor_tensor(
                            x2T[:, i * T + t0:i * T + t0 + TG], ln_tmp[:, 0:TG],
                            rb_r, ALU.mult), inc="gpc")
                        pb.wait("gp", "gpc", pb.cnt["gpc"])
        return pb.cnt["dve"]

    c_scal_done = [0]
    abf_busy = [0]
    tdve_busy = [0]

    def tern(src, dst, width, inv_col, waits):
        ab = abf_r[:, 0:width]
        ca = pb.on("act", lambda e, ab=ab: e.activation(
            ab, src, AF.Sign, bias=halfc[:, 0:1],
            scale=scal[:, inv_col:inv_col + 1]), inc="act",
            waits=list(waits) + [("dve", c_scal_done[0]),
                                 ("dve", abf_busy[0])])
        cb = pb.on("act", lambda e: e.activation(
            dst, src, AF.Sign, bias=halfc[:, 1:2],
            scale=scal[:, inv_col:inv_col + 1]), inc="act")
        cw = pb.on("dve", lambda e, ab=ab: e.tensor_tensor(dst, dst, ab,
                                                            ALU.add), inc="dve",
                   waits=[("act", cb)])
        abf_busy[0] = cw
        return ("dve", cw), ("act", cb)

    def tern_dve(src, dst, width, inv_col, waits):
        ta = ta_r[:, 0:width]
        tb = tb_r[:, 0:width]
        pb.on("dve", lambda e, ta=ta: e.tensor_scalar(
            ta, src, scal[:, inv_col:inv_col + 1], 0.5, ALU.mult, ALU.is_ge),
            inc="dve", waits=list(waits) + [("dve", c_scal_done[0]),
                                            ("dve", tdve_busy[0])])
        pb.on("dve", lambda e, tb=tb: e.tensor_scalar(
            tb, src, scal[:, inv_col:inv_col + 1], -0.5, ALU.mult, ALU.is_le),
            inc="dve")
        pb.wait("dve", "dve", pb.cnt["dve"])
        cw = pb.on("dve", lambda e, ta=ta, tb=tb: e.scalar_tensor_tensor(
            dst, ta, 1.0, tb, ALU.mult, ALU.subtract), inc="dve")
        tdve_busy[0] = cw
        return ("dve", cw), ("dve", cw)

    ffn_idx = [0]

    def emit_ffn(first, pre_waits=(), post_ln_cb=None, ln_stat_waits=()):
        c_ln = emit_ln(extra_waits=pre_waits, stat_waits=ln_stat_waits)
        if post_ln_cb is not None:
            post_ln_cb()
        pb.fence()
        with ExitStack() as hs:
            psb = [hs.enter_context(nc.psum_tensor("psb%d_%d" % (ffn_idx[0], i),
                                                    [128, TG], F32))
                   for i in range(8)]

            def hbank(mi, par, tg_i):
                return psb[par * 4 + mi * 2 + tg_i]

            w_busy = [None] * 4
            t_busy = [[], [], [], []]
            ev_busy = {}
            sA_busy = [0]
            ci = 0
            c_g = [0]

            def emit_evac(hc):
                par = hc % 2
                c_mm = ev_busy.get(("mm", hc), pb.cnt["te"])
                for tg_i in range(n_tg):
                    gdst = g_r[:, hc * T + tg_i * TG: hc * T + tg_i * TG + TG]
                    c_a = pb.on("act", lambda e, par=par, tg_i=tg_i: e.activation(
                        sA_r[:, 0:TG], hbank(0, par, tg_i)[:, :], AF.Silu,
                        scale=scal[:, HS1:HS1 + 1]), inc="act",
                        waits=[("te", c_mm), ("dve", sA_busy[0])])
                    c_g[0] = pb.on("dve", lambda e, par=par, tg_i=tg_i,
                                   gdst=gdst: e.scalar_tensor_tensor(
                                       gdst, hbank(1, par, tg_i)[:, :],
                                       scal[:, HS3:HS3 + 1], sA_r[:, 0:TG],
                                       ALU.mult, ALU.mult), inc="dve",
                        waits=[("act", c_a)])
                    sA_busy[0] = c_g[0]
                ev_busy[(par, "act")] = pb.cnt["act"]
                ev_busy[(par, "dve")] = pb.cnt["dve"]

            srcs = ((ap["w1t"], t1c, INV1, "act"), (ap["w3t"], t3c, INV3, "dve"))
            for hc in range(n_hc):
                par = hc % 2
                for mi, (wsrc, cache, inv, path) in enumerate(srcs):
                    for ch in range(n_wch):
                        ws = ci % 4
                        sl = ci % 4
                        ci += 1
                        if first:
                            wbuf = wf32[:, ws * WCH: ws * WCH + WCH]
                            ldq = "act" if path == "act" else "sy"
                            c_l = dma(ldq, "dwA%d" % ws, wbuf,
                                      wsrc[:, hc, ch * WCH:(ch + 1) * WCH],
                                      waits=([w_busy[ws]] if w_busy[ws] else []))
                            for s_, v_ in t_busy[sl]:
                                pb.wait("act" if path == "act" else "dve", s_, v_)
                            if path == "act":
                                c_t, frees = tern(wbuf, tring[sl], WCH, inv,
                                                  [("dwA%d" % ws, c_l)])
                            else:
                                c_t, frees = tern_dve(wbuf, tring[sl], WCH, inv,
                                                      [("dwA%d" % ws, c_l)])
                            w_busy[ws] = frees
                            c_st = dma("gp", "dtc%d" % sl,
                                       cache.ap()[:, hc, ch * WCH:(ch + 1) * WCH],
                                       tring[sl],
                                       waits=[c_t])
                            tb_new = [("dtc%d" % sl, c_st)]
                            mm_wait = c_t
                        else:
                            ldq = "sy" if sl % 2 == 0 else "act"
                            for s_, v_ in t_busy[sl]:
                                pb.wait(ldq, s_, v_)
                            c_l = dma(ldq, "dtA%d" % sl, tring[sl],
                                      cache.ap()[:, hc, ch * WCH:(ch + 1) * WCH])
                            tb_new = []
                            mm_wait = ("dtA%d" % sl, c_l)
                        for k in range(dpc):
                            dt_i = ch * dpc + k
                            for tg_i in range(n_tg):
                                wts = []
                                if k == 0 and tg_i == 0:
                                    wts = [mm_wait, ("dve", c_ln)]
                                    if ch == 0 and (par, "act") in ev_busy:
                                        wts.append(("act" if mi == 0 else "dve",
                                                    ev_busy[(par, "act" if mi == 0
                                                             else "dve")]))
                                last = (k == dpc - 1 and tg_i == n_tg - 1)
                                pb.on("te", lambda e, k=k, tg_i=tg_i, mi=mi,
                                      par=par, sl=sl, dt_i=dt_i:
                                      e.matmul(hbank(mi, par, tg_i)[:, :],
                                               tring[sl][:, k * 128:(k + 1) * 128],
                                               x2T[:, dt_i * T + tg_i * TG:
                                                   dt_i * T + tg_i * TG + TG],
                                               start=(dt_i == 0),
                                               stop=(dt_i == n_dt - 1)),
                                      inc=("te" if last else None), waits=wts)
                        t_busy[sl] = tb_new + [("te", pb.cnt["te"])]
                ev_busy[("mm", hc)] = pb.cnt["te"]
                if hc > 0:
                    emit_evac(hc - 1)
            emit_evac(n_hc - 1)
        pb.fence()
        n_dq = QD // 128
        with ExitStack() as fs:
            psf = [[fs.enter_context(nc.psum_tensor(
                        "psf%d_%d_%d" % (ffn_idx[0], i, j), [128, TG], F32))
                    for j in range(n_tg)] for i in range(n_dq)]
            w_busy = [None] * 4
            t_busy = [[], [], [], []]
            prev_ev = 0
            ci2 = 0
            for qd_i in range(n_qd):
                for hc in range(n_hc):
                    ws = ci2 % 4
                    sl = ci2 % 4
                    ci2 += 1
                    if first:
                        wbuf = wf32[:, ws * WCH: ws * WCH + QD]
                        ldq = "sy" if ws % 2 == 0 else "act"
                        c_l = dma(ldq, "dwA%d" % ws, wbuf,
                                  ap["w2q"][:, qd_i, hc, :],
                                  waits=([w_busy[ws]] if w_busy[ws] else []))
                        for s_, v_ in t_busy[sl]:
                            pb.wait("act", s_, v_)
                        c_t, frees = tern(wbuf, tring[sl][:, 0:QD], QD, INV2,
                                          [("dwA%d" % ws, c_l)])
                        w_busy[ws] = frees
                        c_st = dma("gp", "dtc%d" % sl,
                                   t2c.ap()[:, qd_i, hc, :], tring[sl][:, 0:QD],
                                   waits=[c_t])
                        tb_new = [("dtc%d" % sl, c_st)]
                        mm_wait = c_t
                    else:
                        ldq = "sy" if sl % 2 == 0 else "act"
                        for s_, v_ in t_busy[sl]:
                            pb.wait(ldq, s_, v_)
                        c_l = dma(ldq, "dtA%d" % sl, tring[sl][:, 0:QD],
                                  t2c.ap()[:, qd_i, hc, :])
                        tb_new = []
                        mm_wait = ("dtA%d" % sl, c_l)
                    for dq in range(n_dq):
                        for tg_i in range(n_tg):
                            wts = []
                            if dq == 0 and tg_i == 0:
                                wts = [mm_wait, ("dve", c_g[0])]
                                if hc == 0 and qd_i > 0:
                                    wts.append(("dve", prev_ev))
                            last = (dq == n_dq - 1 and tg_i == n_tg - 1)
                            pb.on("te", lambda e, dq=dq, tg_i=tg_i, hc=hc, sl=sl:
                                  e.matmul(psf[dq][tg_i][:, :],
                                           tring[sl][:, dq * 128:(dq + 1) * 128],
                                           g_r[:, hc * T + tg_i * TG:
                                               hc * T + tg_i * TG + TG],
                                           start=(hc == 0), stop=(hc == n_hc - 1)),
                                  inc=("te" if last else None), waits=wts)
                    t_busy[sl] = tb_new + [("te", pb.cnt["te"])]
                c_fq = pb.cnt["te"]
                for dq in range(n_dq):
                    ot = qd_i * n_dq + dq
                    for tg_i in range(n_tg):
                        pb.on("dve", lambda e, dq=dq, tg_i=tg_i, ot=ot:
                              e.scalar_tensor_tensor(
                                  strm[:, ot * T + tg_i * TG:
                                       ot * T + tg_i * TG + TG],
                                  psf[dq][tg_i][:, :], scal[:, FS2:FS2 + 1],
                                  strm[:, ot * T + tg_i * TG:
                                       ot * T + tg_i * TG + TG],
                                  ALU.mult, ALU.add), inc="dve",
                              waits=[("te", c_fq)])
                prev_ev = pb.cnt["dve"]
        pb.fence()
        ffn_idx[0] += 1

    def emit_attn():
        c_ln2 = emit_ln()
        pb.fence()
        c_agw = dma("sy", "dagw", agx_in.ap()[:, :], x2T[:, :])
        pairs = [[2 * i, 2 * i + 1] for i in range(C["n_cores"] // 2)]
        pb.on("gp", lambda e: e.collective_compute(
            "AllGather", ALU.bypass, replica_groups=pairs,
            ins=[agx_in.ap().opt()], outs=[agx_out.ap().opt()]), inc="cc",
            waits=[("dagw", c_agw)])
        c_agx = pb.cnt["cc"]

        with ExitStack() as qs:
            qb = [qs.enter_context(nc.psum_tensor("qb%d" % i, [128, TG], F32))
                  for i in range(8)]
            wq_busy = [0, 0]
            bank_busy = {}
            for og in range(0, n_dt, 2):
                sl = (og // 2) % 2
                c_l = dma("sy" if sl == 0 else "act", "dwq%d" % sl, wq_b[sl],
                          wqc.ap()[:, og:og + 2, :]
                          .rearrange("p o f -> p (o f)"),
                          waits=[("te", wq_busy[sl]), ("dcc", c_cc)])
                for oi in range(2):
                    ot = og + oi
                    for tg_i in range(n_tg):
                        bank = qb[sl * 4 + oi * n_tg + tg_i]
                        bkey = sl * 4 + oi * n_tg + tg_i
                        for dc in range(n_dt):
                            wts = []
                            if dc == 0:
                                wts = [("dwq%d" % sl, c_l), ("dve", c_ln2)]
                                if bkey in bank_busy:
                                    wts.append(("act", bank_busy[bkey]))
                            pb.on("te", lambda e, oi=oi, tg_i=tg_i, dc=dc, sl=sl,
                                  bank=bank:
                                  e.matmul(bank[:, :],
                                           wq_b[sl][:, (oi * n_dt + dc) * 128:
                                                    (oi * n_dt + dc + 1) * 128],
                                           x2T[:, dc * T + tg_i * TG:
                                               dc * T + tg_i * TG + TG],
                                           start=(dc == 0), stop=(dc == n_dt - 1)),
                                  inc=("te" if dc == n_dt - 1 else None),
                                  waits=wts)
                        c_mm = pb.cnt["te"]
                        qsv = qmatT.rearrange("p (H G r c) -> p H G r c",
                                              G=n_qt, r=rpb, c=nh)
                        nhb = TG // 128
                        dst = qsv[:, tg_i * nhb:(tg_i + 1) * nhb, :, :, ot]
                        bank_busy[bkey] = pb.on(
                            "act", lambda e, dst=dst, ot=ot, bank=bank:
                            e.activation(dst, bank[:, :], AF.Identity,
                                         bias=bqt[:, ot:ot + 1]),
                            inc="act", waits=[("te", c_mm)])
                wq_busy[sl] = pb.cnt["te"]
        c_q = pb.cnt["act"]
        pb.fence()

        with ExitStack() as ks:
            kb = [ks.enter_context(nc.psum_tensor("kb%d" % i,
                                                  [128, max(KQ, 512)], F32))
                  for i in range(4)]
            vb = [ks.enter_context(nc.psum_tensor("vb%d" % i, [128, 512], F32))
                  for i in range(4)]
            GW = min(512, hw)
            n_gw = hw // GW
            kb_busy = {}
            vb_busy = {}
            wk_busy = [0, 0]
            wv_busy = [0, 0]
            x2q_busy = [0, 0]
            un = 0
            for half in range(2):
                for kq in range(n_kq):
                    x2q = x2qb[un % 2]
                    upar = un % 2
                    un += 1
                    c_xgl = []
                    for dcl in range(n_dt):
                        c_xgl.append(("dxq%d_%d" % (dcl, upar), dma(
                            "sy" if dcl % 2 == 0 else "act",
                            "dxq%d_%d" % (dcl, upar),
                            x2q[:, dcl * KQ:(dcl + 1) * KQ],
                            agx_out.ap()[half * 128:(half + 1) * 128,
                                         dcl * T + kq * KQ:
                                         dcl * T + kq * KQ + KQ],
                            waits=([("cc", c_agx), ("te", x2q_busy[upar])]
                                   if dcl == 0 else ()))))
                    for hh in range(nh_c):
                        sl = hh % 2
                        c_wk = dma("sy", "dwk%d" % sl, wk_b[sl],
                                   wkc.ap()[:, hh, :],
                                   waits=[("te", wk_busy[sl]), ("dcc", c_cc)])
                        bank = kb[hh % 4]
                        for dc in range(n_dt):
                            wts = [c_xgl[dc]]
                            if dc == 0:
                                wts += [("dwk%d" % sl, c_wk)]
                                if (hh % 4) in kb_busy:
                                    wts.append(("act", kb_busy[hh % 4]))
                            pb.on("te", lambda e, hh=hh, dc=dc, bank=bank,
                                  sl=sl, x2q=x2q:
                                  e.matmul(bank[:, 0:KQ],
                                           wk_b[sl][:, dc * 128:(dc + 1) * 128],
                                           x2q[:, dc * KQ:(dc + 1) * KQ],
                                           start=(dc == 0), stop=(dc == n_dt - 1)),
                                  inc=("te" if dc == n_dt - 1 else None),
                                  waits=wts)
                        c_mm = pb.cnt["te"]
                        wk_busy[sl] = c_mm
                        kb_busy[hh % 4] = pb.on(
                            "act", lambda e, hh=hh, half=half, kq=kq, bank=bank:
                            e.activation(K_r[:, hh * t + half * T + kq * KQ:
                                             hh * t + half * T + kq * KQ + KQ],
                                         bank[:, 0:KQ], AF.Identity,
                                         bias=bkt[:, hh:hh + 1]),
                            inc="act", waits=[("te", c_mm)])
                    n_kt = KQ // 128
                    for gw in range(n_gw):
                        for dc in range(n_dt):
                            sl = dc % 2
                            c_wv = dma("sy", "dwv%d" % sl, wv_b[sl][:, 0:GW],
                                       wvc.ap()[:, dc, gw * GW:(gw + 1) * GW],
                                       waits=[("te", wv_busy[sl]),
                                              ("dcc", c_cc)])
                            for kt in range(n_kt):
                                bank = vb[kt]
                                wts = [c_xgl[dc]]
                                if dc == 0:
                                    wts += [("dwv%d" % sl, c_wv),
                                           ("dbv", c_bv)]
                                    if kt in vb_busy:
                                        wts.append(("dve", vb_busy[kt]))
                                else:
                                    wts = [("dwv%d" % sl, c_wv)]
                                last = (kt == n_kt - 1)
                                pb.on("te", lambda e, dc=dc, kt=kt, gw=gw,
                                      bank=bank, sl=sl, x2q=x2q:
                                      e.matmul(bank[:, 0:GW],
                                               x2q[:, dc * KQ + kt * 128:
                                                   dc * KQ + (kt + 1) * 128],
                                               wv_b[sl][:, 0:GW],
                                               start=(dc == 0), stop=False),
                                      inc=("te" if last else None), waits=wts)
                            wv_busy[sl] = pb.cnt["te"]
                        for kt in range(n_kt):
                            ktile = half * n_tt + kq * n_kt + kt
                            bank = vb[kt]
                            pb.on("te", lambda e, gw=gw, bank=bank:
                                  e.matmul(bank[:, 0:GW], onesrb[:, 0:128],
                                           bvr[0:1, gw * GW:(gw + 1) * GW],
                                           start=False, stop=True),
                                  inc="te")
                            c_mm = pb.cnt["te"]
                            vb_busy[kt] = pb.on(
                                "dve", lambda e, ktile=ktile, gw=gw, bank=bank:
                                e.tensor_copy(V_r[:, ktile * hw + gw * GW:
                                                  ktile * hw + gw * GW + GW],
                                              bank[:, 0:GW]),
                                inc="dve", waits=[("te", c_mm)])
                    x2q_busy[upar] = pb.cnt["te"]
        c_kv_act = pb.cnt["act"]
        c_kv_dve = pb.cnt["dve"]
        pb.fence()

        CH = C["CH"]
        Pt2 = [P_r, Pt_r]
        arow4 = arena_t[0:1, WB_OFF:WB_OFF + 512]
        iters = [(gq, hh) for gq in range(n_qt) for hh in range(nh_c)]
        n_it = len(iters)
        with nc.psum_tensor([128, 512], F32) as ss0, \
             nc.psum_tensor([128, 512], F32) as ss1, \
             nc.psum_tensor([128, 512], F32) as st0, \
             nc.psum_tensor([128, 512], F32) as st1, \
             nc.psum_tensor([128, 512], F32) as po0, \
             nc.psum_tensor([128, 512], F32) as po1, \
             nc.psum_tensor([128, 512], F32) as den_c, \
             nc.psum_tensor([128, 512], F32) as misc:
            ssb = [ss0, ss1]
            stb = [st0, st1]
            pob = [po0, po1]
            S = {}
            ss_busy = [0, 0]
            st_busy = [0, 0]
            osb_busy = [0, 0]
            pob_busy = [0, 0]
            den_free = [0]
            misc_busy = [0]
            pt_free = [0, 0]
            arow_busy = [0]
            mcol_busy = [0, 0]
            mx_busy = [0, 0]
            c_aga = []

            def emitA(i):
                gq, hh = iters[i]
                n_k = (gq + 1) * 128
                nblk = (n_k + 511) // 512
                par = i % 2
                q_ap = qmatT[:, (hh * n_qt + gq) * 128:
                             (hh * n_qt + gq + 1) * 128]
                mb = 4 + 4 * par
                for kb in range(nblk):
                    kb0 = kb * 512
                    w_ = min(512, n_k - kb0)
                    diag = (gq // 4 == kb)
                    bi = kb % 2
                    wts = [("dve", ss_busy[bi]), ("dve", mx_busy[par])]
                    if i == 0 and kb == 0:
                        wts += [("act", c_q), ("act", c_kv_act)]
                    pb.on("te", lambda e, kb0=kb0, w_=w_, q_ap=q_ap, hh=hh,
                          diag=diag, bi=bi:
                          e.matmul(ssb[bi][:, 0:w_], q_ap,
                                   K_r[:, hh * t + kb0: hh * t + kb0 + w_],
                                   start=True, stop=not diag),
                          inc=(None if diag else "te"), waits=wts)
                    if diag:
                        pb.on("te", lambda e, gq=gq, kb0=kb0, bi=bi:
                              e.matmul(ssb[bi][:, gq * 128 - kb0:
                                               (gq + 1) * 128 - kb0],
                                       idenT, trib[:, :],
                                       start=False, stop=True), inc="te")
                    c_blk = pb.cnt["te"]
                    ss_busy[bi] = pb.on(
                        "dve", lambda e, w_=w_, kb=kb, bi=bi, mb=mb:
                        e.reduce_max(mx[:, mb + kb:mb + kb + 1],
                                     ssb[bi][:, 0:w_], axis=AX.X),
                        inc="dve", waits=[("te", c_blk)])
                S[i] = dict(q_ap=q_ap, n_k=n_k, nblk=nblk,
                            c_maxes=pb.cnt["dve"])

            def emitB1a(i):
                gq, hh = iters[i]
                st_ = S[i]
                n_k, nblk, q_ap = st_["n_k"], st_["nblk"], st_["q_ap"]
                par = i % 2
                mb = 4 + 4 * par
                pb.wait("dve", "dve", st_["c_maxes"])
                mcol = mb
                if nblk > 1:
                    pb.on("dve", lambda e, nblk=nblk, mb=mb, par=par:
                          e.reduce_max(mx[:, par:par + 1],
                                       mx[:, mb:mb + nblk], axis=AX.X),
                          inc="dve")
                    pb.wait("dve", "dve", pb.cnt["dve"])
                    mcol = par
                c_nm = pb.on("dve", lambda e, mcol=mcol, par=par:
                             e.tensor_scalar(mcolb[:, par:par + 1],
                                             mx[:, mcol:mcol + 1], -1.0, None,
                                             ALU.mult), inc="dve",
                             waits=[("te", mcol_busy[par])])
                mx_busy[par] = c_nm
                c_nmm = pb.on("te", lambda e, par=par: e.matmul(
                    misc[0:1, 0:512], mcolb[:, par:par + 1], iden4,
                    start=True, stop=True), inc="te",
                    waits=[("dve", c_nm), ("act", misc_busy[0])])
                mcol_busy[par] = c_nmm
                c_nmev = pb.on("act", lambda e: e.activation(
                    arow4[0:1, 0:512], misc[0:1, 0:512], AF.Copy), inc="act",
                    waits=[("te", c_nmm), ("te", arow_busy[0])])
                misc_busy[0] = c_nmev
                st_["c_nmev"] = c_nmev

            def emitB1b(i):
                gq, hh = iters[i]
                st_ = S[i]
                n_k, q_ap = st_["n_k"], st_["q_ap"]
                n_kt = n_k // 128
                c_nmev = st_["c_nmev"]
                exp_of_grp = []
                ptbuf = Pt2[i % 2]
                st_["ptbuf"] = ptbuf
                for b0 in range(0, n_kt, 4):
                    nb = min(4, n_kt - b0)
                    bi = (b0 // 4) % 2
                    bank = stb[bi]
                    for j in range(nb):
                        kt = b0 + j
                        wts = []
                        if j == 0:
                            wts = [("act", st_busy[bi]), ("act", c_nmev)]
                        pb.on("te", lambda e, j=j, kt=kt, hh=hh, q_ap=q_ap,
                              bank=bank:
                              e.matmul(bank[:, j * 128:(j + 1) * 128],
                                       K_r[:, hh * t + kt * 128:
                                           hh * t + (kt + 1) * 128],
                                       q_ap, start=(j == 0), stop=False),
                              inc=None, waits=wts)
                        if kt == gq:
                            pb.on("te", lambda e, j=j, bank=bank:
                                  e.matmul(bank[:, j * 128:(j + 1) * 128],
                                           trib[:, :], idenT,
                                           start=False, stop=False),
                                  inc=None)
                    pb.on("te", lambda e, nb=nb, bank=bank:
                          e.matmul(bank[:, 0:nb * 128],
                                   onesrb[0:1, 0:128], arow4[0:1, 0:nb * 128],
                                   start=False, stop=True), inc="te")
                    c_stp = pb.cnt["te"]
                    c_exp = pb.on("act", lambda e, b0=b0, nb=nb, bank=bank,
                                  ptbuf=ptbuf:
                                  e.activation(
                                      ptbuf[:, b0 * 128:(b0 + nb) * 128],
                                      bank[:, 0:nb * 128], AF.Exp,
                                      scale=C["scale"]), inc="act",
                                  waits=[("te", c_stp),
                                         ("te", pt_free[i % 2])])
                    st_busy[bi] = c_exp
                    exp_of_grp.append(c_exp)
                arow_busy[0] = pb.cnt["te"]
                st_["exps"] = exp_of_grp

            def emitB2(i):
                gq, hh = iters[i]
                st_ = S[i]
                n_k = st_["n_k"]
                n_kt = n_k // 128
                exp_of_grp = st_["exps"]
                ptbuf = st_["ptbuf"]
                oi = i % 2
                obank = pob[oi]
                for kt in range(n_kt):
                    wts = [("act", exp_of_grp[kt // 4])] if kt % 4 == 0 else []
                    if kt == 0:
                        wts += [("dve", c_kv_dve), ("act", pob_busy[oi]),
                                ("dve", den_free[0])]
                    pb.on("te", lambda e, kt=kt, hh=hh, obank=obank,
                          n_kt=n_kt, ptbuf=ptbuf:
                          e.matmul(obank[:, 0:128],
                                   ptbuf[:, kt * 128:(kt + 1) * 128],
                                   V_r[:, kt * hw + hh * 128:
                                       kt * hw + (hh + 1) * 128],
                                   start=(kt == 0), stop=(kt == n_kt - 1)),
                          inc=None, waits=wts)
                    pb.on("te", lambda e, kt=kt, n_kt=n_kt, ptbuf=ptbuf:
                          e.matmul(den_c[:, 0:1],
                                   ptbuf[:, kt * 128:(kt + 1) * 128],
                                   onesd[:, 0:1],
                                   start=(kt == 0), stop=(kt == n_kt - 1)),
                          inc=("te" if kt == n_kt - 1 else None))
                c_pv = pb.cnt["te"]
                pt_free[i % 2] = c_pv
                pb.wait("dve", "dve", pb.cnt["dve"])
                c_rc = pb.on("dve", lambda e, oi=oi: e.reciprocal(
                    mx[:, 2 + oi:3 + oi], den_c[:, 0:1]), inc="dve",
                    waits=[("te", c_pv)])
                den_free[0] = c_rc
                osb = outsb[:, oi * 256: oi * 256 + 128]
                c_oe = pb.on("act", lambda e, obank=obank, osb=osb, oi=oi:
                             e.activation(osb, obank[:, 0:128], AF.Identity,
                                          scale=mx[:, 2 + oi:3 + oi]),
                             inc="act",
                             waits=[("te", c_pv), ("dve", c_rc),
                                    ("dos%d" % oi, osb_busy[oi])])
                pob_busy[oi] = c_oe
                osb_busy[oi] = dma(
                    "sy", "dos%d" % oi,
                    aga_in.ap()[gq * 128:(gq + 1) * 128,
                                hh * 128:(hh + 1) * 128],
                    osb, waits=[("act", c_oe)])
                del S[i]
                if hh == nh_c - 1:
                    for (q0, q1) in CH:
                        if gq + 1 == q1:
                            r0, r1 = q0 * 128, q1 * 128
                            pb.on("gp", lambda e, r0=r0, r1=r1:
                                  e.collective_compute(
                                      "AllGather", ALU.bypass,
                                      replica_groups=pairs,
                                      ins=[aga_in.ap()[r0:r1, :].opt()],
                                      outs=[aga_out.ap()[2 * r0:2 * r1, :]
                                            .opt()]), inc="cc",
                                  waits=[("dos0", osb_busy[0]),
                                         ("dos1", osb_busy[1])])
                            c_aga.append(pb.cnt["cc"])

            for s in range(n_it + 2):
                if 1 <= s and s - 1 < n_it:
                    emitB1a(s - 1)
                if s < n_it:
                    emitA(s)
                if 1 <= s and s - 1 < n_it:
                    emitB1b(s - 1)
                if s >= 2:
                    emitB2(s - 2)
        pb.fence()

        for j in range(2):
            for tt in range(n_tt):
                col = j * n_tt + tt
                pb.on("gp", lambda e, j=j, tt=tt, col=col: e.indirect_dma_start(
                    o2tok[:, (tt * 2 + j) * hw: (tt * 2 + j + 1) * hw], None,
                    aga_out.ap()[:, :],
                    IndirectOffsetOnAxis(gidx[:, col:col + 1], 0)),
                    inc="dga", by=16, waits=[("cc", c_aga[-1])])
        c_gath = pb.cnt["dga"]

        with nc.psum_tensor([128, 1024], BF16) as tq0, \
             nc.psum_tensor([128, 1024], BF16) as tq1:
            tqb = [tq0, tq1]
            busy = [0, 0]
            bi = 0
            n_dd = 2 * hw // 128
            for tt in range(n_tt):
                for dc0 in range(0, n_dd, 4):
                    nb = min(4, n_dd - dc0)
                    bank = tqb[bi % 2]
                    for j in range(nb):
                        dc = dc0 + j
                        wts = []
                        if j == 0:
                            wts = [("dga", c_gath), ("dve", busy[bi % 2])]
                        pb.on("te", lambda e, tt=tt, dc=dc, j=j, bank=bank:
                              e.transpose(bank[:, j * 128:(j + 1) * 128],
                                          o2tok[:, tt * 2 * hw + dc * 128:
                                                tt * 2 * hw + (dc + 1) * 128],
                                          idenT),
                              inc=("te" if j == nb - 1 else None), waits=wts)
                    c_tp = pb.cnt["te"]
                    o2v = o2T.rearrange("p (c f) -> p c f", c=n_dd)
                    busy[bi % 2] = pb.on(
                        "dve", lambda e, tt=tt, dc0=dc0, nb=nb, bank=bank, o2v=o2v:
                        e.tensor_copy(
                            o2v[:, dc0:dc0 + nb, tt * 128:(tt + 1) * 128],
                            bank[:, 0:nb * 128]
                            .rearrange("p (c f) -> p c f", c=nb)),
                        inc="dve", waits=[("te", c_tp)])
                    bi += 1
        c_o2t = pb.cnt["dve"]
        pb.fence()

        with ExitStack() as osx:
            ob = [osx.enter_context(nc.psum_tensor("ob%d" % i, [128, TG], F32))
                  for i in range(8)]
            wo_busy = [0, 0]
            bank_busy = {}
            for og in range(0, n_dt, 2):
                sl = (og // 2) % 2
                c_l = dma("sy", "dwo%d" % sl, wo_b[sl],
                          woc.ap()[:, og:og + 2, :]
                          .rearrange("p o f -> p (o f)"),
                          waits=[("te", wo_busy[sl]), ("dcc", c_cc)])
                for oi in range(2):
                    ot = og + oi
                    for tg_i in range(n_tg):
                        bank = ob[sl * 4 + oi * n_tg + tg_i]
                        bkey = sl * 4 + oi * n_tg + tg_i
                        for dc in range(n_dt):
                            wts = []
                            if dc == 0:
                                wts = [("dwo%d" % sl, c_l), ("dve", c_o2t)]
                                if bkey in bank_busy:
                                    wts.append(("dve", bank_busy[bkey]))
                            pb.on("te", lambda e, oi=oi, tg_i=tg_i, dc=dc,
                                  sl=sl, bank=bank:
                                  e.matmul(bank[:, :],
                                           wo_b[sl][:, (oi * n_dt + dc) * 128:
                                                    (oi * n_dt + dc + 1) * 128],
                                           o2T[:, dc * T + tg_i * TG:
                                               dc * T + tg_i * TG + TG],
                                           start=(dc == 0), stop=(dc == n_dt - 1)),
                                  inc=("te" if dc == n_dt - 1 else None),
                                  waits=wts)
                        c_mm = pb.cnt["te"]
                        bank_busy[bkey] = pb.on(
                            "dve", lambda e, ot=ot, tg_i=tg_i, bank=bank:
                            e.scalar_tensor_tensor(
                                strm[:, ot * T + tg_i * TG:
                                     ot * T + tg_i * TG + TG],
                                bank[:, :], bot[:, ot:ot + 1],
                                strm[:, ot * T + tg_i * TG:
                                     ot * T + tg_i * TG + TG],
                                ALU.add, ALU.add), inc="dve",
                            waits=[("te", c_mm)])
                wo_busy[sl] = pb.cnt["te"]
        pb.fence()

    def first_post_ln():
        c_scal_done[0] = emit_scal_finalize()

    emit_ffn(True, pre_waits=[("dx", c_x), ("dcn", c_const)],
             post_ln_cb=first_post_ln, ln_stat_waits=[("dve", c_p3)])
    emit_attn()
    emit_ffn(False)

    c_out = dma("sy", "dou", yT[:, :, :], t3(strm[:, :], n_dt),
                waits=[("dve", pb.cnt["dve"])])
    pb.wait("sy", "dou", c_out)

    sems = {s: st.enter_context(nc.semaphore("s_" + s)) for s in pb.cnt}

    with nc.Block() as block:
        @block.tensor
        def _(e):
            pb.replay("te", e, sems)

        @block.scalar
        def _(e):
            pb.replay("act", e, sems)

        @block.vector
        def _(e):
            pb.replay("dve", e, sems)

        @block.gpsimd
        def _(e):
            pb.replay("gp", e, sems)

        @block.sync
        def _(e):
            pb.replay("sy", e, sems)

    st.close()
    return nc



def stage_inputs(C, inputs):
    import ml_dtypes
    d, t, b, nh, dh, hid = C["d"], C["t"], C["b"], C["nh"], C["dh"], C["hid"]
    T, n_dt, n_tt, nh_c = C["T"], C["n_dt"], C["n_tt"], C["nh_c"]
    n_hc, hidp, QD, n_qd, hw = C["n_hc"], C["hidp"], C["QD"], C["n_qd"], C["hw"]

    x = np.ascontiguousarray(np.asarray(inputs["L0_attn_x"], dtype=np.float32))
    Wq = np.asarray(inputs["L0_attn_Wq"], np.float32)
    Wk = np.asarray(inputs["L0_attn_Wk"], np.float32)
    Wv = np.asarray(inputs["L0_attn_Wv"], np.float32)
    Wo = np.asarray(inputs["L0_attn_Wo"], np.float32)
    bq = np.asarray(inputs["L0_attn_bq"], np.float32)
    bk = np.asarray(inputs["L0_attn_bk"], np.float32)
    bv = np.asarray(inputs["L0_attn_bv"], np.float32)
    bo = np.asarray(inputs["L0_attn_bo"], np.float32)
    w1 = np.asarray(inputs["L0_ffn_w1"], np.float32)
    w3 = np.asarray(inputs["L0_ffn_w3"], np.float32)
    w2 = np.asarray(inputs["L0_ffn_w2"], np.float32)

    def tile_w13(w):
        wp = np.zeros((hidp, d), np.float32)
        wp[:hid] = w
        a = wp.reshape(n_hc, 128, n_dt, 128)
        return np.ascontiguousarray(
            a.transpose(3, 0, 2, 1).reshape(128, n_hc, n_dt * 128))

    w1t, w3t = tile_w13(w1), tile_w13(w3)
    w2p = np.zeros((d, hidp), np.float32)
    w2p[:, :hid] = w2
    w2q = np.ascontiguousarray(
        w2p.reshape(n_qd, QD, n_hc, 128).transpose(3, 0, 2, 1))
    wqt = np.ascontiguousarray(
        Wq.reshape(n_dt, 128, n_dt, 128).transpose(1, 2, 0, 3)
        .reshape(128, n_dt, n_dt * 128))
    wot = np.ascontiguousarray(
        Wo.reshape(n_dt, 128, n_dt, 128).transpose(1, 2, 0, 3)
        .reshape(128, n_dt, n_dt * 128))
    bqt = np.ascontiguousarray(bq.reshape(n_dt, 128).T)
    bot = np.ascontiguousarray(bo.reshape(n_dt, 128).T)

    Ns = C["Ns"]

    def slc(w, c):
        fl = w.reshape(-1)
        seg = fl.size // 8
        return np.ascontiguousarray(fl[c * seg:(c + 1) * seg].reshape(128, Ns))

    trib = np.triu(np.full((128, 128), NEG, np.float32), k=1).astype(
        ml_dtypes.bfloat16)
    iden = np.tile(np.eye(128, dtype=np.float32), (1, 4)).astype(
        ml_dtypes.bfloat16)
    ones_d = np.ones((128, 1), ml_dtypes.bfloat16)
    ones_f = np.ones((128, 1), np.float32)
    ones_r = np.ones((1, 128), np.float32)
    ones_rb = np.ones((1, 128), ml_dtypes.bfloat16)

    in_maps = []
    for c in range(C["n_cores"]):
        bb, h = c // 2, c % 2
        m = {}
        xt = x[bb, h * T:(h + 1) * T, :].T
        m["xT"] = np.ascontiguousarray(
            xt.reshape(n_dt, 128, T).transpose(1, 0, 2))
        m["w1t"], m["w3t"], m["w2q"] = w1t, w3t, w2q
        m["wqt"], m["wot"] = wqt, wot
        cols = slice(h * hw, (h + 1) * hw)
        m["wkt"] = np.ascontiguousarray(
            Wk[:, cols].reshape(n_dt, 128, nh_c, 128)
            .transpose(1, 2, 0, 3).reshape(128, nh_c, n_dt * 128))
        m["wvt"] = np.ascontiguousarray(
            Wv[:, cols].reshape(n_dt, 128, hw).transpose(1, 0, 2))
        m["bqt"], m["bot"] = bqt, bot
        m["bkt"] = np.ascontiguousarray(bk[cols].reshape(nh_c, 128).T)
        m["bvr"] = np.ascontiguousarray(bv[cols].reshape(1, hw))
        m["w1s"], m["w3s"], m["w2s"] = slc(w1, c), slc(w3, c), slc(w2, c)
        m["trib"], m["iden"] = trib, iden
        m["ones_d"], m["ones_f"] = ones_d, ones_f
        m["ones_r"], m["ones_rb"] = ones_r, ones_rb
        m["halfc"] = np.repeat(np.array([[-0.5, 0.5]], np.float32), 128, 0)
        CH = C["CH"]
        starts = {}
        off = 0
        for (q0, q1) in CH:
            starts[q0 * 128] = (off, (q1 - q0) * 128)
            off += 2 * (q1 - q0) * 128
        gi = np.zeros((128, 2 * n_tt), np.int32)
        for j in range(2):
            for tt in range(n_tt):
                r = h * T + tt * 128 + np.arange(128)
                base = np.zeros_like(r)
                loc = np.zeros_like(r)
                ln_ = np.zeros_like(r)
                for s0, (o_, l_) in starts.items():
                    mseg = (r >= s0) & (r < s0 + l_)
                    base[mseg] = o_
                    loc[mseg] = r[mseg] - s0
                    ln_[mseg] = l_
                gi[:, j * n_tt + tt] = base + j * ln_ + loc
        m["gidx"] = gi
        in_maps.append(m)
    return in_maps


def assemble_output(C, results):
    out = np.empty((C["b"], C["t"], C["d"]), np.float32)
    T, n_dt = C["T"], C["n_dt"]
    for c in range(C["n_cores"]):
        bb, h = c // 2, c % 2
        y = np.asarray(results[c]["yT"]).reshape(128, n_dt, T)
        out[bb, h * T:(h + 1) * T, :] = (
            y.transpose(1, 0, 2).reshape(C["d"], T).T)
    return out


def kernel(**inputs):
    C = CFG
    in_maps = stage_inputs(C, inputs)
    nc = build_nc(C)
    res = run_bass_kernel_spmd(nc, in_maps, core_ids=list(range(C["n_cores"])))
    return assemble_output(C, res.results)


# version 18
# speedup vs baseline: 1.0988x; 1.0718x over previous
import math
from contextlib import ExitStack

import numpy as np

import concourse.bass as bass
import concourse.mybir as mybir
from concourse.bass_utils import run_bass_kernel_spmd
from concourse.bass_core_rust import IndirectOffsetOnAxis

F32 = mybir.dt.float32
BF16 = mybir.dt.bfloat16
I32 = mybir.dt.int32
AF = mybir.ActivationFunctionType
ALU = mybir.AluOpType
AX = mybir.AxisListType

NEG = -1.0e9


def make_cfg(d=2048, t=2048, b=4, nh=16, hid=5461):
    dh = 128
    assert d == nh * dh and t == nh * 128
    C = dict(d=d, t=t, b=b, nh=nh, dh=dh, hid=hid)
    C["n_cores"] = 2 * b
    C["T"] = t // 2
    C["n_dt"] = d // 128
    C["n_tt"] = C["T"] // 128
    C["nh_c"] = nh // 2
    C["n_hc"] = (hid + 127) // 128
    C["hidp"] = C["n_hc"] * 128
    C["rpb"] = 128 // nh
    C["n_qt"] = t // 128
    C["TG"] = min(512, C["T"])
    C["n_tg"] = C["T"] // C["TG"]
    C["QD"] = min(512, d)
    C["n_qd"] = d // C["QD"]
    C["KQ"] = min(256, C["T"])
    C["n_kq"] = C["T"] // C["KQ"]
    C["WCH"] = min(512, C["n_dt"] * 128)
    C["SW"] = min(t, 2048)
    C["scale"] = 1.0 / math.sqrt(dh)
    C["Nw"] = hid * d
    assert (C["Nw"] // 8) % 128 == 0
    C["Ns"] = C["Nw"] // 8 // 128
    C["SCH"] = min(2 * C["WCH"], C["Ns"])
    assert C["QD"] <= C["WCH"]
    C["hw"] = C["nh_c"] * dh
    nq = C["n_qt"]
    if nq >= 16:
        C["CH"] = [(0, 8), (8, 12), (12, 15), (15, 16)]
    elif nq >= 2:
        C["CH"] = [(0, nq // 2), (nq // 2, nq)]
    else:
        C["CH"] = [(0, nq)]
    return C


CFG = make_cfg()


class PB:

    ENGS = ("te", "act", "dve", "gp", "sy")

    def __init__(self):
        self.q = {e: [] for e in self.ENGS}
        self.cnt = {}
        for s in ("te", "act", "dve", "gpc", "cc"):
            self.cnt[s] = 0
        self.hw = {}
        self.pending = {e: None for e in self.ENGS}

    def sem(self, name):
        if name not in self.cnt:
            self.cnt[name] = 0
        return name

    def wait(self, eng, sem, val):
        if val is None or val <= 0:
            return
        if self.hw.get((eng, sem), 0) >= val:
            return
        self.hw[(eng, sem)] = val
        self.q[eng].append(("w", sem, val))

    def fence(self):
        snap = dict(self.cnt)
        for e in self.ENGS:
            self.pending[e] = snap

    def on(self, eng, fn, inc=None, by=1, waits=()):
        if self.pending[eng] is not None:
            snap, self.pending[eng] = self.pending[eng], None
            for s, v in snap.items():
                self.wait(eng, s, v)
        for s, v in waits:
            self.wait(eng, s, v)
        post = None
        if inc is not None:
            self.sem(inc)
            self.cnt[inc] += by
            post = self.cnt[inc]
        self.q[eng].append(("i", fn, inc, by))
        return post

    def replay(self, eng_name, e, sems):
        for item in self.q[eng_name]:
            if item[0] == "w":
                e.wait_ge(sems[item[1]], item[2])
            else:
                _, fn, inc, by = item
                ins = fn(e)
                if inc is not None:
                    ins.then_inc(sems[inc], by)


class Arena:

    def __init__(self, tensor, cols):
        self.t = tensor
        self.cols = cols
        self.used = []

    def region(self, name, off, n):
        assert off + n <= self.cols, (name, off, n, self.cols)
        self.used.append((name, off, n))
        return self.t[:, off:off + n]


def build_nc(C):
    nc = bass.Bass("TRN2", target_bir_lowering=False)
    d, t, T = C["d"], C["t"], C["T"]
    nh, dh, nh_c, hw = C["nh"], C["dh"], C["nh_c"], C["hw"]
    n_dt, n_tt, n_hc = C["n_dt"], C["n_tt"], C["n_hc"]
    TG, n_tg, QD, n_qd = C["TG"], C["n_tg"], C["QD"], C["n_qd"]
    KQ, n_kq = C["KQ"], C["n_kq"]
    n_qt, rpb, SW = C["n_qt"], C["rpb"], C["SW"]
    WCH, SCH = C["WCH"], C["SCH"]
    n_wch = (n_dt * 128) // WCH
    dpc = WCH // 128

    ap = {}

    def din(name, shape, dt=F32):
        ap[name] = nc.dram_tensor(name, list(shape), dt, kind="ExternalInput").ap()

    din("xT", (128, n_dt, T))
    din("w1t", (128, n_hc, n_dt * 128))
    din("w3t", (128, n_hc, n_dt * 128))
    din("w2q", (128, n_qd, n_hc, QD))
    din("wqt", (128, n_dt, n_dt * 128))
    din("wkt", (128, nh_c, n_dt * 128))
    din("wvt", (128, n_dt, hw))
    din("wot", (128, n_dt, n_dt * 128))
    din("bqt", (128, n_dt))
    din("bkt", (128, nh_c))
    din("bvr", (1, hw))
    din("bot", (128, n_dt))
    din("w1s", (128, C["Ns"]))
    din("w3s", (128, C["Ns"]))
    din("w2s", (128, C["Ns"]))
    din("trib", (128, 128), BF16)
    din("iden", (128, 512), BF16)
    din("ones_d", (128, 1), BF16)
    din("ones_f", (128, 1))
    din("ones_r", (1, 128))
    din("ones_rb", (1, 128), BF16)
    din("gidx", (128, 2 * n_tt), I32)
    din("halfc", (128, 2))
    yT = nc.dram_tensor("yT", [128, n_dt, T], F32, kind="ExternalOutput").ap()

    agx_in = nc.dram_tensor("agx_in", [128, n_dt * T], BF16)
    agx_out = nc.dram_tensor("agx_out", [2 * 128, n_dt * T], BF16)
    aga_in = nc.dram_tensor("aga_in", [t, hw], BF16)
    aga_out = nc.dram_tensor("aga_out", [2 * t, hw], BF16)
    ar_in = nc.dram_tensor("ar_in", [3, 1], F32)
    ar_out = nc.dram_tensor("ar_out", [3, 1], F32)
    t1c = nc.dram_tensor("t1c", [128, n_hc, n_dt * 128], BF16)
    t3c = nc.dram_tensor("t3c", [128, n_hc, n_dt * 128], BF16)
    t2c = nc.dram_tensor("t2c", [128, n_qd, n_hc, QD], BF16)
    wqc = nc.dram_tensor("wqc", [128, n_dt, n_dt * 128], BF16)
    wkc = nc.dram_tensor("wkc", [128, nh_c, n_dt * 128], BF16)
    wvc = nc.dram_tensor("wvc", [128, n_dt, hw], BF16)
    woc = nc.dram_tensor("woc", [128, n_dt, n_dt * 128], BF16)

    pb = PB()
    st = ExitStack()

    def sb(name, shape, dt=F32):
        return st.enter_context(nc.sbuf_tensor(name, list(shape), dt))

    strm = sb("strm", (128, n_dt * T))
    AR_COLS = 65536 if d >= 2048 else 32768
    arena_t = sb("arena", (128, AR_COLS), BF16)
    A = Arena(arena_t, AR_COLS)
    SCR = max(WCH, TG)
    wf32 = sb("wf32", (128, 4 * WCH + 3 * SCR))
    trib = sb("trib_s", (128, 128), BF16)
    onesd = sb("onesd", (128, 1), BF16)
    onesf = sb("onesf", (128, 1))
    onesr = sb("onesr", (1, 128))
    onesrb = sb("onesrb", (1, 128), BF16)
    scal = sb("scal", (128, 8))
    bqt = sb("bqt_s", (128, n_dt))
    bkt = sb("bkt_s", (128, nh_c))
    bot = sb("bot_s", (128, n_dt))
    gidx = sb("gidx_s", (128, 2 * n_tt), I32)
    part = sb("part", (128, 8))
    srow = sb("srow", (1, 16))
    mx = sb("mx", (128, 12))
    halfc = sb("halfc_s", (128, 2))

    CON = 0
    iden4 = A.region("iden4", CON, 512)
    idenT = iden4[:, 0:128]
    bvr = A.region("bvr", CON + 512, hw)[0:1, :]
    outsb_base = CON + 512 + hw
    outsb = A.region("outsb", outsb_base, 512)
    arowb = arena_t[0:1, outsb_base + 128:outsb_base + 256]
    mcolb = arena_t[:, outsb_base + 384:outsb_base + 386]
    BASE = outsb_base + 512

    G_OFF = BASE
    G_COLS = n_hc * T
    TB_OFF = G_OFF + G_COLS
    X2_OFF = TB_OFF + 5 * WCH + TG
    X2_COLS = n_dt * T
    g_r = A.region("g", G_OFF, G_COLS)
    tring = [A.region("tr%d" % i, TB_OFF + i * WCH, WCH) for i in range(4)]
    abf_r = A.region("abf", TB_OFF + 4 * WCH, WCH)
    sA_r = A.region("sA", TB_OFF + 5 * WCH, TG)
    x2T = A.region("x2T", X2_OFF, X2_COLS)
    assert X2_OFF + X2_COLS <= AR_COLS, (X2_OFF, X2_COLS, AR_COLS)
    xbf = arena_t[:, G_OFF:G_OFF + n_dt * T]
    xsq = arena_t[:, G_OFF + n_dt * T:G_OFF + 2 * n_dt * T]
    assert 2 * n_dt * T <= G_COLS + 4 * WCH + TG

    QM_OFF = G_OFF
    K_OFF = QM_OFF + n_dt * T
    V_OFF = K_OFF + nh_c * t
    XQ_OFF = V_OFF + (t // 128) * hw
    P_OFF = XQ_OFF + 2 * n_dt * KQ
    PT_OFF = P_OFF + SW
    WB_OFF = PT_OFF + SW
    qmatT = arena_t[:, QM_OFF:QM_OFF + n_dt * T]
    K_r = arena_t[:, K_OFF:K_OFF + nh_c * t]
    V_r = arena_t[:, V_OFF:V_OFF + (t // 128) * hw]
    x2qb = [arena_t[:, XQ_OFF + i * n_dt * KQ:
                    XQ_OFF + (i + 1) * n_dt * KQ] for i in range(2)]
    P_r = arena_t[:, P_OFF:P_OFF + SW]
    Pt_r = arena_t[:, PT_OFF:PT_OFF + SW]
    wq_b = [arena_t[:, K_OFF + i * 2 * n_dt * 128:
                    K_OFF + (i + 1) * 2 * n_dt * 128] for i in range(2)]
    wk_b = [arena_t[:, P_OFF + i * (n_dt * 128):
                    P_OFF + (i + 1) * (n_dt * 128)] for i in range(2)]
    wv_b = [arena_t[:, WB_OFF + i * hw: WB_OFF + (i + 1) * hw] for i in range(2)]
    assert WB_OFF + 2 * hw <= AR_COLS, (WB_OFF + 2 * hw, AR_COLS)
    assert P_OFF + 2 * n_dt * 128 <= WB_OFF + 2 * hw
    o2tok = arena_t[:, K_OFF:K_OFF + n_tt * 2 * hw]
    o2T = arena_t[:, V_OFF:V_OFF + n_dt * T]
    assert V_OFF + n_dt * T <= AR_COLS
    wo_b = [arena_t[:, QM_OFF + i * 2 * n_dt * 128:
                    QM_OFF + (i + 1) * 2 * n_dt * 128] for i in range(2)]

    mb_r = wf32[:, 4 * WCH:4 * WCH + TG]
    rb_r = wf32[:, 4 * WCH + SCR:4 * WCH + SCR + TG]
    ln_tmp = wf32[:, 4 * WCH + 2 * SCR:4 * WCH + 2 * SCR + TG]
    ta_r = wf32[:, 4 * WCH:4 * WCH + WCH]
    tb_r = wf32[:, 4 * WCH + SCR:4 * WCH + SCR + WCH]
    m_row = rb_r[0:1, 0:TG]
    r_row = ln_tmp[0:1, 0:TG]

    def dma(eng, semname, out, in_, waits=()):
        pb.sem(semname)
        return pb.on(eng, lambda e, o=out, i=in_: e.dma_start(out=o, in_=i),
                     inc=semname, by=16, waits=waits)

    def t3(sb_ap, cdim):
        return sb_ap.rearrange("p (c f) -> p c f", c=cdim)

    c_x = dma("sy", "dx", t3(strm[:, :], n_dt), ap["xT"][:, :, :])
    dma("sy", "dcn", trib[:, :], ap["trib"][:, :])
    dma("sy", "dcn", iden4, ap["iden"][:, :])
    dma("sy", "dcn", onesd[:, :], ap["ones_d"][:, :])
    dma("sy", "dcn", onesf[:, :], ap["ones_f"][:, :])
    dma("sy", "dcn", onesr[:, :], ap["ones_r"][:, :])
    dma("sy", "dcn", onesrb[:, :], ap["ones_rb"][:, :])
    dma("sy", "dcn", bqt[:, :], ap["bqt"][:, :])
    dma("sy", "dcn", bkt[:, :], ap["bkt"][:, :])
    dma("sy", "dcn", bot[:, :], ap["bot"][:, :])
    dma("sy", "dcn", gidx[:, :], ap["gidx"][:, :])
    c_const = dma("sy", "dcn", halfc[:, :], ap["halfc"][:, :])
    c_bv = dma("gp", "dbv", bvr, ap["bvr"][:, :])

    n_sch = (C["Ns"] + SCH - 1) // SCH
    slot_busy = [0, 0]
    for j, nm in enumerate(("w1s", "w3s", "w2s")):
        for ch in range(n_sch):
            w = min(SCH, C["Ns"] - ch * SCH)
            sl = (j * n_sch + ch) % 2
            buf = wf32[:, sl * 2 * WCH: sl * 2 * WCH + w]
            c_l = dma("sy" if sl == 0 else "act", "dsl%d" % sl, buf,
                      ap[nm][:, ch * SCH: ch * SCH + w],
                      waits=[("dve", slot_busy[sl])])
            c_r = pb.on("dve", lambda e, buf=buf: e.reduce_sum(
                part[:, 6:7], buf, axis=AX.X, apply_absolute_value=True),
                inc="dve", waits=[("dsl%d" % sl, c_l)])
            slot_busy[sl] = c_r
            pb.wait("dve", "dve", c_r)
            if ch == 0:
                pb.on("dve", lambda e, j=j: e.tensor_copy(
                    part[:, j:j + 1], part[:, 6:7]), inc="dve")
            else:
                pb.on("dve", lambda e, j=j: e.tensor_tensor(
                    part[:, j:j + 1], part[:, j:j + 1], part[:, 6:7],
                    ALU.add), inc="dve")
            pb.wait("dve", "dve", pb.cnt["dve"])
    c_part = pb.cnt["dve"]

    with nc.psum_tensor([3, 512], F32) as ps_s3:
        c_mm3 = pb.on("te", lambda e: e.matmul(ps_s3[:, 0:1], part[:, 0:3],
                                               onesf[:, 0:1], start=True, stop=True),
                      inc="te", waits=[("dve", c_part), ("dcn", c_const)])
        pb.on("dve", lambda e: e.tensor_copy(part[0:3, 4:5], ps_s3[:, 0:1]),
              inc="dve", waits=[("te", c_mm3)])
    c_p3 = pb.cnt["dve"]
    c_ari = dma("sy", "dar", ar_in[:, :], part[0:3, 4:5],
                waits=[("dve", c_p3)])
    pb.on("gp", lambda e: e.collective_compute(
        "AllReduce", ALU.add, replica_groups=[list(range(C["n_cores"]))],
        ins=[ar_in.ap().opt()], outs=[ar_out.ap().opt()]), inc="cc",
        waits=[("dar", c_ari)])
    c_ar = pb.cnt["cc"]
    c_sr = dma("sy", "dsr", srow[0:1, 0:3],
               ar_out.ap().rearrange("a b -> b a"), waits=[("cc", c_ar)])

    dma("gp", "dcc", wqc.ap()[:, :, :], ap["wqt"][:, :, :],
        waits=[("dsl0", pb.cnt["dsl0"]), ("dsl1", pb.cnt["dsl1"])])
    dma("gp", "dcc", wkc.ap()[:, :, :], ap["wkt"][:, :, :])
    dma("gp", "dcc", wvc.ap()[:, :, :], ap["wvt"][:, :, :])
    c_cc = dma("gp", "dcc", woc.ap()[:, :, :], ap["wot"][:, :, :])

    def emit_scal_finalize():
        Nw = float(C["Nw"])
        pb.on("dve", lambda e: e.reciprocal(srow[0:1, 4:7], srow[0:1, 0:3]),
              inc="dve", waits=[("dsr", c_sr)])
        pb.wait("dve", "dve", pb.cnt["dve"])
        pb.on("dve", lambda e: e.tensor_scalar(srow[0:1, 8:11], srow[0:1, 4:7],
                                               Nw, None, ALU.mult), inc="dve")
        pb.on("dve", lambda e: e.tensor_scalar(srow[0:1, 11:12], srow[0:1, 0:1],
                                               0.5 / Nw, None, ALU.mult), inc="dve")
        pb.on("dve", lambda e: e.tensor_scalar(srow[0:1, 12:13], srow[0:1, 1:2],
                                               1.0 / Nw, None, ALU.mult), inc="dve")
        pb.on("dve", lambda e: e.tensor_scalar(srow[0:1, 13:14], srow[0:1, 2:3],
                                               0.25 / Nw, None, ALU.mult), inc="dve")
        c_fin = pb.cnt["dve"]
        with nc.psum_tensor([128, 512], F32) as ps_b6:
            c_b6 = pb.on("te", lambda e: e.matmul(
                ps_b6[:, 0:6], onesr[:, 0:128], srow[0:1, 8:14],
                start=True, stop=True), inc="te",
                waits=[("dve", c_fin), ("dcn", c_const)])
            pb.on("dve", lambda e: e.tensor_copy(scal[:, 0:6], ps_b6[:, 0:6]),
                  inc="dve", waits=[("te", c_b6)])
        return pb.cnt["dve"]

    INV1, INV3, INV2, HS1, HS3, FS2 = range(6)

    def emit_ln(extra_waits=(), stat_waits=()):
        c0 = pb.on("act", lambda e: e.activation(xbf[:, :], strm[:, :], AF.Copy),
                   inc="act", waits=list(extra_waits))
        c1 = pb.on("act", lambda e: e.activation(xsq[:, :], strm[:, :], AF.Square),
                   inc="act", waits=list(extra_waits))
        for tg_i in range(n_tg):
            t0 = tg_i * TG
            with nc.psum_tensor([1, max(TG, 512)], F32) as ps_m, \
                 nc.psum_tensor([1, max(TG, 512)], F32) as ps_q, \
                 nc.psum_tensor([128, max(TG, 512)], F32) as ps_mb, \
                 nc.psum_tensor([128, max(TG, 512)], F32) as ps_rb, \
                 nc.psum_tensor([128, max(TG, 512)], F32) as ps_tmp, \
                 nc.psum_tensor([1, max(TG, 512)], F32) as ln_s:
                for dt_i in range(n_dt):
                    pb.on("te", lambda e, i=dt_i, t0=t0: e.matmul(
                        ps_m[:, 0:TG], onesd[:, 0:1], xbf[:, i * T + t0:i * T + t0 + TG],
                        start=(i == 0), stop=(i == n_dt - 1)),
                        inc=("te" if dt_i == n_dt - 1 else None),
                        waits=(([("act", c0)] + list(stat_waits))
                               if dt_i == 0 else []))
                cm = pb.cnt["te"]
                for dt_i in range(n_dt):
                    pb.on("te", lambda e, i=dt_i, t0=t0: e.matmul(
                        ps_q[:, 0:TG], onesd[:, 0:1], xsq[:, i * T + t0:i * T + t0 + TG],
                        start=(i == 0), stop=(i == n_dt - 1)),
                        inc=("te" if dt_i == n_dt - 1 else None),
                        waits=([("act", c1)] if dt_i == 0 else []))
                cq = pb.cnt["te"]
                pb.on("dve", lambda e: e.tensor_scalar(m_row, ps_m[:, 0:TG], 1.0 / d,
                                                       None, ALU.mult), inc="dve",
                      waits=[("te", cm), ("gpc", pb.cnt["gpc"])])
                pb.wait("dve", "dve", pb.cnt["dve"])
                pb.on("dve", lambda e: e.tensor_tensor(r_row, m_row, m_row,
                                                       ALU.mult), inc="dve")
                pb.wait("dve", "dve", pb.cnt["dve"])
                pb.on("dve", lambda e: e.tensor_scalar(
                    r_row, r_row, -1.0, None, ALU.mult), inc="dve")
                pb.wait("dve", "dve", pb.cnt["dve"])
                pb.on("dve", lambda e: e.scalar_tensor_tensor(
                    r_row, ps_q[:, 0:TG], 1.0 / d, r_row, ALU.mult, ALU.add),
                    inc="dve", waits=[("te", cq)])
                pb.wait("dve", "dve", pb.cnt["dve"])
                cv = pb.on("dve", lambda e: e.tensor_scalar(r_row, r_row, 1e-5,
                                                            None, ALU.add),
                           inc="dve")
                cs = pb.on("act", lambda e: e.activation(
                    ln_s[0:1, 0:TG], r_row, AF.Sqrt), inc="act",
                    waits=[("dve", cv)])
                pb.on("dve", lambda e: e.reciprocal(mb_r[0:1, 0:TG],
                                                    ln_s[0:1, 0:TG]),
                      inc="dve", waits=[("act", cs), ("gpc", pb.cnt["gpc"])])
                pb.wait("dve", "dve", pb.cnt["dve"])
                pb.on("dve", lambda e: e.tensor_tensor(
                    ln_s[0:1, 0:TG], mb_r[0:1, 0:TG], mb_r[0:1, 0:TG], ALU.mult),
                    inc="dve")
                pb.wait("dve", "dve", pb.cnt["dve"])
                pb.on("dve", lambda e: e.tensor_tensor(
                    ln_s[0:1, 0:TG], ln_s[0:1, 0:TG], r_row, ALU.mult),
                    inc="dve")
                pb.wait("dve", "dve", pb.cnt["dve"])
                pb.on("dve", lambda e: e.tensor_scalar(
                    ln_s[0:1, 0:TG], ln_s[0:1, 0:TG], -0.5, 1.5, ALU.mult,
                    ALU.add), inc="dve")
                pb.wait("dve", "dve", pb.cnt["dve"])
                cr = pb.on("dve", lambda e: e.tensor_tensor(
                    r_row, mb_r[0:1, 0:TG], ln_s[0:1, 0:TG], ALU.mult),
                    inc="dve")
                c_rb1 = pb.on("te", lambda e: e.matmul(
                    ps_mb[:, 0:TG], onesr[:, 0:128], m_row, start=True, stop=True),
                    inc="te", waits=[("dve", cr)])
                c_rb2 = pb.on("te", lambda e: e.matmul(
                    ps_rb[:, 0:TG], onesr[:, 0:128], r_row, start=True, stop=True),
                    inc="te")
                pb.on("dve", lambda e: e.tensor_copy(mb_r, ps_mb[:, 0:TG]), inc="dve",
                      waits=[("te", c_rb1), ("gpc", pb.cnt["gpc"])])
                pb.on("dve", lambda e: e.tensor_copy(rb_r, ps_rb[:, 0:TG]), inc="dve",
                      waits=[("te", c_rb2)])
                c_bc = pb.cnt["dve"]
                c_bc_gp = c_bc
                for dt_i in range(n_dt):
                    if dt_i % 2 == 0:
                        c_t = pb.on("dve", lambda e, i=dt_i, t0=t0:
                                    e.tensor_tensor(
                                        ps_tmp[:, 0:TG],
                                        strm[:, i * T + t0:i * T + t0 + TG],
                                        mb_r, ALU.subtract), inc="dve",
                                    waits=[("dve", c_bc)])
                        pb.wait("dve", "dve", c_t)
                        pb.on("dve", lambda e, i=dt_i, t0=t0: e.tensor_tensor(
                            x2T[:, i * T + t0:i * T + t0 + TG], ps_tmp[:, 0:TG],
                            rb_r, ALU.mult), inc="dve")
                        pb.wait("dve", "dve", pb.cnt["dve"])
                    else:
                        c_t = pb.on("gp", lambda e, i=dt_i, t0=t0:
                                    e.tensor_tensor(
                                        ln_tmp[:, 0:TG],
                                        strm[:, i * T + t0:i * T + t0 + TG],
                                        mb_r, ALU.subtract), inc="gpc",
                                    waits=[("dve", c_bc)])
                        pb.wait("gp", "gpc", c_t)
                        pb.on("gp", lambda e, i=dt_i, t0=t0: e.tensor_tensor(
                            x2T[:, i * T + t0:i * T + t0 + TG], ln_tmp[:, 0:TG],
                            rb_r, ALU.mult), inc="gpc")
                        pb.wait("gp", "gpc", pb.cnt["gpc"])
        return pb.cnt["dve"]

    c_scal_done = [0]
    abf_busy = [0]
    tdve_busy = [0]

    def tern(src, dst, width, inv_col, waits):
        ab = abf_r[:, 0:width]
        ca = pb.on("act", lambda e, ab=ab: e.activation(
            ab, src, AF.Sign, bias=halfc[:, 0:1],
            scale=scal[:, inv_col:inv_col + 1]), inc="act",
            waits=list(waits) + [("dve", c_scal_done[0]),
                                 ("dve", abf_busy[0])])
        cb = pb.on("act", lambda e: e.activation(
            dst, src, AF.Sign, bias=halfc[:, 1:2],
            scale=scal[:, inv_col:inv_col + 1]), inc="act")
        cw = pb.on("dve", lambda e, ab=ab: e.tensor_tensor(dst, dst, ab,
                                                            ALU.add), inc="dve",
                   waits=[("act", cb)])
        abf_busy[0] = cw
        return ("dve", cw), ("act", cb)

    def tern_dve(src, dst, width, inv_col, waits):
        ta = ta_r[:, 0:width]
        tb = tb_r[:, 0:width]
        pb.on("dve", lambda e, ta=ta: e.tensor_scalar(
            ta, src, scal[:, inv_col:inv_col + 1], 0.5, ALU.mult, ALU.is_ge),
            inc="dve", waits=list(waits) + [("dve", c_scal_done[0]),
                                            ("dve", tdve_busy[0])])
        pb.on("dve", lambda e, tb=tb: e.tensor_scalar(
            tb, src, scal[:, inv_col:inv_col + 1], -0.5, ALU.mult, ALU.is_le),
            inc="dve")
        pb.wait("dve", "dve", pb.cnt["dve"])
        cw = pb.on("dve", lambda e, ta=ta, tb=tb: e.scalar_tensor_tensor(
            dst, ta, 1.0, tb, ALU.mult, ALU.subtract), inc="dve")
        tdve_busy[0] = cw
        return ("dve", cw), ("dve", cw)

    ffn_idx = [0]

    def emit_ffn(first, pre_waits=(), post_ln_cb=None, ln_stat_waits=()):
        c_ln = emit_ln(extra_waits=pre_waits, stat_waits=ln_stat_waits)
        if post_ln_cb is not None:
            post_ln_cb()
        pb.fence()
        with ExitStack() as hs:
            psb = [hs.enter_context(nc.psum_tensor("psb%d_%d" % (ffn_idx[0], i),
                                                    [128, TG], F32))
                   for i in range(8)]

            def hbank(mi, par, tg_i):
                return psb[par * 4 + mi * 2 + tg_i]

            w_busy = [None] * 4
            t_busy = [[], [], [], []]
            ev_busy = {}
            sA_busy = [0]
            ci = 0
            c_g = [0]

            def emit_evac(hc):
                par = hc % 2
                c_mm = ev_busy.get(("mm", hc), pb.cnt["te"])
                for tg_i in range(n_tg):
                    gdst = g_r[:, hc * T + tg_i * TG: hc * T + tg_i * TG + TG]
                    c_a = pb.on("act", lambda e, par=par, tg_i=tg_i: e.activation(
                        sA_r[:, 0:TG], hbank(0, par, tg_i)[:, :], AF.Silu,
                        scale=scal[:, HS1:HS1 + 1]), inc="act",
                        waits=[("te", c_mm), ("dve", sA_busy[0])])
                    c_g[0] = pb.on("dve", lambda e, par=par, tg_i=tg_i,
                                   gdst=gdst: e.scalar_tensor_tensor(
                                       gdst, hbank(1, par, tg_i)[:, :],
                                       scal[:, HS3:HS3 + 1], sA_r[:, 0:TG],
                                       ALU.mult, ALU.mult), inc="dve",
                        waits=[("act", c_a)])
                    sA_busy[0] = c_g[0]
                ev_busy[(par, "act")] = pb.cnt["act"]
                ev_busy[(par, "dve")] = pb.cnt["dve"]

            srcs = ((ap["w1t"], t1c, INV1, "act"), (ap["w3t"], t3c, INV3, "dve"))
            for hc in range(n_hc):
                par = hc % 2
                for mi, (wsrc, cache, inv, path) in enumerate(srcs):
                    for ch in range(n_wch):
                        ws = ci % 4
                        sl = ci % 4
                        ci += 1
                        if first:
                            wbuf = wf32[:, ws * WCH: ws * WCH + WCH]
                            ldq = "act" if path == "act" else "sy"
                            c_l = dma(ldq, "dwA%d" % ws, wbuf,
                                      wsrc[:, hc, ch * WCH:(ch + 1) * WCH],
                                      waits=([w_busy[ws]] if w_busy[ws] else []))
                            for s_, v_ in t_busy[sl]:
                                pb.wait("act" if path == "act" else "dve", s_, v_)
                            if path == "act":
                                c_t, frees = tern(wbuf, tring[sl], WCH, inv,
                                                  [("dwA%d" % ws, c_l)])
                            else:
                                c_t, frees = tern_dve(wbuf, tring[sl], WCH, inv,
                                                      [("dwA%d" % ws, c_l)])
                            w_busy[ws] = frees
                            c_st = dma("gp", "dtc%d" % sl,
                                       cache.ap()[:, hc, ch * WCH:(ch + 1) * WCH],
                                       tring[sl],
                                       waits=[c_t])
                            tb_new = [("dtc%d" % sl, c_st)]
                            mm_wait = c_t
                        else:
                            ldq = "sy" if sl % 2 == 0 else "act"
                            for s_, v_ in t_busy[sl]:
                                pb.wait(ldq, s_, v_)
                            c_l = dma(ldq, "dtA%d" % sl, tring[sl],
                                      cache.ap()[:, hc, ch * WCH:(ch + 1) * WCH])
                            tb_new = []
                            mm_wait = ("dtA%d" % sl, c_l)
                        for k in range(dpc):
                            dt_i = ch * dpc + k
                            for tg_i in range(n_tg):
                                wts = []
                                if k == 0 and tg_i == 0:
                                    wts = [mm_wait, ("dve", c_ln)]
                                    if ch == 0 and (par, "act") in ev_busy:
                                        wts.append(("act" if mi == 0 else "dve",
                                                    ev_busy[(par, "act" if mi == 0
                                                             else "dve")]))
                                last = (k == dpc - 1 and tg_i == n_tg - 1)
                                pb.on("te", lambda e, k=k, tg_i=tg_i, mi=mi,
                                      par=par, sl=sl, dt_i=dt_i:
                                      e.matmul(hbank(mi, par, tg_i)[:, :],
                                               tring[sl][:, k * 128:(k + 1) * 128],
                                               x2T[:, dt_i * T + tg_i * TG:
                                                   dt_i * T + tg_i * TG + TG],
                                               start=(dt_i == 0),
                                               stop=(dt_i == n_dt - 1)),
                                      inc=("te" if last else None), waits=wts)
                        t_busy[sl] = tb_new + [("te", pb.cnt["te"])]
                ev_busy[("mm", hc)] = pb.cnt["te"]
                if hc > 0:
                    emit_evac(hc - 1)
            emit_evac(n_hc - 1)
        pb.fence()
        n_dq = QD // 128
        with ExitStack() as fs:
            psf = [[fs.enter_context(nc.psum_tensor(
                        "psf%d_%d_%d" % (ffn_idx[0], i, j), [128, TG], F32))
                    for j in range(n_tg)] for i in range(n_dq)]
            w_busy = [None] * 4
            t_busy = [[], [], [], []]
            prev_ev = 0
            ci2 = 0
            for qd_i in range(n_qd):
                for hc in range(n_hc):
                    ws = ci2 % 4
                    sl = ci2 % 4
                    ci2 += 1
                    if first:
                        wbuf = wf32[:, ws * WCH: ws * WCH + QD]
                        ldq = "sy" if ws % 2 == 0 else "act"
                        c_l = dma(ldq, "dwA%d" % ws, wbuf,
                                  ap["w2q"][:, qd_i, hc, :],
                                  waits=([w_busy[ws]] if w_busy[ws] else []))
                        for s_, v_ in t_busy[sl]:
                            pb.wait("act", s_, v_)
                        c_t, frees = tern(wbuf, tring[sl][:, 0:QD], QD, INV2,
                                          [("dwA%d" % ws, c_l)])
                        w_busy[ws] = frees
                        c_st = dma("gp", "dtc%d" % sl,
                                   t2c.ap()[:, qd_i, hc, :], tring[sl][:, 0:QD],
                                   waits=[c_t])
                        tb_new = [("dtc%d" % sl, c_st)]
                        mm_wait = c_t
                    else:
                        ldq = "sy" if sl % 2 == 0 else "act"
                        for s_, v_ in t_busy[sl]:
                            pb.wait(ldq, s_, v_)
                        c_l = dma(ldq, "dtA%d" % sl, tring[sl][:, 0:QD],
                                  t2c.ap()[:, qd_i, hc, :])
                        tb_new = []
                        mm_wait = ("dtA%d" % sl, c_l)
                    for dq in range(n_dq):
                        for tg_i in range(n_tg):
                            wts = []
                            if dq == 0 and tg_i == 0:
                                wts = [mm_wait, ("dve", c_g[0])]
                                if hc == 0 and qd_i > 0:
                                    wts.append(("dve", prev_ev))
                            last = (dq == n_dq - 1 and tg_i == n_tg - 1)
                            pb.on("te", lambda e, dq=dq, tg_i=tg_i, hc=hc, sl=sl:
                                  e.matmul(psf[dq][tg_i][:, :],
                                           tring[sl][:, dq * 128:(dq + 1) * 128],
                                           g_r[:, hc * T + tg_i * TG:
                                               hc * T + tg_i * TG + TG],
                                           start=(hc == 0), stop=(hc == n_hc - 1)),
                                  inc=("te" if last else None), waits=wts)
                    t_busy[sl] = tb_new + [("te", pb.cnt["te"])]
                c_fq = pb.cnt["te"]
                for dq in range(n_dq):
                    ot = qd_i * n_dq + dq
                    for tg_i in range(n_tg):
                        pb.on("dve", lambda e, dq=dq, tg_i=tg_i, ot=ot:
                              e.scalar_tensor_tensor(
                                  strm[:, ot * T + tg_i * TG:
                                       ot * T + tg_i * TG + TG],
                                  psf[dq][tg_i][:, :], scal[:, FS2:FS2 + 1],
                                  strm[:, ot * T + tg_i * TG:
                                       ot * T + tg_i * TG + TG],
                                  ALU.mult, ALU.add), inc="dve",
                              waits=[("te", c_fq)])
                prev_ev = pb.cnt["dve"]
        pb.fence()
        ffn_idx[0] += 1

    def emit_attn():
        c_ln2 = emit_ln()
        pb.fence()
        c_agw = dma("sy", "dagw", agx_in.ap()[:, :], x2T[:, :])
        pairs = [[2 * i, 2 * i + 1] for i in range(C["n_cores"] // 2)]
        pb.on("gp", lambda e: e.collective_compute(
            "AllGather", ALU.bypass, replica_groups=pairs,
            ins=[agx_in.ap().opt()], outs=[agx_out.ap().opt()]), inc="cc",
            waits=[("dagw", c_agw)])
        c_agx = pb.cnt["cc"]

        with ExitStack() as qs:
            qb = [qs.enter_context(nc.psum_tensor("qb%d" % i, [128, TG], F32))
                  for i in range(8)]
            wq_busy = [0, 0]
            bank_busy = {}
            for og in range(0, n_dt, 2):
                sl = (og // 2) % 2
                c_l = dma("sy" if sl == 0 else "act", "dwq%d" % sl, wq_b[sl],
                          wqc.ap()[:, og:og + 2, :]
                          .rearrange("p o f -> p (o f)"),
                          waits=[("te", wq_busy[sl]), ("dcc", c_cc)])
                for oi in range(2):
                    ot = og + oi
                    for tg_i in range(n_tg):
                        bank = qb[sl * 4 + oi * n_tg + tg_i]
                        bkey = sl * 4 + oi * n_tg + tg_i
                        for dc in range(n_dt):
                            wts = []
                            if dc == 0:
                                wts = [("dwq%d" % sl, c_l), ("dve", c_ln2)]
                                if bkey in bank_busy:
                                    wts.append(("act", bank_busy[bkey]))
                            pb.on("te", lambda e, oi=oi, tg_i=tg_i, dc=dc, sl=sl,
                                  bank=bank:
                                  e.matmul(bank[:, :],
                                           wq_b[sl][:, (oi * n_dt + dc) * 128:
                                                    (oi * n_dt + dc + 1) * 128],
                                           x2T[:, dc * T + tg_i * TG:
                                               dc * T + tg_i * TG + TG],
                                           start=(dc == 0), stop=(dc == n_dt - 1)),
                                  inc=("te" if dc == n_dt - 1 else None),
                                  waits=wts)
                        c_mm = pb.cnt["te"]
                        qsv = qmatT.rearrange("p (H G r c) -> p H G r c",
                                              G=n_qt, r=rpb, c=nh)
                        nhb = TG // 128
                        dst = qsv[:, tg_i * nhb:(tg_i + 1) * nhb, :, :, ot]
                        bank_busy[bkey] = pb.on(
                            "act", lambda e, dst=dst, ot=ot, bank=bank:
                            e.activation(dst, bank[:, :], AF.Identity,
                                         bias=bqt[:, ot:ot + 1]),
                            inc="act", waits=[("te", c_mm)])
                wq_busy[sl] = pb.cnt["te"]
        c_q = pb.cnt["act"]
        pb.fence()

        K_rg = K_r
        n_units = 2 * n_kq
        with ExitStack() as ks:
            pkv = [ks.enter_context(nc.psum_tensor("pkv%d" % i, [128, 512], F32))
                   for i in range(8)]
            GW = min(512, hw)
            n_gw = hw // GW
            n_kt = KQ // 128
            bank_busy = {}
            wk_busy = [0, 0]
            wv_busy = [0, 0]
            x2q_busy = [0, 0]
            for pr in range((n_units + 1) // 2):
                uns = [u for u in (2 * pr, 2 * pr + 1) if u < n_units]
                xinfo = []
                for ui, u in enumerate(uns):
                    half, kq = u // n_kq, u % n_kq
                    x2q = x2qb[ui]
                    c_xgl = []
                    for dcl in range(n_dt):
                        c_xgl.append(("dxq%d_%d" % (dcl, ui), dma(
                            "sy" if dcl % 2 == 0 else "act",
                            "dxq%d_%d" % (dcl, ui),
                            x2q[:, dcl * KQ:(dcl + 1) * KQ],
                            agx_out.ap()[half * 128:(half + 1) * 128,
                                         dcl * T + kq * KQ:
                                         dcl * T + kq * KQ + KQ],
                            waits=([("te", x2q_busy[ui])]
                                   if dcl <= 1 else ()))))
                    xinfo.append((u, half, kq, x2q, c_xgl))
                pb.wait("sy", "cc", c_agx)
                ki = 0
                for hh in range(nh_c):
                    sl = hh % 2
                    c_wk = dma("gp", "dwk%d" % sl, wk_b[sl],
                               wkc.ap()[:, hh, :],
                               waits=[("te", wk_busy[sl]), ("dcc", c_cc)])
                    for (u, half, kq, x2q, c_xgl) in xinfo:
                        bank = pkv[ki % 4]
                        bkey = ki % 4
                        ki += 1
                        for dc in range(n_dt):
                            wts = [c_xgl[dc]]
                            if dc == 0:
                                wts += [("dwk%d" % sl, c_wk)]
                                if bkey in bank_busy:
                                    wts.append(bank_busy[bkey])
                            pb.on("te", lambda e, hh=hh, dc=dc, bank=bank,
                                  sl=sl, x2q=x2q:
                                  e.matmul(bank[:, 0:KQ],
                                           wk_b[sl][:, dc * 128:(dc + 1) * 128],
                                           x2q[:, dc * KQ:(dc + 1) * KQ],
                                           start=(dc == 0),
                                           stop=(dc == n_dt - 1)),
                                  inc=("te" if dc == n_dt - 1 else None),
                                  waits=wts)
                        c_mm = pb.cnt["te"]
                        wk_busy[sl] = c_mm
                        c_ev = pb.on(
                            "act", lambda e, hh=hh, half=half, kq=kq, bank=bank:
                            e.activation(K_rg[:, hh * t + half * T + kq * KQ:
                                              hh * t + half * T + kq * KQ + KQ],
                                         bank[:, 0:KQ], AF.Identity,
                                         bias=bkt[:, hh:hh + 1]),
                            inc="act", waits=[("te", c_mm)])
                        bank_busy[bkey] = ("act", c_ev)
                for dc in range(n_dt):
                    sl = dc % 2
                    c_wv = dma("sy", "dwv%d" % sl, wv_b[sl],
                               wvc.ap()[:, dc, :],
                               waits=[("te", wv_busy[sl]), ("dcc", c_cc)])
                    for ui, (u, half, kq, x2q, c_xgl) in enumerate(xinfo):
                        for kt in range(n_kt):
                            for gw in range(n_gw):
                                b_i = ui * 4 + kt * n_gw + gw
                                bank = pkv[b_i]
                                wts = [c_xgl[dc], ("dwv%d" % sl, c_wv)]
                                if dc == 0:
                                    wts += [("dbv", c_bv)]
                                    if b_i in bank_busy:
                                        wts.append(bank_busy[b_i])
                                last = (ui == len(xinfo) - 1
                                        and kt == n_kt - 1 and gw == n_gw - 1)
                                pb.on("te", lambda e, dc=dc, kt=kt, gw=gw,
                                      bank=bank, sl=sl, x2q=x2q:
                                      e.matmul(bank[:, 0:GW],
                                               x2q[:, dc * KQ + kt * 128:
                                                   dc * KQ + (kt + 1) * 128],
                                               wv_b[sl][:, gw * GW:
                                                        gw * GW + GW],
                                               start=(dc == 0), stop=False),
                                      inc=("te" if last else None), waits=wts)
                    wv_busy[sl] = pb.cnt["te"]
                for ui, (u, half, kq, x2q, c_xgl) in enumerate(xinfo):
                    for kt in range(n_kt):
                        ktile = half * n_tt + kq * n_kt + kt
                        for gw in range(n_gw):
                            b_i = ui * 4 + kt * n_gw + gw
                            bank = pkv[b_i]
                            pb.on("te", lambda e, gw=gw, bank=bank:
                                  e.matmul(bank[:, 0:GW], onesrb[0:1, 0:128],
                                           bvr[0:1, gw * GW:(gw + 1) * GW],
                                           start=False, stop=True),
                                  inc="te")
                            c_mm = pb.cnt["te"]
                            c_ev = pb.on(
                                "dve", lambda e, ktile=ktile, gw=gw, bank=bank:
                                e.tensor_copy(V_r[:, ktile * hw + gw * GW:
                                                  ktile * hw + gw * GW + GW],
                                              bank[:, 0:GW]),
                                inc="dve", waits=[("te", c_mm)])
                            bank_busy[b_i] = ("dve", c_ev)
                    x2q_busy[ui] = pb.cnt["te"]
        c_kv_act = pb.cnt["act"]
        c_kv_dve = pb.cnt["dve"]
        pb.fence()

        CH = C["CH"]
        Pt2 = [P_r, Pt_r]
        arow4 = arena_t[0:1, WB_OFF:WB_OFF + 512]
        iters = [(gq, hh) for gq in range(n_qt) for hh in range(nh_c)]
        n_it = len(iters)
        with nc.psum_tensor([128, 512], F32) as ss0, \
             nc.psum_tensor([128, 512], F32) as ss1, \
             nc.psum_tensor([128, 512], F32) as st0, \
             nc.psum_tensor([128, 512], F32) as st1, \
             nc.psum_tensor([128, 512], F32) as po0, \
             nc.psum_tensor([128, 512], F32) as po1, \
             nc.psum_tensor([128, 512], F32) as den_c, \
             nc.psum_tensor([128, 512], F32) as misc:
            ssb = [ss0, ss1]
            stb = [st0, st1]
            pob = [po0, po1]
            S = {}
            ss_busy = [0, 0]
            st_busy = [0, 0]
            osb_busy = [0, 0]
            pob_busy = [0, 0]
            den_free = [0]
            misc_busy = [0]
            pt_free = [0, 0]
            arow_busy = [0]
            mcol_busy = [0, 0]
            mx_busy = [0, 0]
            c_aga = []

            def emitA(i):
                gq, hh = iters[i]
                n_k = (gq + 1) * 128
                nblk = (n_k + 511) // 512
                par = i % 2
                q_ap = qmatT[:, (hh * n_qt + gq) * 128:
                             (hh * n_qt + gq + 1) * 128]
                mb = 4 + 4 * par
                for kb in range(nblk):
                    kb0 = kb * 512
                    w_ = min(512, n_k - kb0)
                    diag = (gq // 4 == kb)
                    bi = kb % 2
                    wts = [("dve", ss_busy[bi]), ("dve", mx_busy[par])]
                    if i == 0 and kb == 0:
                        wts += [("act", c_q), ("act", c_kv_act)]
                    pb.on("te", lambda e, kb0=kb0, w_=w_, q_ap=q_ap, hh=hh,
                          diag=diag, bi=bi:
                          e.matmul(ssb[bi][:, 0:w_], q_ap,
                                   K_r[:, hh * t + kb0: hh * t + kb0 + w_],
                                   start=True, stop=not diag),
                          inc=(None if diag else "te"), waits=wts)
                    if diag:
                        pb.on("te", lambda e, gq=gq, kb0=kb0, bi=bi:
                              e.matmul(ssb[bi][:, gq * 128 - kb0:
                                               (gq + 1) * 128 - kb0],
                                       idenT, trib[:, :],
                                       start=False, stop=True), inc="te")
                    c_blk = pb.cnt["te"]
                    ss_busy[bi] = pb.on(
                        "dve", lambda e, w_=w_, kb=kb, bi=bi, mb=mb:
                        e.reduce_max(mx[:, mb + kb:mb + kb + 1],
                                     ssb[bi][:, 0:w_], axis=AX.X),
                        inc="dve", waits=[("te", c_blk)])
                S[i] = dict(q_ap=q_ap, n_k=n_k, nblk=nblk,
                            c_maxes=pb.cnt["dve"])

            def emitB1a(i):
                gq, hh = iters[i]
                st_ = S[i]
                n_k, nblk, q_ap = st_["n_k"], st_["nblk"], st_["q_ap"]
                par = i % 2
                mb = 4 + 4 * par
                pb.wait("dve", "dve", st_["c_maxes"])
                mcol = mb
                if nblk > 1:
                    pb.on("dve", lambda e, nblk=nblk, mb=mb, par=par:
                          e.reduce_max(mx[:, par:par + 1],
                                       mx[:, mb:mb + nblk], axis=AX.X),
                          inc="dve")
                    pb.wait("dve", "dve", pb.cnt["dve"])
                    mcol = par
                c_nm = pb.on("dve", lambda e, mcol=mcol, par=par:
                             e.tensor_scalar(mcolb[:, par:par + 1],
                                             mx[:, mcol:mcol + 1], -1.0, None,
                                             ALU.mult), inc="dve",
                             waits=[("te", mcol_busy[par])])
                mx_busy[par] = c_nm
                c_nmm = pb.on("te", lambda e, par=par: e.matmul(
                    misc[0:1, 0:512], mcolb[:, par:par + 1], iden4,
                    start=True, stop=True), inc="te",
                    waits=[("dve", c_nm), ("act", misc_busy[0])])
                mcol_busy[par] = c_nmm
                c_nmev = pb.on("act", lambda e: e.activation(
                    arow4[0:1, 0:512], misc[0:1, 0:512], AF.Copy), inc="act",
                    waits=[("te", c_nmm), ("te", arow_busy[0])])
                misc_busy[0] = c_nmev
                st_["c_nmev"] = c_nmev

            def emitB1b(i):
                gq, hh = iters[i]
                st_ = S[i]
                n_k, q_ap = st_["n_k"], st_["q_ap"]
                n_kt = n_k // 128
                c_nmev = st_["c_nmev"]
                exp_of_grp = []
                ptbuf = Pt2[i % 2]
                st_["ptbuf"] = ptbuf
                for b0 in range(0, n_kt, 4):
                    nb = min(4, n_kt - b0)
                    bi = (b0 // 4) % 2
                    bank = stb[bi]
                    for j in range(nb):
                        kt = b0 + j
                        wts = []
                        if j == 0:
                            wts = [("act", st_busy[bi]), ("act", c_nmev)]
                        pb.on("te", lambda e, j=j, kt=kt, hh=hh, q_ap=q_ap,
                              bank=bank:
                              e.matmul(bank[:, j * 128:(j + 1) * 128],
                                       K_r[:, hh * t + kt * 128:
                                           hh * t + (kt + 1) * 128],
                                       q_ap, start=(j == 0), stop=False),
                              inc=None, waits=wts)
                        if kt == gq:
                            pb.on("te", lambda e, j=j, bank=bank:
                                  e.matmul(bank[:, j * 128:(j + 1) * 128],
                                           trib[:, :], idenT,
                                           start=False, stop=False),
                                  inc=None)
                    pb.on("te", lambda e, nb=nb, bank=bank:
                          e.matmul(bank[:, 0:nb * 128],
                                   onesrb[0:1, 0:128], arow4[0:1, 0:nb * 128],
                                   start=False, stop=True), inc="te")
                    c_stp = pb.cnt["te"]
                    c_exp = pb.on("act", lambda e, b0=b0, nb=nb, bank=bank,
                                  ptbuf=ptbuf:
                                  e.activation(
                                      ptbuf[:, b0 * 128:(b0 + nb) * 128],
                                      bank[:, 0:nb * 128], AF.Exp,
                                      scale=C["scale"]), inc="act",
                                  waits=[("te", c_stp),
                                         ("te", pt_free[i % 2])])
                    st_busy[bi] = c_exp
                    exp_of_grp.append(c_exp)
                arow_busy[0] = pb.cnt["te"]
                st_["exps"] = exp_of_grp

            def emitB2(i):
                gq, hh = iters[i]
                st_ = S[i]
                n_k = st_["n_k"]
                n_kt = n_k // 128
                exp_of_grp = st_["exps"]
                ptbuf = st_["ptbuf"]
                oi = i % 2
                obank = pob[oi]
                for kt in range(n_kt):
                    wts = [("act", exp_of_grp[kt // 4])] if kt % 4 == 0 else []
                    if kt == 0:
                        wts += [("dve", c_kv_dve), ("act", pob_busy[oi]),
                                ("dve", den_free[0])]
                    pb.on("te", lambda e, kt=kt, hh=hh, obank=obank,
                          n_kt=n_kt, ptbuf=ptbuf:
                          e.matmul(obank[:, 0:128],
                                   ptbuf[:, kt * 128:(kt + 1) * 128],
                                   V_r[:, kt * hw + hh * 128:
                                       kt * hw + (hh + 1) * 128],
                                   start=(kt == 0), stop=(kt == n_kt - 1)),
                          inc=None, waits=wts)
                    pb.on("te", lambda e, kt=kt, n_kt=n_kt, ptbuf=ptbuf:
                          e.matmul(den_c[:, 0:1],
                                   ptbuf[:, kt * 128:(kt + 1) * 128],
                                   onesd[:, 0:1],
                                   start=(kt == 0), stop=(kt == n_kt - 1)),
                          inc=("te" if kt == n_kt - 1 else None))
                c_pv = pb.cnt["te"]
                pt_free[i % 2] = c_pv
                pb.wait("dve", "dve", pb.cnt["dve"])
                c_rc = pb.on("dve", lambda e, oi=oi: e.reciprocal(
                    mx[:, 2 + oi:3 + oi], den_c[:, 0:1]), inc="dve",
                    waits=[("te", c_pv)])
                den_free[0] = c_rc
                osb = outsb[:, oi * 256: oi * 256 + 128]
                c_oe = pb.on("act", lambda e, obank=obank, osb=osb, oi=oi:
                             e.activation(osb, obank[:, 0:128], AF.Identity,
                                          scale=mx[:, 2 + oi:3 + oi]),
                             inc="act",
                             waits=[("te", c_pv), ("dve", c_rc),
                                    ("dos%d" % oi, osb_busy[oi])])
                pob_busy[oi] = c_oe
                osb_busy[oi] = dma(
                    "sy", "dos%d" % oi,
                    aga_in.ap()[gq * 128:(gq + 1) * 128,
                                hh * 128:(hh + 1) * 128],
                    osb, waits=[("act", c_oe)])
                del S[i]
                if hh == nh_c - 1:
                    for (q0, q1) in CH:
                        if gq + 1 == q1:
                            r0, r1 = q0 * 128, q1 * 128
                            pb.on("gp", lambda e, r0=r0, r1=r1:
                                  e.collective_compute(
                                      "AllGather", ALU.bypass,
                                      replica_groups=pairs,
                                      ins=[aga_in.ap()[r0:r1, :].opt()],
                                      outs=[aga_out.ap()[2 * r0:2 * r1, :]
                                            .opt()]), inc="cc",
                                  waits=[("dos0", osb_busy[0]),
                                         ("dos1", osb_busy[1])])
                            c_aga.append(pb.cnt["cc"])

            for s in range(n_it + 2):
                if 1 <= s and s - 1 < n_it:
                    emitB1a(s - 1)
                if s < n_it:
                    emitA(s)
                if 1 <= s and s - 1 < n_it:
                    emitB1b(s - 1)
                if s >= 2:
                    emitB2(s - 2)
        pb.fence()

        for j in range(2):
            for tt in range(n_tt):
                col = j * n_tt + tt
                pb.on("gp", lambda e, j=j, tt=tt, col=col: e.indirect_dma_start(
                    o2tok[:, (tt * 2 + j) * hw: (tt * 2 + j + 1) * hw], None,
                    aga_out.ap()[:, :],
                    IndirectOffsetOnAxis(gidx[:, col:col + 1], 0)),
                    inc="dga", by=16, waits=[("cc", c_aga[-1])])
        c_gath = pb.cnt["dga"]

        with nc.psum_tensor([128, 1024], BF16) as tq0, \
             nc.psum_tensor([128, 1024], BF16) as tq1:
            tqb = [tq0, tq1]
            busy = [0, 0]
            bi = 0
            n_dd = 2 * hw // 128
            for tt in range(n_tt):
                for dc0 in range(0, n_dd, 4):
                    nb = min(4, n_dd - dc0)
                    bank = tqb[bi % 2]
                    for j in range(nb):
                        dc = dc0 + j
                        wts = []
                        if j == 0:
                            wts = [("dga", c_gath), ("dve", busy[bi % 2])]
                        pb.on("te", lambda e, tt=tt, dc=dc, j=j, bank=bank:
                              e.transpose(bank[:, j * 128:(j + 1) * 128],
                                          o2tok[:, tt * 2 * hw + dc * 128:
                                                tt * 2 * hw + (dc + 1) * 128],
                                          idenT),
                              inc=("te" if j == nb - 1 else None), waits=wts)
                    c_tp = pb.cnt["te"]
                    o2v = o2T.rearrange("p (c f) -> p c f", c=n_dd)
                    busy[bi % 2] = pb.on(
                        "dve", lambda e, tt=tt, dc0=dc0, nb=nb, bank=bank, o2v=o2v:
                        e.tensor_copy(
                            o2v[:, dc0:dc0 + nb, tt * 128:(tt + 1) * 128],
                            bank[:, 0:nb * 128]
                            .rearrange("p (c f) -> p c f", c=nb)),
                        inc="dve", waits=[("te", c_tp)])
                    bi += 1
        c_o2t = pb.cnt["dve"]
        pb.fence()

        with ExitStack() as osx:
            ob = [osx.enter_context(nc.psum_tensor("ob%d" % i, [128, TG], F32))
                  for i in range(8)]
            wo_busy = [0, 0]
            bank_busy = {}
            for og in range(0, n_dt, 2):
                sl = (og // 2) % 2
                c_l = dma("sy", "dwo%d" % sl, wo_b[sl],
                          woc.ap()[:, og:og + 2, :]
                          .rearrange("p o f -> p (o f)"),
                          waits=[("te", wo_busy[sl]), ("dcc", c_cc)])
                for oi in range(2):
                    ot = og + oi
                    for tg_i in range(n_tg):
                        bank = ob[sl * 4 + oi * n_tg + tg_i]
                        bkey = sl * 4 + oi * n_tg + tg_i
                        for dc in range(n_dt):
                            wts = []
                            if dc == 0:
                                wts = [("dwo%d" % sl, c_l), ("dve", c_o2t)]
                                if bkey in bank_busy:
                                    wts.append(("dve", bank_busy[bkey]))
                            pb.on("te", lambda e, oi=oi, tg_i=tg_i, dc=dc,
                                  sl=sl, bank=bank:
                                  e.matmul(bank[:, :],
                                           wo_b[sl][:, (oi * n_dt + dc) * 128:
                                                    (oi * n_dt + dc + 1) * 128],
                                           o2T[:, dc * T + tg_i * TG:
                                               dc * T + tg_i * TG + TG],
                                           start=(dc == 0), stop=(dc == n_dt - 1)),
                                  inc=("te" if dc == n_dt - 1 else None),
                                  waits=wts)
                        c_mm = pb.cnt["te"]
                        bank_busy[bkey] = pb.on(
                            "dve", lambda e, ot=ot, tg_i=tg_i, bank=bank:
                            e.scalar_tensor_tensor(
                                strm[:, ot * T + tg_i * TG:
                                     ot * T + tg_i * TG + TG],
                                bank[:, :], bot[:, ot:ot + 1],
                                strm[:, ot * T + tg_i * TG:
                                     ot * T + tg_i * TG + TG],
                                ALU.add, ALU.add), inc="dve",
                            waits=[("te", c_mm)])
                wo_busy[sl] = pb.cnt["te"]
        pb.fence()

    def first_post_ln():
        c_scal_done[0] = emit_scal_finalize()

    emit_ffn(True, pre_waits=[("dx", c_x), ("dcn", c_const)],
             post_ln_cb=first_post_ln, ln_stat_waits=[("dve", c_p3)])
    emit_attn()
    emit_ffn(False)

    c_out = dma("sy", "dou", yT[:, :, :], t3(strm[:, :], n_dt),
                waits=[("dve", pb.cnt["dve"])])
    pb.wait("sy", "dou", c_out)

    sems = {s: st.enter_context(nc.semaphore("s_" + s)) for s in pb.cnt}

    with nc.Block() as block:
        @block.tensor
        def _(e):
            pb.replay("te", e, sems)

        @block.scalar
        def _(e):
            pb.replay("act", e, sems)

        @block.vector
        def _(e):
            pb.replay("dve", e, sems)

        @block.gpsimd
        def _(e):
            pb.replay("gp", e, sems)

        @block.sync
        def _(e):
            pb.replay("sy", e, sems)

    st.close()
    return nc



def stage_inputs(C, inputs):
    import ml_dtypes
    d, t, b, nh, dh, hid = C["d"], C["t"], C["b"], C["nh"], C["dh"], C["hid"]
    T, n_dt, n_tt, nh_c = C["T"], C["n_dt"], C["n_tt"], C["nh_c"]
    n_hc, hidp, QD, n_qd, hw = C["n_hc"], C["hidp"], C["QD"], C["n_qd"], C["hw"]

    x = np.ascontiguousarray(np.asarray(inputs["L0_attn_x"], dtype=np.float32))
    Wq = np.asarray(inputs["L0_attn_Wq"], np.float32)
    Wk = np.asarray(inputs["L0_attn_Wk"], np.float32)
    Wv = np.asarray(inputs["L0_attn_Wv"], np.float32)
    Wo = np.asarray(inputs["L0_attn_Wo"], np.float32)
    bq = np.asarray(inputs["L0_attn_bq"], np.float32)
    bk = np.asarray(inputs["L0_attn_bk"], np.float32)
    bv = np.asarray(inputs["L0_attn_bv"], np.float32)
    bo = np.asarray(inputs["L0_attn_bo"], np.float32)
    w1 = np.asarray(inputs["L0_ffn_w1"], np.float32)
    w3 = np.asarray(inputs["L0_ffn_w3"], np.float32)
    w2 = np.asarray(inputs["L0_ffn_w2"], np.float32)

    def tile_w13(w):
        wp = np.zeros((hidp, d), np.float32)
        wp[:hid] = w
        a = wp.reshape(n_hc, 128, n_dt, 128)
        return np.ascontiguousarray(
            a.transpose(3, 0, 2, 1).reshape(128, n_hc, n_dt * 128))

    w1t, w3t = tile_w13(w1), tile_w13(w3)
    w2p = np.zeros((d, hidp), np.float32)
    w2p[:, :hid] = w2
    w2q = np.ascontiguousarray(
        w2p.reshape(n_qd, QD, n_hc, 128).transpose(3, 0, 2, 1))
    wqt = np.ascontiguousarray(
        Wq.reshape(n_dt, 128, n_dt, 128).transpose(1, 2, 0, 3)
        .reshape(128, n_dt, n_dt * 128))
    wot = np.ascontiguousarray(
        Wo.reshape(n_dt, 128, n_dt, 128).transpose(1, 2, 0, 3)
        .reshape(128, n_dt, n_dt * 128))
    bqt = np.ascontiguousarray(bq.reshape(n_dt, 128).T)
    bot = np.ascontiguousarray(bo.reshape(n_dt, 128).T)

    Ns = C["Ns"]

    def slc(w, c):
        fl = w.reshape(-1)
        seg = fl.size // 8
        return np.ascontiguousarray(fl[c * seg:(c + 1) * seg].reshape(128, Ns))

    trib = np.triu(np.full((128, 128), NEG, np.float32), k=1).astype(
        ml_dtypes.bfloat16)
    iden = np.tile(np.eye(128, dtype=np.float32), (1, 4)).astype(
        ml_dtypes.bfloat16)
    ones_d = np.ones((128, 1), ml_dtypes.bfloat16)
    ones_f = np.ones((128, 1), np.float32)
    ones_r = np.ones((1, 128), np.float32)
    ones_rb = np.ones((1, 128), ml_dtypes.bfloat16)

    in_maps = []
    for c in range(C["n_cores"]):
        bb, h = c // 2, c % 2
        m = {}
        xt = x[bb, h * T:(h + 1) * T, :].T
        m["xT"] = np.ascontiguousarray(
            xt.reshape(n_dt, 128, T).transpose(1, 0, 2))
        m["w1t"], m["w3t"], m["w2q"] = w1t, w3t, w2q
        m["wqt"], m["wot"] = wqt, wot
        cols = slice(h * hw, (h + 1) * hw)
        m["wkt"] = np.ascontiguousarray(
            Wk[:, cols].reshape(n_dt, 128, nh_c, 128)
            .transpose(1, 2, 0, 3).reshape(128, nh_c, n_dt * 128))
        m["wvt"] = np.ascontiguousarray(
            Wv[:, cols].reshape(n_dt, 128, hw).transpose(1, 0, 2))
        m["bqt"], m["bot"] = bqt, bot
        m["bkt"] = np.ascontiguousarray(bk[cols].reshape(nh_c, 128).T)
        m["bvr"] = np.ascontiguousarray(bv[cols].reshape(1, hw))
        m["w1s"], m["w3s"], m["w2s"] = slc(w1, c), slc(w3, c), slc(w2, c)
        m["trib"], m["iden"] = trib, iden
        m["ones_d"], m["ones_f"] = ones_d, ones_f
        m["ones_r"], m["ones_rb"] = ones_r, ones_rb
        m["halfc"] = np.repeat(np.array([[-0.5, 0.5]], np.float32), 128, 0)
        CH = C["CH"]
        starts = {}
        off = 0
        for (q0, q1) in CH:
            starts[q0 * 128] = (off, (q1 - q0) * 128)
            off += 2 * (q1 - q0) * 128
        gi = np.zeros((128, 2 * n_tt), np.int32)
        for j in range(2):
            for tt in range(n_tt):
                r = h * T + tt * 128 + np.arange(128)
                base = np.zeros_like(r)
                loc = np.zeros_like(r)
                ln_ = np.zeros_like(r)
                for s0, (o_, l_) in starts.items():
                    mseg = (r >= s0) & (r < s0 + l_)
                    base[mseg] = o_
                    loc[mseg] = r[mseg] - s0
                    ln_[mseg] = l_
                gi[:, j * n_tt + tt] = base + j * ln_ + loc
        m["gidx"] = gi
        in_maps.append(m)
    return in_maps


def assemble_output(C, results):
    out = np.empty((C["b"], C["t"], C["d"]), np.float32)
    T, n_dt = C["T"], C["n_dt"]
    for c in range(C["n_cores"]):
        bb, h = c // 2, c % 2
        y = np.asarray(results[c]["yT"]).reshape(128, n_dt, T)
        out[bb, h * T:(h + 1) * T, :] = (
            y.transpose(1, 0, 2).reshape(C["d"], T).T)
    return out


def kernel(**inputs):
    C = CFG
    in_maps = stage_inputs(C, inputs)
    nc = build_nc(C)
    res = run_bass_kernel_spmd(nc, in_maps, core_ids=list(range(C["n_cores"])))
    return assemble_output(C, res.results)
